# version 04
# baseline (speedup 1.0000x reference)
import numpy as np
import ml_dtypes
import concourse.bass as bass
import concourse.mybir as mybir
from concourse.bass_utils import run_bass_kernel_spmd

F32 = mybir.dt.float32
BF16 = mybir.dt.bfloat16
AF = mybir.ActivationFunctionType
ALU = mybir.AluOpType

D = 1024
G = 8
DST = 16
HEADS = 8
DH = 64
INNER = 512
QG = 4
WIN = 512
HID = 2730
QKSCALE = 8.0
LNEPS = 1e-5
NCORE = 8

S_FULL = 8192
T_CHUNK = 1024

W1M = [128] * 21 + [42]
NW1 = len(W1M)

def build_wplan():
    plan = {}
    col = [0]

    def grp(key, width):
        a = col[0]
        plan[key] = (a, a + width)
        col[0] += width

    grp(("u",), 2 * G * DST)
    for l in range(2):
        for m in range(8):
            grp(("qkv", l, m), G * 128)
        grp(("v", l, 0), (G // 2) * 512)
        grp(("v", l, 1), (G // 2) * 512)
        for m in range(8):
            grp(("wo", l, m), HEADS * 128)
        for m in range(NW1):
            grp(("w1", l, m), 2 * G * W1M[m])
        for m in range(8):
            grp(("w2", l, m), NW1 * 128)
    for m in range(8):
        grp(("wg", m), G * 128)
    grp(("wom",), 8 * 128)
    return plan, col[0]


WPLAN, WCOLS = build_wplan()


def pack_wblob(f):
    blob = np.zeros((128, WCOLS), np.float32)

    def put(key, mat):
        a, b = WPLAN[key]
        assert mat.shape[1] == b - a and mat.shape[0] <= 128, (key, mat.shape)
        blob[: mat.shape[0], a:b] = mat

    WsT = np.ascontiguousarray(np.asarray(f["L0_attn_Ws"], np.float32).T)
    WiT = np.ascontiguousarray(np.asarray(f["L0_attn_Wi"], np.float32).T)
    put(("u",), np.concatenate(
        [WsT[128 * g:128 * (g + 1)] for g in range(G)]
        + [WiT[128 * g:128 * (g + 1)] for g in range(G)], axis=1))

    wqkv = np.asarray(f["L0_ffn_wqkv"], np.float32)
    wo = np.asarray(f["L0_ffn_wo"], np.float32)
    w1 = np.asarray(f["L0_ffn_w1"], np.float32)
    w2 = np.asarray(f["L0_ffn_w2"], np.float32)
    for l in range(2):
        for m in range(8):
            put(("qkv", l, m), np.concatenate(
                [wqkv[l][128 * k:128 * (k + 1), 128 * m:128 * (m + 1)]
                 for k in range(G)], 1))
        put(("v", l, 0), np.concatenate(
            [wqkv[l][128 * k:128 * (k + 1), 1024:1536] for k in range(4)], 1))
        put(("v", l, 1), np.concatenate(
            [wqkv[l][128 * k:128 * (k + 1), 1024:1536] for k in range(4, G)], 1))
        for m in range(8):
            tiles = []
            for h in range(HEADS):
                t = np.zeros((128, 128), np.float32)
                t[:64] = wo[l][64 * h:64 * (h + 1), 128 * m:128 * (m + 1)]
                tiles.append(t)
            put(("wo", l, m), np.concatenate(tiles, 1))
        for m in range(NW1):
            w = W1M[m]
            a_t = [w1[l][128 * k:128 * (k + 1), 128 * m:128 * m + w]
                   for k in range(G)]
            g_t = [w1[l][128 * k:128 * (k + 1), HID + 128 * m:HID + 128 * m + w]
                   for k in range(G)]
            put(("w1", l, m), np.concatenate(a_t + g_t, 1))
        for m in range(8):
            tiles = []
            for k in range(NW1):
                kk = min(128, HID - 128 * k)
                t = np.zeros((128, 128), np.float32)
                t[:kk] = w2[l][128 * k:128 * k + kk, 128 * m:128 * (m + 1)]
                tiles.append(t)
            put(("w2", l, m), np.concatenate(tiles, 1))
    WgT = np.ascontiguousarray(np.asarray(f["L0_attn_Wg"], np.float32).T)
    for m in range(8):
        put(("wg", m), np.concatenate(
            [WgT[128 * k:128 * (k + 1), 128 * m:128 * (m + 1)]
             for k in range(G)], 1))
    WomT = np.ascontiguousarray(np.asarray(f["L0_attn_Wo"], np.float32).T)
    tiles = []
    for m in range(8):
        t = np.zeros((128, 128), np.float32)
        t[:DST] = WomT[:, 128 * m:128 * (m + 1)]
        tiles.append(t)
    put(("wom",), np.concatenate(tiles, 1))
    return blob


def rope_tables(S):
    half = np.arange(0, DH, 2, dtype=np.float64)
    inv_freq = 1.0 / 10000.0 ** (half / DH)
    t = np.arange(S, dtype=np.float64)
    freqs = np.concatenate([t[:, None] * inv_freq] * 2, 1)
    base = np.concatenate([(half + 0.4 * DH) / (1.4 * DH)] * 2)
    cos, sin = np.cos(freqs), np.sin(freqs)
    pq = base[None, :] ** (((t % WIN) / (WIN // 2))[:, None])
    pk = base[None, :] ** ((-(t % WIN) / (WIN // 2))[:, None])

    def lay(tab):
        return np.concatenate([tab.T, tab.T], 0).astype(np.float32)

    return (lay(cos * pq), lay(sin * pq), lay(cos * pk), lay(sin * pk),
            base.astype(np.float32))


def const_misc(AT=None):
    P = np.zeros((128, 128), np.float32)
    for m in range(128):
        base = (m // 64) * 64
        dm = m % 64
        if dm < 32:
            P[base + dm + 32, m] = -1.0
        else:
            P[base + dm - 32, m] = 1.0
    triU = np.zeros((128, 128), np.float32)
    triL = np.zeros((128, 128), np.float32)
    for p in range(128):
        triU[p, p:] = 1.0
        triL[p, :p + 1] = 1.0
    at = np.zeros((128, 16), np.float32)
    if AT is not None:
        at[:16] = AT
    return np.concatenate([P, triU, triL, at], 1).astype(ml_dtypes.bfloat16)


CP_BSQ = 0
CP_QS = {}
_c = 1
for _l in range(2):
    for _k in range(4):
        CP_QS[(_l, _k)] = _c
        _c += 1
CP_AT = _c; _c += 16
CP_BS = _c; _c += 1
CP_BI = _c; _c += 1
CP_BO = _c; _c += 8
CP_BG = _c; _c += 8
CP_LNW = _c; _c += 8
CP_LNB = _c; _c += 8
CP_LG = _c; _c += 16
CP_LB = _c; _c += 16
CP_B1 = _c; _c += 2 * NW1
CP_B1G = _c; _c += 2 * NW1
CP_B2 = _c; _c += 16
CP_SEL = _c; _c += 128
CP_SIGN = _c; _c += 1
CPC = _c


def pack_cpat(f, base64):
    cp = np.zeros((128, CPC), np.float32)
    cp[:, CP_BSQ] = np.tile(base64 ** 2, 2)
    qs = np.asarray(f["L0_ffn_q_scale"], np.float32)
    ks = np.asarray(f["L0_ffn_k_scale"], np.float32)
    for l in range(2):
        cp[:, CP_QS[(l, 0)]] = np.tile(qs[l], 2)
        cp[:, CP_QS[(l, 1)]] = np.tile(np.concatenate([qs[l][32:], qs[l][:32]]), 2)
        cp[:, CP_QS[(l, 2)]] = np.tile(ks[l], 2)
        cp[:, CP_QS[(l, 3)]] = np.tile(np.concatenate([ks[l][32:], ks[l][:32]]), 2)
    cp[:DST, CP_AT:CP_AT + 16] = np.asarray(f["L0_attn_A"], np.float32).T
    cp[:DST, CP_BS] = np.asarray(f["L0_attn_bs"], np.float32)
    cp[:DST, CP_BI] = np.asarray(f["L0_attn_bi"], np.float32)
    for g in range(G):
        sl = slice(128 * g, 128 * (g + 1))
        cp[:, CP_BO + g] = np.asarray(f["L0_attn_bo"], np.float32)[sl]
        cp[:, CP_BG + g] = np.asarray(f["L0_attn_bg"], np.float32)[sl]
        cp[:, CP_LNW + g] = np.asarray(f["L0_attn_ln_w"], np.float32)[sl]
        cp[:, CP_LNB + g] = np.asarray(f["L0_attn_ln_b"], np.float32)[sl]
    for l in range(2):
        for g in range(G):
            sl = slice(128 * g, 128 * (g + 1))
            cp[:, CP_LG + 8 * l + g] = np.asarray(f["L0_ffn_ln_g"], np.float32)[l][sl]
            cp[:, CP_LB + 8 * l + g] = np.asarray(f["L0_ffn_ln_b"], np.float32)[l][sl]
        b1 = np.asarray(f["L0_ffn_b1"], np.float32)[l]
        for m in range(NW1):
            w = W1M[m]
            cp[:w, CP_B1 + NW1 * l + m] = b1[128 * m:128 * m + w]
            cp[:w, CP_B1G + NW1 * l + m] = b1[HID + 128 * m:HID + 128 * m + w]
        b2 = np.asarray(f["L0_ffn_b2"], np.float32)[l]
        for g in range(G):
            cp[:, CP_B2 + 8 * l + g] = b2[128 * g:128 * (g + 1)]
    cp[64, CP_SEL:CP_SEL + 64] = 1.0
    cp[65, CP_SEL + 64:CP_SEL + 128] = 1.0
    sgn = np.where(np.arange(128) % 64 < 32, -1.0, 1.0)
    cp[:, CP_SIGN] = sgn
    return cp


class Op:
    __slots__ = ("eng", "fn", "deps", "idx", "inc", "cum", "kind", "tag")

    def __init__(self, eng, fn, deps, idx, kind, tag):
        self.eng, self.fn, self.deps, self.idx = eng, fn, deps, idx
        self.inc, self.cum, self.kind, self.tag = 0, 0, kind, tag


class Em:
    def __init__(self):
        self.ops = []
        self.res = {}

    def op(self, eng, fn, r=(), w=(), dep=(), dma=False, tag="", force_inc=False):
        deps, seen = [], set()

        def add(o):
            if o is not None and id(o) not in seen:
                seen.add(id(o))
                deps.append(o)

        for res in r:
            e = self.res.get(res)
            if e and e[0] is not None:
                add(e[0])
        for res in w:
            e = self.res.get(res)
            if e:
                if e[0] is not None:
                    add(e[0])
                for rd in e[1]:
                    add(rd)
        for d in dep:
            add(d)
        o = Op(eng, fn, deps, len(self.ops), "d" if dma else "c", tag)
        if force_inc:
            o.inc = 1
        self.ops.append(o)
        for res in r:
            self.res.setdefault(res, [None, []])[1].append(o)
        for res in w:
            self.res[res] = [o, []]
        return o


ELIDE = True
STREAMS = [("tensor", "c"), ("vector", "c"), ("scalar", "c"),
           ("gpsimd", "c"), ("gpsimd", "d"), ("sync", "d")]


def emit_program(nc, em):
    for o in em.ops:
        if o.kind == "d":
            o.inc = 1
        for d in o.deps:
            if (d.eng, d.kind) != (o.eng, o.kind) or d.kind == "d":
                d.inc = 1
    cum = {s: 0 for s in STREAMS}
    for o in em.ops:
        s = (o.eng, o.kind)
        if o.inc:
            cum[s] += 16 if o.kind == "d" else 1
        o.cum = cum[s]
    totals = dict(cum)

    sem_cm = {s: nc.semaphore("sem_%s_%s" % s) for s in STREAMS}
    sem = {s: cm.__enter__() for s, cm in sem_cm.items()}

    know = {e: {s2: 0 for s2 in STREAMS} for e in
            ("tensor", "vector", "scalar", "gpsimd", "sync")}
    opclock = {}
    by_eng = {}
    for o in em.ops:
        by_eng.setdefault(o.eng, []).append(o)

    def run_engine(engname):
        def body(eng):
            kn = know[engname]
            for o in by_eng.get(engname, []):
                s_own = (o.eng, o.kind)
                for d in o.deps:
                    ds = (d.eng, d.kind)
                    if ds[0] == engname and ds[1] == "c":
                        continue
                    tgt = d.cum
                    if kn.get(ds, 0) >= tgt:
                        continue
                    eng.wait_ge(sem[ds], tgt)
                    kn[ds] = tgt
                    if ELIDE:
                        dk = opclock.get(id(d))
                        if dk:
                            for s2, v in dk.items():
                                if kn.get(s2, 0) < v:
                                    kn[s2] = v
                instr = o.fn(eng)
                if o.inc:
                    if instr is None:
                        raise RuntimeError("inc op returned no instr: " + o.tag)
                    instr.then_inc(sem[s_own], 16 if o.kind == "d" else 1)
                    snap = dict(kn)
                    snap[s_own] = o.cum
                    opclock[id(o)] = snap
        return body

    with nc.allow_low_precision(reason="bf16 operands intended"), \
            nc.Block() as block:
        block.tensor(run_engine("tensor"))
        block.vector(run_engine("vector"))
        block.scalar(run_engine("scalar"))
        block.gpsimd(run_engine("gpsimd"))

        def sync_body(eng):
            run_engine("sync")(eng)
            if totals[("gpsimd", "d")]:
                eng.wait_ge(sem[("gpsimd", "d")], totals[("gpsimd", "d")])
            if totals[("sync", "d")]:
                eng.wait_ge(sem[("sync", "d")], totals[("sync", "d")])
        block.sync(sync_body)

    for cm in sem_cm.values():
        cm.__exit__(None, None, None)


def build_nc(S, T, verbose=False, dbg=False):
    NT = S // T
    NP = T // 512
    NSLOT = (T + WIN) // 128
    KRING = T + WIN
    NJ = T // 128

    nc = bass.Bass("TRN2", target_bir_lowering=False)

    xT = nc.dram_tensor("xT", [D, S], F32, kind="ExternalInput").ap()
    wblob = nc.dram_tensor("wblob", [128, WCOLS], F32, kind="ExternalInput").ap()
    tabs = nc.dram_tensor("tabs", [128, 4 * S], F32, kind="ExternalInput").ap()
    cpat_in = nc.dram_tensor("cpin", [128, CPC], F32, kind="ExternalInput").ap()
    cmisc_in = nc.dram_tensor("cmin", [128, 400], BF16, kind="ExternalInput").ap()
    yout = nc.dram_tensor("y", [128, G * S], F32, kind="ExternalOutput").ap()
    wscr = nc.dram_tensor("wscr", [128, WCOLS], BF16, kind="Internal").ap()
    dbg_outs = {}

    AL = nc.alloc_sbuf_tensor

    SB_TOP = 229344
    ob_bytes = HEADS * T * 2
    OB_BASE = SB_TOP - ob_bytes
    arena_bytes = (G * T + 64) * 4
    ARENA_BASE = OB_BASE - arena_bytes
    arena = nc.alloc_sbuf_tensor_at("arena", [128, G * T + 64], F32,
                                    offset=ARENA_BASE).ap()
    xnar = AL("xnar", [128, G * (T + 1)], BF16).ap()
    xffn = AL("xffn", [128, G * (T + 1)], F32).ap()
    qk = AL("qk", [128, QG * T], BF16).ap()
    obf = nc.alloc_sbuf_tensor_at("obf", [128, HEADS * T], BF16,
                                  offset=OB_BASE).ap()
    shs = AL("shs", [128, 2560], F32).ap()
    expS = AL("expS", [128, 2 * 640], BF16).ap()
    knx = AL("knx", [128, 2 * 128], BF16).ap()
    tbuf = AL("tbuf", [128, 1024], F32).ap()
    WSLOT = 2816
    wring = AL("wring", [128, 2 * WSLOT], BF16).ap()
    sqt = AL("sqt", [128, 2 * 513], BF16).ap()
    uh = AL("uh", [128, T + 8], F32).ap()
    hbf = AL("hbf", [DST, T], BF16).ap()
    Hbuf = AL("Hbuf", [DST, 2 * (T + 1)], BF16).ap()
    scr = AL("scr", [128, 3 * 512], F32).ap()
    scrb = AL("scrb", [128, 2 * 512], BF16).ap()
    cpat = AL("cpat", [128, CPC], F32).ap()
    cmisc = AL("cmisc", [128, 400], BF16).ap()
    ones_f = AL("ones_f", [128, 128], F32).ap()
    ones_b = AL("ones_b", [128, 128], BF16).ap()
    blk2 = AL("blk2", [128, 2], BF16).ap()
    halos = AL("halos", [128, 32], F32).ap()
    dconst = AL("dconst", [128, 8], F32).ap()
    kpr = [AL("kpr%d" % l, [128, QG * KRING], BF16).ap() for l in range(2)]
    bvr = [AL("bvr%d" % l, [128, NSLOT * HEADS * 65], BF16).ap() for l in range(2)]

    h2p = nc.alloc_sbuf_tensor_at("h2p", [128, NW1 * 512], BF16,
                                  offset=ARENA_BASE).ap()
    x2 = nc.alloc_sbuf_tensor_at("x2b", [128, G * (T + 1)], BF16,
                                 offset=ARENA_BASE).ap()
    strips = nc.alloc_sbuf_tensor_at("strips", [128, 4 * T], F32,
                                     offset=OB_BASE).ap()
    recbf = AL("recbf", [128, 1024], BF16).ap()

    rem = nc.sbuf_bytes_remaining
    if verbose:
        print("SBUF bytes remaining:", rem)
    assert rem >= (SB_TOP - ARENA_BASE) + 512, "allocator near pinned slabs"

    psA = nc.alloc_psum_tensor("psA", [128, 1024], F32).ap()
    psB = nc.alloc_psum_tensor("psB", [128, 1024], F32).ap()
    psR = nc.alloc_psum_tensor("psR", [128, 1024], F32).ap()
    psX = nc.alloc_psum_tensor("psX", [128, 512], F32).ap()
    psS = nc.alloc_psum_tensor("psS", [128, 512], F32).ap()
    PRES = {"psA": 0, "psB": 2, "psR": 4, "psX": 6, "psS": 7}

    em = Em()

    def bres(name, lo, hi):
        base = PRES[name]
        return tuple("pb%d" % (base + b) for b in range(lo // 512, (hi + 511) // 512))

    def mm(e, out, lhsT, rhs, start, stop):
        return e.matmul(out, lhsT, rhs, start=start, stop=stop,
                        skip_group_check=True)

    def cpc(c, rows=128):
        return cpat[0:rows, c:c + 1]

    def xg(g, a, b):
        return arena[:, g * T + a:g * T + b]

    def xfd(g, a, b):
        base = g * (T + 1)
        return xffn[:, base + 1 + a:base + 1 + b]

    def xfs(g, a, b):
        base = g * (T + 1)
        o = 0 if g >= 4 else 1
        return xffn[:, base + o + a:base + o + b]

    def xnd(g, a, b):
        base = g * (T + 1)
        return xnar[:, base + 1 + a:base + 1 + b]

    def xns(g, a, b):
        base = g * (T + 1)
        o = 0 if g >= 4 else 1
        return xnar[:, base + o + a:base + o + b]

    def x2d(g, a, b):
        base = g * (T + 1)
        return x2[:, base + 1 + a:base + 1 + b]

    def x2s(g, a, b):
        base = g * (T + 1)
        o = 0 if g >= 4 else 1
        return x2[:, base + o + a:base + o + b]

    AR_ALL = ("arQ0", "arQ1", "arQ2", "arQ3")
    H2_RES = ("arQ0", "arQ1", "arQ2")
    X2_RES = ("arQ0", "arQ1")

    wslot = [0]
    first_touch = set()

    def wload(key, nslots=1):
        a, b = WPLAN[key]
        width = b - a
        slot = wslot[0] % 2
        wslot[0] += 1
        dst = wring[:, slot * WSLOT:slot * WSLOT + width]
        res_slot = "wr%d" % slot
        res_reg = "wreg%d" % a
        if key not in first_touch:
            first_touch.add(key)
            ld = em.op("gpsimd", lambda e, d=dst, s=wblob[:, a:b]: e.dma_start(d, s),
                       w=(res_slot,), dma=True, tag="wc%s" % (key,))
            em.op("gpsimd", lambda e, d=wscr[:, a:b], s=dst: e.dma_start(d, s),
                  r=(res_slot,), w=(res_reg,), dma=True, tag="ws%s" % (key,))
        else:
            ld = em.op("sync", lambda e, d=dst, s=wscr[:, a:b]: e.dma_start(d, s),
                       r=(res_reg,), w=(res_slot,), dma=True, tag="wl%s" % (key,))
        return dst, ld, res_slot

    scan = {"emitted": T, "udeps": [None, None]}

    def u_col(t):
        return uh[0:DST, t:t + 1]

    def H_col(t):
        return Hbuf[0:DST, 2 * t:2 * t + 1]

    pump_gate = [0]

    def scan_pump(n=1):
        if n == 1:
            pump_gate[0] += 1
            if pump_gate[0] % 2:
                return
        for _ in range(n):
            t = scan["emitted"]
            if t >= T:
                return
            udep = scan["udeps"][min(t // 512, NP - 1)]
            if udep is None:
                return
            o1 = em.op("tensor",
                       lambda e, t=t: mm(e, psS[0:DST, 0:1],
                                         cmisc[0:DST, 384:400],
                                         H_col(t), True, True),
                       r=("H", "cmisc") + bres("psS", 0, 1),
                       w=bres("psS", 0, 1), dep=[udep], tag="smm%d" % t)
            em.op("scalar",
                  lambda e, t=t: e.activation(H_col(t + 1), psS[0:DST, 0:1],
                                              AF.Tanh, bias=u_col(t)),
                  r=bres("psS", 0, 1) + ("u",), w=("H",), dep=[o1],
                  tag="stanh%d" % t)
            scan["emitted"] = t + 1

    R64 = 64

    def ln_derive(dep_ops, a_dst, b_dst, tag):
        a_ap, a_col, a_res = a_dst
        b_ap, b_col, b_res = b_dst
        o_mean = em.op("scalar",
                       lambda e: e.mul(shs[R64:R64 + 1, 0:512],
                                       psA[R64:R64 + 1, 0:512], 1.0 / D),
                       r=bres("psA", 0, 512), w=("shsm",), dep=dep_ops,
                       tag=tag + "mean")
        o_m2 = em.op("scalar",
                     lambda e: e.mul(shs[R64:R64 + 1, 512:1024],
                                     psA[R64:R64 + 1, 512:1024], 1.0 / D),
                     r=bres("psA", 512, 1024), w=("shsm2",), dep=dep_ops,
                     tag=tag + "m2")
        o_mm = em.op("vector",
                     lambda e: e.tensor_mul(shs[R64:R64 + 1, 1024:1536],
                                            shs[R64:R64 + 1, 0:512],
                                            shs[R64:R64 + 1, 0:512]),
                     r=("shsm",), w=("shs1",), dep=[o_mean], tag=tag + "mm")
        o_var = em.op("vector",
                      lambda e: e.tensor_sub(shs[R64:R64 + 1, 1024:1536],
                                             shs[R64:R64 + 1, 512:1024],
                                             shs[R64:R64 + 1, 1024:1536]),
                      r=("shsm2", "shs1"), w=("shs1",), dep=[o_m2, o_mm],
                      tag=tag + "var")
        o_sq = em.op("scalar",
                     lambda e: e.activation(shs[R64:R64 + 1, 1024:1536],
                                            shs[R64:R64 + 1, 1024:1536],
                                            AF.Sqrt, bias=dconst[R64:R64 + 1, 1:2]),
                     r=("shs1", "dconst"), w=("shs1",), dep=[o_var],
                     tag=tag + "sqrt")
        o_a = em.op("vector",
                    lambda e: e.reciprocal(a_ap[R64:R64 + 1, a_col:a_col + 512],
                                           shs[R64:R64 + 1, 1024:1536]),
                    r=("shs1",), w=(a_res,), dep=[o_sq], tag=tag + "a")
        o_ma = em.op("vector",
                     lambda e: e.tensor_mul(shs[R64:R64 + 1, 1024:1536],
                                            shs[R64:R64 + 1, 0:512],
                                            a_ap[R64:R64 + 1, a_col:a_col + 512]),
                     r=("shsm", a_res), w=("shs1",), dep=[o_a], tag=tag + "ma")
        o_b = em.op("scalar",
                    lambda e: e.mul(b_ap[R64:R64 + 1, b_col:b_col + 512],
                                    shs[R64:R64 + 1, 1024:1536], -1.0),
                    r=("shs1",), w=(b_res,), dep=[o_ma], tag=tag + "b")
        return o_a, o_b, o_mean

    def bcast64(dst_name, lo, src_ap, src_col, src_res, dep, tag="bc"):
        ps = {"psA": psA, "psB": psB, "psR": psR, "psX": psX}[dst_name]
        o = em.op("tensor",
                  lambda e: mm(e, ps[0:128, lo:lo + 512],
                               ones_f[R64:R64 + 1, 0:128],
                               src_ap[R64:R64 + 1, src_col:src_col + 512],
                               True, True),
                  r=(src_res, "onesf") + bres(dst_name, lo, lo + 512),
                  w=bres(dst_name, lo, lo + 512), dep=dep, tag=tag)
        return o

    def dump(name, ap, res, dep=()):
        if not dbg:
            return
        shp = list(ap.shape)
        dt = ap.dtype
        t = nc.dram_tensor("dbg_" + name, shp, dt, kind="ExternalOutput").ap()
        dbg_outs[name] = True
        em.op("gpsimd", lambda e, t=t, ap=ap: e.dma_start(t, ap),
              r=res, dma=True, force_inc=True, tag="dump" + name)

    em.op("sync", lambda e: e.dma_start(cpat, cpat_in), w=("cpat",), dma=True)
    em.op("sync", lambda e: e.dma_start(cmisc, cmisc_in), w=("cmisc",), dma=True)
    em.op("vector", lambda e: e.memset(ones_f, 1.0), w=("onesf",))
    em.op("vector", lambda e: e.memset(ones_b, 1.0), w=("onesb",))
    em.op("vector", lambda e: e.memset(blk2[0:64, 0:1], 1.0), w=("blk2",))
    em.op("vector", lambda e: e.memset(blk2[64:128, 0:1], 0.0), w=("blk2",))
    em.op("vector", lambda e: e.memset(blk2[0:64, 1:2], 0.0), w=("blk2",))
    em.op("vector", lambda e: e.memset(blk2[64:128, 1:2], 1.0), w=("blk2",))
    em.op("vector", lambda e: e.memset(halos, 0.0), w=("hB", "hC", "hD"))
    em.op("vector", lambda e: e.memset(H_col(0), 0.0), w=("H",))
    em.op("vector", lambda e: e.memset(dconst[:, 1:2], LNEPS), w=("dconst",))
    em.op("vector", lambda e: e.memset(dconst[:, 2:3], QKSCALE), w=("dconst",))
    em.op("vector",
          lambda e: e.tensor_add(dconst[0:DST, 0:1], cpat[0:DST, CP_BS:CP_BS + 1],
                                 cpat[0:DST, CP_BI:CP_BI + 1]),
          r=("cpat",), w=("dconst",))
    for _i, (_l, _k) in enumerate([(0, 1), (0, 3), (1, 1), (1, 3)]):
        em.op("vector",
              lambda e, _i=_i, _l=_l, _k=_k: e.tensor_mul(
                  dconst[:, 3 + _i:4 + _i], cpat[:, CP_QS[(_l, _k)]:CP_QS[(_l, _k)] + 1],
                  cpat[:, CP_SIGN:CP_SIGN + 1]),
              r=("cpat",), w=("dconst",), tag="sgn")
    for l in range(2):
        em.op("vector",
              lambda e, l=l: e.memset(
                  bvr[l].rearrange("p (s h c) -> p s h c", h=HEADS, c=65)[:, :, :, 64:65],
                  1.0),
              w=tuple("bv%d_%d" % (l, s) for s in range(NSLOT)))
    for g in range(G):
        em.op("vector", lambda e, g=g: e.memset(
            xnar[:, g * (T + 1):g * (T + 1) + 1], 0.0), w=("xa%d" % g,))
        em.op("vector", lambda e, g=g: e.memset(
            x2[:, g * (T + 1):g * (T + 1) + 1], 0.0), w=X2_RES)

    for c in range(NT):
        t0 = c * T
        scan["emitted"] = 0
        scan["udeps"] = [None, None]

        xin = em.op("sync",
                    lambda e, t0=t0: e.dma_start(
                        arena[:, 0:G * T].rearrange("p (g t) -> p g t", g=G),
                        xT[:, t0:t0 + T].rearrange("(g p) t -> p g t", p=128)),
                    w=AR_ALL, dma=True, tag="xload%d" % c)

        if c > 0:
            em.op("vector",
                  lambda e: e.tensor_copy(
                      xnar.rearrange("p (g t) -> p g t", g=G + 0, t=T + 1)[:, :, 0:1],
                      halos[:, 24:32].rearrange("p (g o) -> p g o", o=1)),
                  r=("hA",), w=tuple("xa%d" % g for g in range(G)),
                  tag="xnh%d" % c)
            em.op("vector", lambda e: e.tensor_copy(H_col(0), H_col(T)),
                  w=("H",), tag="hcar%d" % c)

        for g in range(G):
            em.op("scalar", lambda e, g=g: e.copy(xnd(g, 0, T), xg(g, 0, T)),
                  r=("arQ%d" % (g // 2),), w=("xa%d" % g,), dep=[xin],
                  tag="xbf%d" % g)

        ab1 = []
        for p in range(NP):
            a = p * 512
            prev = []
            for g in range(G):
                sl = g % 2
                o_sq = em.op("gpsimd",
                             lambda e, g=g, a=a, sl=sl:
                             e.tensor_mul(scrb[:, sl * 512:sl * 512 + 512],
                                          xnd(g, a, a + 512), xnd(g, a, a + 512)),
                             r=("xa%d" % g,), w=("scrb%d" % sl,), tag="sq")
                o1 = em.op("tensor",
                           lambda e, g=g, a=a, first=(g == 0):
                           mm(e, psA[R64:R64 + 1, 0:512], ones_b[:, 0:1],
                              xnd(g, a, a + 512), first, g == G - 1),
                           r=("xa%d" % g, "onesb") + (bres("psA", 0, 512) if g == 0 else ()),
                           w=bres("psA", 0, 512) if g == 0 else (),
                           dep=prev, tag="sm")
                prev = [o1]
                o2 = em.op("tensor",
                           lambda e, sl=sl, first=(g == 0), g=g:
                           mm(e, psA[R64:R64 + 1, 512:1024], ones_b[:, 0:1],
                              scrb[:, sl * 512:sl * 512 + 512], first, g == G - 1),
                           r=("scrb%d" % sl, "onesb") + (bres("psA", 512, 1024) if g == 0 else ()),
                           w=bres("psA", 512, 1024) if g == 0 else (),
                           dep=[o_sq] + prev, tag="sms")
                prev = [o2]
            ab1.append(ln_derive(prev, (strips, a, "stA"),
                                 (strips, T + a, "stB"), "ln1_%d_" % p))

        xn_done = []
        for p in range(NP):
            a = p * 512
            o_a, o_b, _ = ab1[p]
            bc_a = bcast64("psX", 0, strips, a, "stA", [o_a], tag="bca")
            bc_b = bcast64("psR", 0, strips, T + a, "stB", [o_b], tag="bcb")
            for g in range(G):
                sl = g % 2
                o1 = em.op("vector",
                           lambda e, g=g, a=a, sl=sl:
                           e.tensor_mul(scr[:, sl * 512:sl * 512 + 512],
                                        xg(g, a, a + 512), psX[0:128, 0:512]),
                           r=("arQ%d" % (g // 2),) + bres("psX", 0, 512),
                           w=("scr%d" % sl,), dep=[bc_a, xin], tag="xn1")
                o2 = em.op("vector",
                           lambda e, g=g, a=a, sl=sl:
                           e.tensor_add(xfd(g, a, a + 512),
                                        scr[:, sl * 512:sl * 512 + 512],
                                        psR[0:128, 0:512]),
                           r=("scr%d" % sl,) + bres("psR", 0, 512),
                           w=("xf%d" % g,), dep=[bc_b, o1], tag="xn2")
                o3 = em.op("scalar",
                           lambda e, g=g, a=a: e.copy(xnd(g, a, a + 512),
                                                      xfd(g, a, a + 512)),
                           r=("xf%d" % g,), w=("xa%d" % g,), dep=[o2], tag="xnc")
                xn_done.append(o3)

        if c == 0:
            dump("xar", arena, AR_ALL)
            dump("xnbf", xnar, tuple("xa%d" % g for g in range(G)))
            dump("xffn0", xffn, tuple("xf%d" % g for g in range(G)))
            dump("strips", strips, ("stA", "stB"))

        em.op("vector",
              lambda e: e.tensor_copy(
                  halos[:, 24:32].rearrange("p (g o) -> p g o", o=1),
                  xnar.rearrange("p (g t) -> p g t", g=G)[:, :, T:T + 1]),
              r=tuple("xa%d" % g for g in range(G)), w=("hA",),
              dep=xn_done[-1:], tag="hsA")

        ab2 = []
        for p in range(NP):
            a = p * 512
            prev = []
            for g in range(G):
                sl = g % 2
                o_sq = em.op("gpsimd",
                             lambda e, g=g, a=a, sl=sl:
                             e.tensor_mul(scrb[:, sl * 512:sl * 512 + 512],
                                          xnd(g, a, a + 512), xnd(g, a, a + 512)),
                             r=("xa%d" % g,), w=("scrb%d" % sl,), tag="sq2")
                o1 = em.op("tensor",
                           lambda e, g=g, a=a, first=(g == 0):
                           mm(e, psA[R64:R64 + 1, 0:512], ones_b[:, 0:1],
                              xnd(g, a, a + 512), first, g == G - 1),
                           r=("xa%d" % g, "onesb") + (bres("psA", 0, 512) if g == 0 else ()),
                           w=bres("psA", 0, 512) if g == 0 else (),
                           dep=prev, tag="s2")
                prev = [o1]
                o2 = em.op("tensor",
                           lambda e, sl=sl, first=(g == 0), g=g:
                           mm(e, psA[R64:R64 + 1, 512:1024], ones_b[:, 0:1],
                              scrb[:, sl * 512:sl * 512 + 512], first, g == G - 1),
                           r=("scrb%d" % sl, "onesb") + (bres("psA", 512, 1024) if g == 0 else ()),
                           w=bres("psA", 512, 1024) if g == 0 else (),
                           dep=[o_sq] + prev, tag="ss2")
                prev = [o2]
            o_r2, o_nm2r2, _ = ln_derive(prev, (strips, 2 * T + a, "stA2"),
                                         (strips, 3 * T + a, "stB2"),
                                         "ln2_%d_" % p)
            o_a, o_b, _ = ab1[p]
            o_br = em.op("vector",
                         lambda e, a=a: e.tensor_mul(
                             shs[R64:R64 + 1, 1024:1536],
                             strips[R64:R64 + 1, T + a:T + a + 512],
                             strips[R64:R64 + 1, 2 * T + a:2 * T + a + 512]),
                         r=("stA2", "stB", "shs1"), w=("shs1",),
                         dep=[o_r2, o_b], tag="br")
            o_A2 = em.op("vector",
                         lambda e, a=a: e.tensor_mul(
                             strips[R64:R64 + 1, 2 * T + a:2 * T + a + 512],
                             strips[R64:R64 + 1, 2 * T + a:2 * T + a + 512],
                             strips[R64:R64 + 1, a:a + 512]),
                         r=("stA2", "stA"), w=("stA2",),
                         dep=[o_br, o_a], tag="A2")
            o_B2 = em.op("vector",
                         lambda e, a=a: e.tensor_add(
                             strips[R64:R64 + 1, 3 * T + a:3 * T + a + 512],
                             strips[R64:R64 + 1, 3 * T + a:3 * T + a + 512],
                             shs[R64:R64 + 1, 1024:1536]),
                         r=("stB2", "shs1"), w=("stB2",),
                         dep=[o_nm2r2, o_br], tag="B2")
            ab2.append((o_A2, o_B2))

        wu, wu_ld, wu_res = wload(("u",))
        for p in range(NP):
            a = p * 512
            o_A2, o_B2 = ab2[p]
            bcA = bcast64("psX", 0, strips, 2 * T + a, "stA2", [o_A2], tag="bcA2")
            bcB = bcast64("psR", 0, strips, 3 * T + a, "stB2", [o_B2], tag="bcB2")
            prev_mm = []
            for g in range(G):
                sl = g % 2
                o1 = em.op("vector",
                           lambda e, g=g, a=a, sl=sl:
                           e.tensor_mul(scr[:, sl * 512:sl * 512 + 512],
                                        xg(g, a, a + 512), psX[0:128, 0:512]),
                           r=("arQ%d" % (g // 2),) + bres("psX", 0, 512),
                           w=("scr%d" % sl,), dep=[bcA, xin], tag="u1")
                o2 = em.op("vector",
                           lambda e, g=g, sl=sl:
                           e.tensor_add(scr[:, sl * 512:sl * 512 + 512],
                                        scr[:, sl * 512:sl * 512 + 512],
                                        psR[0:128, 0:512]),
                           r=("scr%d" % sl,) + bres("psR", 0, 512),
                           w=("scr%d" % sl,), dep=[bcB, o1], tag="u2")
                o3 = em.op("scalar",
                           lambda e, g=g, sl=sl:
                           e.activation(scrb[:, sl * 512:sl * 512 + 512],
                                        scr[:, sl * 512:sl * 512 + 512],
                                        AF.Identity, bias=cpc(CP_LNB + g),
                                        scale=cpc(CP_LNW + g)),
                           r=("scr%d" % sl, "cpat"), w=("scrb%d" % sl,),
                           dep=[o2], tag="u3")
                m1 = em.op("tensor",
                           lambda e, g=g, sl=sl, first=(g == 0), wu=wu:
                           mm(e, psB[0:DST, 0:512], wu[:, g * DST:(g + 1) * DST],
                              scrb[:, sl * 512:sl * 512 + 512], first, False),
                           r=("scrb%d" % sl, wu_res) + (bres("psB", 0, 512) if g == 0 else ()),
                           w=bres("psB", 0, 512) if g == 0 else (),
                           dep=[o3, wu_ld] + prev_mm, tag="umm")
                prev_mm = [m1]
                m2 = em.op("tensor",
                           lambda e, g=g, sl=sl, wu=wu:
                           mm(e, psB[0:DST, 0:512],
                              wu[:, G * DST + g * DST:G * DST + (g + 1) * DST],
                              scrb[:, sl * 512:sl * 512 + 512], False, g == G - 1),
                           r=("scrb%d" % sl, wu_res),
                           dep=[o3, wu_ld] + prev_mm, tag="umm2")
                prev_mm = [m2]
            uo = em.op("scalar",
                       lambda e, a=a: e.activation(uh[0:DST, a:a + 512],
                                                   psB[0:DST, 0:512],
                                                   AF.Identity,
                                                   bias=dconst[0:DST, 0:1]),
                       r=bres("psB", 0, 512) + ("dconst",), w=("u",),
                       dep=prev_mm, tag="uev%d" % p)
            scan["udeps"][p] = uo

        if c == 0:
            dump("u", uh, ("u",), dep=[x for x in scan["udeps"] if x])

        for l in range(2):
            lin_s = xns if l == 0 else x2s
            lres = (lambda g: "xa%d" % g) if l == 0 else (lambda g, _r=X2_RES: _r)

            def lrtup(g):
                r = lres(g)
                return (r,) if isinstance(r, str) else tuple(r)

            def qkv_m(m):
                wt, wld, wres = wload(("qkv", l, m))
                ps, pn = (psA, "psA") if m % 2 == 0 else (psB, "psB")
                prev = []
                for k in range(G):
                    for cc2 in range(NP):
                        scan_pump(1)
                        prev.append(em.op(
                            "tensor",
                            lambda e, k=k, cc2=cc2, ps=ps, wt=wt, lin_s=lin_s:
                            mm(e, ps[:, cc2 * 512:cc2 * 512 + 512],
                               wt[:, k * 128:(k + 1) * 128],
                               lin_s(k, cc2 * 512, cc2 * 512 + 512),
                               k == 0, k == G - 1),
                            r=lrtup(k) + (wres,) + (bres(pn, 0, 1024) if (k == 0 and cc2 == 0) else ()),
                            w=bres(pn, 0, 1024) if (k == 0 and cc2 == 0) else (),
                            dep=[wld] + prev[-1:], tag="qkv"))
                q = m % 4
                em.op("scalar",
                      lambda e, q=q, ps=ps: e.copy(qk[:, q * T:q * T + T],
                                                   ps[:, 0:T]),
                      r=bres(pn, 0, 1024), w=("qk%d" % q,), dep=prev[-1:],
                      tag="qkev")

            def rope_swap(q):
                for (d0, d1, s0, s1) in ((0, 32, 32, 64), (32, 64, 0, 32),
                                         (64, 96, 96, 128), (96, 128, 64, 96)):
                    em.op("sync",
                          lambda e, q=q, d0=d0, d1=d1, s0=s0, s1=s1:
                          e.dma_start(scrb[d0:d1, 0:T],
                                      qk[s0:s1, q * T:q * T + T]),
                          r=("qk%d" % q,), w=("scrb0", "scrb1"), dma=True,
                          tag="swdma")

            def rope_pre(is_k, q, p):
                cS = CP_QS[(l, 2 if is_k else 0)]
                sgn_col = 3 + 2 * l + (1 if is_k else 0)
                tabbase = (2 * S if is_k else 0)
                a = p * 512
                st = {}
                st["tl"] = em.op("sync",
                           lambda e, a=a, tabbase=tabbase, t0=t0:
                           e.dma_start(
                               tbuf.rearrange("p (u t) -> p u t", u=2),
                               tabs.rearrange("p (u s) -> p u s", u=4)
                               [:, (tabbase // S):(tabbase // S) + 2,
                                t0 + a:t0 + a + 512]),
                           w=("tbuf",), dma=True, tag="tab")
                o_sq = em.op("gpsimd",
                             lambda e, q=q, a=a:
                             e.tensor_mul(sqt[:, 0:512],
                                          qk[:, q * T + a:q * T + a + 512],
                                          qk[:, q * T + a:q * T + a + 512]),
                             r=("qk%d" % q,), w=("sqtA",), tag="qsq")
                o_ss = em.op("tensor",
                             lambda e: mm(e, psX[R64:R64 + 2, 0:512],
                                          blk2, sqt[:, 0:512], True, True),
                             r=("sqtA", "blk2") + bres("psX", 0, 512),
                             w=bres("psX", 0, 512), dep=[o_sq], tag="ssq")
                o_nrm = em.op("scalar",
                              lambda e: e.activation(shs[R64:R64 + 2, 0:512],
                                                     psX[R64:R64 + 2, 0:512],
                                                     AF.Sqrt),
                              r=bres("psX", 0, 512), w=("shsm",),
                              dep=[o_ss], tag="nsqrt")
                o_mx = em.op("vector",
                             lambda e: e.tensor_scalar_max(
                                 shs[R64:R64 + 2, 0:512],
                                 shs[R64:R64 + 2, 0:512], 1e-12),
                             r=("shsm",), w=("shsm",), dep=[o_nrm])
                st["rc"] = em.op("vector",
                              lambda e: e.reciprocal(shs[R64:R64 + 2, 0:512],
                                                     shs[R64:R64 + 2, 0:512]),
                              r=("shsm",), w=("shsm",), dep=[o_mx], tag="nrc")
                st["args"] = (is_k, q, p, cS, sgn_col)
                return st

            def rope_post(st):
                is_k, q, p, cS, sgn_col = st["args"]
                a = p * 512
                tl = st["tl"]
                o_fbc = em.op("tensor",
                              lambda e: mm(e, psX[0:128, 0:512],
                                           cpat[R64:R64 + 2, CP_SEL:CP_SEL + 128],
                                           shs[R64:R64 + 2, 0:512],
                                           True, True),
                              r=("shsm", "cpat") + bres("psX", 0, 512),
                              w=bres("psX", 0, 512), dep=[st["rc"]], tag="fbc")
                o_t1 = em.op("vector",
                             lambda e, q=q, a=a, cS=cS:
                             e.scalar_tensor_tensor(
                                 scr[:, 0:512],
                                 qk[:, q * T + a:q * T + a + 512],
                                 cpc(cS), tbuf[:, 0:512],
                                 ALU.mult, ALU.mult),
                             r=("qk%d" % q, "tbuf", "cpat"),
                             w=("scr0",), dep=[tl], tag="t1")
                o_t2 = em.op("vector",
                             lambda e, a=a, sgn_col=sgn_col:
                             e.scalar_tensor_tensor(
                                 scr[:, 512:1024],
                                 scrb[:, a:a + 512],
                                 dconst[:, sgn_col:sgn_col + 1],
                                 tbuf[:, 512:1024],
                                 ALU.mult, ALU.mult),
                             r=("scrb0", "scrb1", "dconst", "tbuf"),
                             w=("scr1",), dep=[tl], tag="t2")
                o_pre = em.op("vector",
                              lambda e: e.tensor_add(scr[:, 0:512],
                                                     scr[:, 0:512],
                                                     scr[:, 512:1024]),
                              r=("scr0", "scr1"), w=("scr0",),
                              dep=[o_t1, o_t2], tag="pre")
                if is_k:
                    rc0 = (t0 + a) % KRING
                    dst = kpr[l][:, q * KRING + rc0:q * KRING + rc0 + 512]
                    wres_k = tuple(
                        "kp%d_%d" % (l, ((t0 + a) // 128 + i) % NSLOT)
                        for i in range(4))
                    em.op("vector",
                          lambda e, dst=dst: e.tensor_mul(
                              dst, scr[:, 0:512], psX[0:128, 0:512]),
                          r=("scr0",) + bres("psX", 0, 512),
                          w=wres_k, dep=[o_pre, o_fbc], tag="kout")
                else:
                    em.op("vector",
                          lambda e, q=q, a=a: e.tensor_mul(
                              qk[:, q * T + a:q * T + a + 512],
                              scr[:, 0:512], psX[0:128, 0:512]),
                          r=("scr0",) + bres("psX", 0, 512),
                          w=("qk%d" % q,), dep=[o_pre, o_fbc], tag="qout")

            ilq = []

            def ilq_pump(n=1):
                for _ in range(n):
                    if ilq:
                        ilq.pop(0)()

            def ilq_drain():
                while ilq:
                    ilq.pop(0)()

            def enqueue_rope(is_k, q):
                def pre0(q=q, is_k=is_k):
                    rope_swap(q)
                    enqueue_rope.state[(is_k, q, 0)] = rope_pre(is_k, q, 0)
                def post0(q=q, is_k=is_k):
                    rope_post(enqueue_rope.state.pop((is_k, q, 0)))
                def pre1(q=q, is_k=is_k):
                    enqueue_rope.state[(is_k, q, 1)] = rope_pre(is_k, q, 1)
                def post1(q=q, is_k=is_k):
                    rope_post(enqueue_rope.state.pop((is_k, q, 1)))
                ilq.extend([pre0, post0, pre1, post1])
            enqueue_rope.state = {}

            def v_tb(tb, wv):
                (wtvA, wldvA, wresvA), (wtvB, wldvB, wresvB) = wv
                slot = (t0 // 128 + tb) % NSLOT
                ps, pn = (psA, "psA") if tb % 2 == 0 else (psB, "psB")
                prev = []
                for k in range(G):
                    scan_pump(1)
                    wtv = wtvA if k < 4 else wtvB
                    wresv = wresvA if k < 4 else wresvB
                    wldv = wldvA if k < 4 else wldvB
                    prev.append(em.op(
                        "tensor",
                        lambda e, k=k, tb=tb, ps=ps, wtv=wtv, lin_s=lin_s:
                        mm(e, ps[:, 0:512],
                           lin_s(k, tb * 128, tb * 128 + 128),
                           wtv[:, (k % 4) * 512:(k % 4) * 512 + 512],
                           k == 0, k == G - 1),
                        r=lrtup(k) + (wresv,) + (bres(pn, 0, 512) if k == 0 else ()),
                        w=bres(pn, 0, 512) if k == 0 else (),
                        dep=[wldv] + prev[-1:], tag="vmm"))
                em.op("vector",
                      lambda e, slot=slot, ps=ps, l=l:
                      e.tensor_copy(
                          bvr[l][:, slot * HEADS * 65:(slot + 1) * HEADS * 65]
                          .rearrange("p (h z) -> p h z", z=65)[:, :, 0:64],
                          ps[:, 0:512].rearrange("p (h z) -> p h z", z=64)),
                      r=bres(pn, 0, 512), w=("bv%d_%d" % (l, slot),),
                      dep=prev[-1:], tag="vev")

            for m in range(4, 8):
                qkv_m(m)
            for q in range(QG):
                enqueue_rope(True, q)
            wv = (wload(("v", l, 0)), wload(("v", l, 1)))
            for tb in range(T // 128):
                v_tb(tb, wv)
                ilq_pump(2)
            for m in range(0, 4):
                qkv_m(m)
                if m >= 1:
                    enqueue_rope(False, m - 1)
                ilq_pump(2)
            enqueue_rope(False, 3)

            Jlo = max(0, t0 // 128 - 4)
            Jhi = (t0 + T) // 128 - 1
            pend_drain = []

            def attn_head(h):
                hp = h % 2
                q = h // 2
                av_last = []
                av_virgin = {4: True, 5: True}
                pend_av = []
                for J in range(Jlo, Jhi + 1):
                    jp = J % 2
                    simps, simn = (psA, "psA") if jp == 0 else (psB, "psB")
                    r = J % 4
                    sn = 512 - 128 * r
                    a = max(128 * J, t0) - 128 * J
                    b = min(128 * J + 640, t0 + T) - 128 * J
                    slot = J % NSLOT
                    rc0 = (128 * J) % KRING
                    kres = "kp%d_%d" % (l, slot)
                    need_next = b > sn
                    if need_next:
                        o_knx = em.op("vector",
                                      lambda e, q=q, rc0=rc0, jp=jp, l=l:
                                      e.tensor_scalar_mul(
                                          knx[:, jp * 128:jp * 128 + 128],
                                          kpr[l][:, q * KRING + rc0:q * KRING + rc0 + 128],
                                          cpc(CP_BSQ)),
                                      r=(kres, "cpat"), w=("knx%d" % jp,),
                                      tag="knx")
                    cuts = sorted({a, b, min(max(a, sn), b), min(max(a, 512), b)})
                    sim_ops = []
                    for ci in range(len(cuts) - 1):
                        p0, p1 = cuts[ci], cuts[ci + 1]
                        if p1 <= p0:
                            continue
                        use_next = p0 >= sn
                        scan_pump(1)
                        if use_next:
                            lhs = knx[64 * hp:64 * hp + 64, jp * 128:jp * 128 + 128]
                            lr = ("knx%d" % jp,)
                            ldep = [o_knx]
                        else:
                            lhs = kpr[l][64 * hp:64 * hp + 64,
                                         q * KRING + rc0:q * KRING + rc0 + 128]
                            lr = (kres,)
                            ldep = []
                        sim_ops.append(em.op(
                            "tensor",
                            lambda e, p0=p0, p1=p1, lhs=lhs, simps=simps, J=J, q=q,
                            h=h, t0=t0:
                            mm(e, simps[:, p0:p1], lhs,
                               qk[64 * (h % 2):64 * (h % 2) + 64,
                                  q * T + 128 * J + p0 - t0:q * T + 128 * J + p1 - t0],
                               True, True),
                            r=lr + ("qk%d" % q,) + bres(simn, p0, p1),
                            w=bres(simn, p0, p1), dep=ldep, tag="sim"))
                    o_exp = em.op("scalar",
                                  lambda e, a=a, b=b, jp=jp, simps=simps:
                                  e.activation(expS[:, jp * 640 + a:jp * 640 + b],
                                               simps[:, a:b], AF.Exp,
                                               scale=QKSCALE),
                                  r=bres(simn, a, b), w=("ex%d" % jp,),
                                  dep=sim_ops, tag="exp")
                    mdeps = [o_exp]
                    mask_ranges = []
                    u0, u1 = a, min(b, 128)
                    if u1 > u0:
                        mask_ranges.append((u0, u1))
                        mdeps.append(em.op(
                            "gpsimd",
                            lambda e, u0=u0, u1=u1, jp=jp:
                            e.tensor_mul(expS[:, jp * 640 + u0:jp * 640 + u1],
                                         expS[:, jp * 640 + u0:jp * 640 + u1],
                                         cmisc[:, 128 + u0:128 + u1]),
                            r=("ex%d" % jp, "cmisc"), w=("ex%d" % jp,),
                            dep=[o_exp], tag="mU"))
                    l0, l1 = max(a, 512), b
                    if l1 > l0:
                        mask_ranges.append((l0, l1))
                        mdeps.append(em.op(
                            "gpsimd",
                            lambda e, l0=l0, l1=l1, jp=jp:
                            e.tensor_mul(expS[:, jp * 640 + l0:jp * 640 + l1],
                                         expS[:, jp * 640 + l0:jp * 640 + l1],
                                         cmisc[:, 256 + l0 - 512:256 + l1 - 512]),
                            r=("ex%d" % jp, "cmisc"), w=("ex%d" % jp,),
                            dep=[o_exp], tag="mL"))
                    if dbg and c == 0 and l == 0 and h == 0 and J in (0, 1):
                        dump("expS%d" % J, expS, ("ex%d" % jp,), )

                    ilo = 128 * J + a
                    ihi = 128 * J + b
                    newlo = 128 * J + 512
                    avcuts = sorted({ilo, ihi, min(max(ilo, newlo), ihi),
                                     min(max(ilo, t0 + 512), ihi)})
                    av_emits = []
                    for ci in range(len(avcuts) - 1):
                        i0, i1 = avcuts[ci], avcuts[ci + 1]
                        if i1 <= i0:
                            continue
                        bank = 4 + (i0 - t0) // 512
                        st = av_virgin[bank]
                        av_virgin[bank] = False
                        scan_pump(1)
                        c0m, c1m = i0 - 128 * J, i1 - 128 * J
                        if any(c1m > m0 and c0m < m1 for m0, m1 in mask_ranges):
                            pdeps = list(mdeps)
                        else:
                            pdeps = [o_exp]
                        av_emits.append((i0, i1, st, jp, J, slot, pdeps))
                    if J == Jlo + 1 and pend_drain:
                        pend_drain.pop(0)()
                    if (J - Jlo) % 3 == 2:
                        ilq_pump(1)
                    pend_av.append(av_emits)
                    if len(pend_av) > globals().get("AVLAG", 1):
                        for i0, i1, st, jp, Jx, slot, md in pend_av.pop(0):
                            av_last.append(em.op(
                                "tensor",
                                lambda e, i0=i0, i1=i1, st=st, jp=jp, Jx=Jx,
                                slot=slot, l=l, h=h, t0=t0:
                                mm(e, psR[0:65, i0 - t0:i1 - t0],
                                   bvr[l][:, slot * HEADS * 65 + h * 65:
                                          slot * HEADS * 65 + h * 65 + 65],
                                   expS[:, jp * 640 + i0 - 128 * Jx:jp * 640 + i1 - 128 * Jx],
                                   st, True),
                                r=("bv%d_%d" % (l, slot), "ex%d" % jp)
                                + bres("psR", i0 - t0, i1 - t0),
                                w=bres("psR", i0 - t0, i1 - t0),
                                dep=md, tag="av"))
                for i0, i1, st, jp, Jx, slot, md in (pend_av.pop(0) if pend_av else []):
                    av_last.append(em.op(
                        "tensor",
                        lambda e, i0=i0, i1=i1, st=st, jp=jp, Jx=Jx,
                        slot=slot, l=l, h=h, t0=t0:
                        mm(e, psR[0:65, i0 - t0:i1 - t0],
                           bvr[l][:, slot * HEADS * 65 + h * 65:
                                  slot * HEADS * 65 + h * 65 + 65],
                           expS[:, jp * 640 + i0 - 128 * Jx:jp * 640 + i1 - 128 * Jx],
                           st, True),
                        r=("bv%d_%d" % (l, slot), "ex%d" % jp)
                        + bres("psR", i0 - t0, i1 - t0),
                        w=bres("psR", i0 - t0, i1 - t0),
                        dep=md, tag="av"))

                def emit_drain(h=h, av_last=av_last):
                    o_rc = em.op("vector",
                                 lambda e: e.reciprocal(recbf[R64:R64 + 1, 0:T],
                                                        psR[R64:R64 + 1, 0:T]),
                                 r=bres("psR", 0, T), w=("rec",), dep=av_last,
                                 tag="drec")
                    for p in range(NP):
                        a = p * 512
                        o_bc = em.op("tensor",
                                     lambda e, a=a: mm(e, psX[0:64, 0:512],
                                                       ones_b[R64:R64 + 1, 0:64],
                                                       recbf[R64:R64 + 1, a:a + 512],
                                                       True, True),
                                     r=("rec", "onesb") + bres("psX", 0, 512),
                                     w=bres("psX", 0, 512), dep=[o_rc], tag="dbc")
                        o_cf = em.op("scalar",
                                     lambda e, a=a: e.copy(shs[0:64, a:a + 512],
                                                           psX[0:64, 0:512]),
                                     r=bres("psX", 0, 512), w=("fbc%d" % p,),
                                     dep=[o_bc], tag="dcf")
                        em.op("vector",
                              lambda e, a=a, h=h: e.tensor_mul(
                                  obf[0:64, h * T + a:h * T + a + 512],
                                  psR[0:64, a:a + 512], shs[0:64, a:a + 512]),
                              r=bres("psR", a, a + 512) + ("fbc%d" % p,),
                              w=("ob%d" % h,), dep=[o_cf] + av_last, tag="dmul")
                pend_drain.append(emit_drain)

            for q in range(QG):
                while len(ilq) > 4 * (QG - 1 - q):
                    ilq_pump(1)
                attn_head(2 * q)
                attn_head(2 * q + 1)

            if c == 0:
                dump("kpr%d" % l, kpr[l],
                     tuple("kp%d_%d" % (l, s) for s in range(NSLOT)))
                dump("qp%d" % l, qk, tuple("qk%d" % i for i in range(4)))
                dump("bvr%d" % l, bvr[l],
                     tuple("bv%d_%d" % (l, s) for s in range(NSLOT)))
                dump("obf%d" % l, obf, tuple("ob%d" % h for h in range(HEADS)))

            while pend_drain:
                pend_drain.pop(0)()

            for m in range(8):
                wt, wld, wres = wload(("wo", l, m))
                for cc2 in range(NP):
                    ps, pn = (psA, "psA") if (m * NP + cc2) % 2 == 0 else (psB, "psB")
                    prev = []
                    for h in range(HEADS):
                        scan_pump(1)
                        prev.append(em.op(
                            "tensor",
                            lambda e, h=h, cc2=cc2, ps=ps, wt=wt:
                            mm(e, ps[:, 0:512], wt[0:64, h * 128:(h + 1) * 128],
                               obf[0:64, h * T + cc2 * 512:h * T + cc2 * 512 + 512],
                               h == 0, h == HEADS - 1),
                            r=("ob%d" % h, wres) + (bres(pn, 0, 512) if h == 0 else ()),
                            w=bres(pn, 0, 512) if h == 0 else (),
                            dep=[wld] + prev[-1:], tag="womm"))
                    em.op("vector",
                          lambda e, m=m, cc2=cc2, ps=ps:
                          e.tensor_add(xfd(m, cc2 * 512, cc2 * 512 + 512),
                                       ps[:, 0:512],
                                       xfd(m, cc2 * 512, cc2 * 512 + 512)),
                          r=bres(pn, 0, 512) + ("xf%d" % m,), w=("xf%d" % m,),
                          dep=prev[-1:], tag="woev")

            if c == 0:
                dump("xwo%d" % l, xffn, tuple("xf%d" % g for g in range(G)))

            hres = "hB" if l == 0 else "hD"
            hcol = 0 if l == 0 else 16
            o_hr = em.op("vector",
                         lambda e, hcol=hcol: e.tensor_copy(
                             xffn.rearrange("p (g t) -> p g t", g=G)[:, :, 0:1],
                             halos[:, hcol:hcol + 8].rearrange("p (g o) -> p g o", o=1)),
                         r=(hres,), w=tuple("xf%d" % g for g in range(G)),
                         tag="hrest")
            o_hs = em.op("vector",
                         lambda e, hcol=hcol: e.tensor_copy(
                             halos[:, hcol:hcol + 8].rearrange("p (g o) -> p g o", o=1),
                             xffn.rearrange("p (g t) -> p g t", g=G)[:, :, T:T + 1]),
                         r=tuple("xf%d" % g for g in range(G)), w=(hres,),
                         tag="hsave")

            for p in range(NP):
                a = p * 512
                prev = []
                for g in range(G):
                    base = g * (T + 1)
                    o_c = em.op("scalar",
                                lambda e, base=base, a=a:
                                e.copy(sqt[:, 0:513],
                                       xffn[:, base + a:base + a + 513]),
                                r=("xf%d" % g,), w=("sqtA",), dep=[o_hr],
                                tag="l2c")
                    o_s = em.op("scalar",
                                lambda e, base=base, a=a:
                                e.activation(sqt[:, 513:1026],
                                             xffn[:, base + a:base + a + 513],
                                             AF.Square),
                                r=("xf%d" % g,), w=("sqtB",), dep=[o_hr],
                                tag="l2s")
                    off = 0 if g >= 4 else 1
                    o1 = em.op("tensor",
                               lambda e, off=off, first=(g == 0), g=g:
                               mm(e, psA[R64:R64 + 1, 0:512], ones_b[:, 0:1],
                                  sqt[:, off:off + 512], first, g == G - 1),
                               r=("sqtA", "onesb") + (bres("psA", 0, 512) if g == 0 else ()),
                               w=bres("psA", 0, 512) if g == 0 else (),
                               dep=[o_c] + prev, tag="l2m")
                    prev = [o1]
                    o2 = em.op("tensor",
                               lambda e, off=off, first=(g == 0), g=g:
                               mm(e, psA[R64:R64 + 1, 512:1024], ones_b[:, 0:1],
                                  sqt[:, 513 + off:513 + off + 512], first,
                                  g == G - 1),
                               r=("sqtB", "onesb") + (bres("psA", 512, 1024) if g == 0 else ()),
                               w=bres("psA", 512, 1024) if g == 0 else (),
                               dep=[o_s] + prev, tag="l2ms")
                    prev = [o2]
                o_a, o_b, _ = ln_derive(prev, (shs, 1536, "l2a"),
                                        (shs, 2048, "l2b"), "l2d_%d_" % p)
                bc_a = bcast64("psX", 0, shs, 1536, "l2a", [o_a], tag="l2bca")
                bc_b = bcast64("psR", 0, shs, 2048, "l2b", [o_b], tag="l2bcb")
                for g in range(G):
                    sl = g % 2
                    o1 = em.op("vector",
                               lambda e, g=g, a=a, sl=sl:
                               e.tensor_mul(scr[:, sl * 512:sl * 512 + 512],
                                            xfs(g, a, a + 512), psX[0:128, 0:512]),
                               r=("xf%d" % g,) + bres("psX", 0, 512),
                               w=("scr%d" % sl,), dep=[bc_a], tag="xh1")
                    o2 = em.op("vector",
                               lambda e, g=g, sl=sl:
                               e.tensor_add(scr[:, sl * 512:sl * 512 + 512],
                                            scr[:, sl * 512:sl * 512 + 512],
                                            psR[0:128, 0:512]),
                               r=("scr%d" % sl,) + bres("psR", 0, 512),
                               w=("scr%d" % sl,), dep=[bc_b, o1], tag="xh2")
                    em.op("scalar",
                          lambda e, g=g, a=a, sl=sl, l=l:
                          e.activation(xnd(g, a, a + 512),
                                       scr[:, sl * 512:sl * 512 + 512],
                                       AF.Identity, bias=cpc(CP_LB + 8 * l + g),
                                       scale=cpc(CP_LG + 8 * l + g)),
                          r=("scr%d" % sl, "cpat"), w=("xa%d" % g,),
                          dep=[o2], tag="xh3")

            if c == 0:
                dump("xh%d" % l, xnar, tuple("xa%d" % g for g in range(G)))

            for cc2 in range(NP):
                a = cc2 * 512
                for m in range(NW1):
                    w = W1M[m]
                    wt, wld, wres = wload(("w1", l, m))
                    prevA, prevB = [], []
                    for k in range(G):
                        scan_pump(1)
                        prevA.append(em.op(
                            "tensor",
                            lambda e, k=k, w=w, a=a, wt=wt:
                            mm(e, psA[0:w, 0:512], wt[:, k * w:(k + 1) * w],
                               xnd(k, a, a + 512), k == 0, k == G - 1),
                            r=("xa%d" % k, wres) + (bres("psA", 0, 512) if k == 0 else ()),
                            w=bres("psA", 0, 512) if k == 0 else (),
                            dep=[wld] + prevA[-1:], tag="w1a"))
                        scan_pump(1)
                        prevB.append(em.op(
                            "tensor",
                            lambda e, k=k, w=w, a=a, wt=wt:
                            mm(e, psB[0:w, 0:512],
                               wt[:, (G + k) * w:(G + k + 1) * w],
                               xnd(k, a, a + 512), k == 0, k == G - 1),
                            r=("xa%d" % k, wres) + (bres("psB", 0, 512) if k == 0 else ()),
                            w=bres("psB", 0, 512) if k == 0 else (),
                            dep=[wld] + prevB[-1:], tag="w1g"))
                    o_g = em.op("scalar",
                                lambda e, w=w, m=m, l=l:
                                e.activation(scr[:, 1024:1024 + 512][0:w, :],
                                             psB[0:w, 0:512], AF.Gelu,
                                             bias=cpc(CP_B1G + NW1 * l + m, w)),
                                r=bres("psB", 0, 512) + ("cpat",), w=("scr2",),
                                dep=prevB[-1:], tag="gelu")
                    em.op("vector",
                          lambda e, w=w, m=m, l=l:
                          e.scalar_tensor_tensor(
                              h2p[0:w, m * 512:m * 512 + 512],
                              psA[0:w, 0:512], cpc(CP_B1 + NW1 * l + m, w),
                              scr[0:w, 1024:1024 + 512], ALU.add, ALU.mult),
                          r=bres("psA", 0, 512) + ("scr2", "cpat"),
                          w=H2_RES + ("h2t%d" % m,), dep=[o_g] + prevA[-1:],
                          tag="h2")
                for mo in range(8):
                    wt, wld, wres = wload(("w2", l, mo))
                    prev = []
                    for k in range(NW1):
                        kk = W1M[k]
                        scan_pump(1)
                        prev.append(em.op(
                            "tensor",
                            lambda e, k=k, kk=kk, wt=wt:
                            mm(e, psR[0:128, 0:512], wt[0:kk, k * 128:(k + 1) * 128],
                               h2p[0:kk, k * 512:k * 512 + 512],
                               k == 0, k == NW1 - 1),
                            r=("h2t%d" % k, wres) + (bres("psR", 0, 512) if k == 0 else ()),
                            w=bres("psR", 0, 512) if k == 0 else (),
                            dep=[wld] + prev[-1:], tag="w2mm"))
                    em.op("vector",
                          lambda e, mo=mo, a=a, l=l:
                          e.scalar_tensor_tensor(
                              xfd(mo, a, a + 512), psR[0:128, 0:512],
                              cpc(CP_B2 + 8 * l + mo),
                              xfd(mo, a, a + 512), ALU.add, ALU.add),
                          r=bres("psR", 0, 512) + ("cpat", "xf%d" % mo),
                          w=("xf%d" % mo,), dep=prev[-1:], tag="w2ev")

            if l == 0:
                o_hr2 = em.op("vector",
                              lambda e: e.tensor_copy(
                                  xffn.rearrange("p (g t) -> p g t", g=G)[:, :, 0:1],
                                  halos[:, 8:16].rearrange("p (g o) -> p g o", o=1)),
                              r=("hC",), w=tuple("xf%d" % g for g in range(G)),
                              tag="hrC")
                em.op("vector",
                      lambda e: e.tensor_copy(
                          halos[:, 8:16].rearrange("p (g o) -> p g o", o=1),
                          xffn.rearrange("p (g t) -> p g t", g=G)[:, :, T:T + 1]),
                      r=tuple("xf%d" % g for g in range(G)), w=("hC",),
                      tag="hsC")
                for g in range(G):
                    em.op("scalar",
                          lambda e, g=g: e.copy(
                              x2[:, g * (T + 1):(g + 1) * (T + 1)],
                              xffn[:, g * (T + 1):(g + 1) * (T + 1)]),
                          r=("xf%d" % g,), w=X2_RES, dep=[o_hr2], tag="x2c")

        if c == 0:
            dump("xw2", xffn, tuple("xf%d" % g for g in range(G)))
            dump("H", Hbuf, ("H",))

        scan_pump(T)
        for g in range(G):
            em.op("scalar",
                  lambda e, g=g: e.activation(xfd(g, 0, T), xfd(g, 0, T),
                                              AF.Silu),
                  r=("xf%d" % g,), w=("xf%d" % g,), tag="silu")
        h2rd = [em.res.get("h2t%d" % m, [None, []]) for m in range(NW1)]
        h2deps = [o for e2 in h2rd for o in ([e2[0]] if e2[0] else []) + e2[1]]
        xin2 = em.op("sync",
                     lambda e, t0=t0: e.dma_start(
                         arena[:, 0:G * T].rearrange("p (g t) -> p g t", g=G),
                         xT[:, t0:t0 + T].rearrange("(g p) t -> p g t", p=128)),
                     w=AR_ALL, dma=True, dep=h2deps, tag="xre%d" % c)
        for p in range(NP):
            a = p * 512
            bcA = bcast64("psX", 0, strips, 2 * T + a, "stA2", [], tag="cbcA")
            bcB = bcast64("psR", 512, strips, 3 * T + a, "stB2", [], tag="cbcB")
            for g in range(G):
                sl = g % 2
                o1 = em.op("vector",
                           lambda e, g=g, a=a, sl=sl:
                           e.tensor_mul(scr[:, sl * 512:sl * 512 + 512],
                                        xg(g, a, a + 512), psX[0:128, 0:512]),
                           r=("arQ%d" % (g // 2),) + bres("psX", 0, 512),
                           w=("scr%d" % sl,), dep=[bcA, xin2], tag="c1")
                o2 = em.op("vector",
                           lambda e, g=g, sl=sl:
                           e.tensor_add(scr[:, sl * 512:sl * 512 + 512],
                                        scr[:, sl * 512:sl * 512 + 512],
                                        psR[0:128, 512:1024]),
                           r=("scr%d" % sl,) + bres("psR", 512, 1024),
                           w=("scr%d" % sl,), dep=[bcB, o1], tag="c2")
                em.op("scalar",
                      lambda e, g=g, a=a, sl=sl:
                      e.activation(xnd(g, a, a + 512),
                                   scr[:, sl * 512:sl * 512 + 512],
                                   AF.Identity, bias=cpc(CP_LNB + g),
                                   scale=cpc(CP_LNW + g)),
                      r=("scr%d" % sl, "cpat"), w=("xa%d" % g,), dep=[o2],
                      tag="c3")
        o_hb = em.op("scalar", lambda e: e.copy(
                         hbf, Hbuf.rearrange("p (t u) -> p t u", u=2)[0:DST, 1:T + 1, 0:1]),
                     r=("H",), w=("hbf",), tag="hcast")
        for g in range(G):
            wt, wld, wres = wload(("wg", g))
            womt, womld, womres = wload(("wom",))
            for p in range(NP):
                a = p * 512
                prev = []
                for k in range(G):
                    scan_pump(1)
                    prev.append(em.op(
                        "tensor",
                        lambda e, k=k, a=a, wt=wt: mm(e, psA[0:128, 0:512],
                                               wt[:, k * 128:(k + 1) * 128],
                                               xnd(k, a, a + 512),
                                               k == 0, k == G - 1),
                        r=("xa%d" % k, wres) + (bres("psA", 0, 512) if k == 0 else ()),
                        w=bres("psA", 0, 512) if k == 0 else (),
                        dep=[wld] + prev[-1:], tag="gmm"))
                o_om = em.op("tensor",
                             lambda e, g=g, a=a, womt=womt:
                             mm(e, psB[0:128, 0:512],
                                womt[0:DST, g * 128:(g + 1) * 128],
                                hbf[:, a:a + 512], True, True),
                             r=("hbf", womres) + bres("psB", 0, 512),
                             w=bres("psB", 0, 512), dep=[womld, o_hb],
                             tag="ommm")
                o_sg = em.op("scalar",
                             lambda e, g=g: e.activation(scr[:, 0:512],
                                                         psA[0:128, 0:512],
                                                         AF.Sigmoid,
                                                         bias=cpc(CP_BG + g)),
                             r=bres("psA", 0, 512) + ("cpat",), w=("scr0",),
                             dep=prev[-1:], tag="sig")
                o_s = em.op("vector",
                            lambda e, g=g: e.scalar_tensor_tensor(
                                scr[:, 512:1024], psB[0:128, 0:512],
                                cpc(CP_BO + g), scr[:, 0:512],
                                ALU.add, ALU.mult),
                            r=bres("psB", 0, 512) + ("cpat", "scr0"),
                            w=("scr1",), dep=[o_om, o_sg], tag="sg")
                bcA2 = bcast64("psX", 0, strips, a, "stA", [], tag="fbca")
                bcB2 = bcast64("psR", 512, strips, T + a, "stB", [], tag="fbcb")
                o_t = em.op("vector",
                            lambda e, g=g, a=a: e.tensor_mul(
                                scr[:, 1024:1536], xg(g, a, a + 512),
                                psX[0:128, 0:512]),
                            r=("arQ%d" % (g // 2),) + bres("psX", 0, 512),
                            w=("scr2",), dep=[bcA2, xin2], tag="ft")
                o_v = em.op("vector",
                            lambda e: e.tensor_add(scr[:, 1024:1536],
                                                   scr[:, 1024:1536],
                                                   psR[0:128, 512:1024]),
                            r=("scr2",) + bres("psR", 512, 1024),
                            w=("scr2",), dep=[bcB2, o_t], tag="fv")
                o_w = em.op("vector",
                            lambda e: e.tensor_add(scr[:, 1024:1536],
                                                   scr[:, 1024:1536],
                                                   scr[:, 512:1024]),
                            r=("scr2", "scr1"), w=("scr2",), dep=[o_v, o_s],
                            tag="fw")
                o_z = em.op("vector",
                            lambda e, g=g, a=a: e.tensor_mul(
                                scr[:, 1024:1536], scr[:, 1024:1536],
                                xfd(g, a, a + 512)),
                            r=("scr2", "xf%d" % g), w=("scr2",), dep=[o_w],
                            tag="fz")
                o_f = em.op("vector",
                            lambda e, g=g, a=a: e.tensor_add(
                                xfd(g, a, a + 512), scr[:, 1024:1536],
                                xg(g, a, a + 512)),
                            r=("scr2", "arQ%d" % (g // 2)),
                            w=("xf%d" % g,), dep=[o_z, xin2], tag="ff")
                em.op("gpsimd",
                      lambda e, g=g, a=a, t0=t0: e.dma_start(
                          yout[:, g * S + t0 + a:g * S + t0 + a + 512],
                          xfd(g, a, a + 512)),
                      r=("xf%d" % g,), dma=True, force_inc=True,
                      dep=[o_f], tag="yout")

    emit_program(nc, em)
    return nc


def _host_prep(inputs, S):
    f = inputs
    wblob = pack_wblob(f)
    TQc, TQs, TKc, TKs, base64 = rope_tables(S)
    tabs = np.concatenate([TQc, TQs, TKc, TKs], 1)
    cpat = pack_cpat(f, base64)
    cmisc = const_misc(np.asarray(inputs["L0_attn_A"], np.float32).T)
    x = np.asarray(f["L0_attn_x"], np.float32)
    in_maps = []
    for b in range(NCORE):
        in_maps.append({
            "xT": np.ascontiguousarray(x[b].T),
            "wblob": wblob,
            "tabs": tabs,
            "cpin": cpat,
            "cmin": cmisc,
        })
    return in_maps


def run(inputs, S=S_FULL, T=T_CHUNK, verbose=False, dbg=False):
    in_maps = _host_prep(inputs, S)
    nc = build_nc(S, T, verbose=verbose, dbg=dbg)
    res = run_bass_kernel_spmd(nc, in_maps, core_ids=list(range(NCORE)))
    outs = []
    for b in range(NCORE):
        y = res.results[b]["y"]
        y = y.reshape(128, G, S).transpose(1, 0, 2).reshape(D, S)
        outs.append(y.T)
    out = np.stack(outs, 0).astype(np.float32)
    if dbg:
        return out, [{k: v for k, v in res.results[b].items() if k.startswith("dbg_")}
                     for b in range(NCORE)]
    return out


def kernel(**inputs):
    return run(inputs, S=S_FULL, T=T_CHUNK)


# version 05
# speedup vs baseline: 1.0305x; 1.0305x over previous
import numpy as np
import ml_dtypes
import concourse.bass as bass
import concourse.mybir as mybir
from concourse.bass_utils import run_bass_kernel_spmd

F32 = mybir.dt.float32
BF16 = mybir.dt.bfloat16
AF = mybir.ActivationFunctionType
ALU = mybir.AluOpType

D = 1024
G = 8
DST = 16
HEADS = 8
DH = 64
INNER = 512
QG = 4
WIN = 512
HID = 2730
QKSCALE = 8.0
LNEPS = 1e-5
NCORE = 8

S_FULL = 8192
T_CHUNK = 1024

W1M = [128] * 21 + [42]
NW1 = len(W1M)

def build_wplan():
    plan = {}
    col = [0]

    def grp(key, width):
        a = col[0]
        plan[key] = (a, a + width)
        col[0] += width

    grp(("u",), 2 * G * DST)
    for l in range(2):
        for m in range(8):
            grp(("qkv", l, m), G * 128)
        grp(("v", l, 0), (G // 2) * 512)
        grp(("v", l, 1), (G // 2) * 512)
        for m in range(8):
            grp(("wo", l, m), HEADS * 128)
        for m in range(NW1):
            grp(("w1", l, m), 2 * G * W1M[m])
        for m in range(8):
            grp(("w2", l, m), NW1 * 128)
    for m in range(8):
        grp(("wg", m), G * 128)
    grp(("wom",), 8 * 128)
    return plan, col[0]


WPLAN, WCOLS = build_wplan()


def pack_wblob(f):
    blob = np.zeros((128, WCOLS), np.float32)

    def put(key, mat):
        a, b = WPLAN[key]
        assert mat.shape[1] == b - a and mat.shape[0] <= 128, (key, mat.shape)
        blob[: mat.shape[0], a:b] = mat

    WsT = np.ascontiguousarray(np.asarray(f["L0_attn_Ws"], np.float32).T)
    WiT = np.ascontiguousarray(np.asarray(f["L0_attn_Wi"], np.float32).T)
    put(("u",), np.concatenate(
        [WsT[128 * g:128 * (g + 1)] for g in range(G)]
        + [WiT[128 * g:128 * (g + 1)] for g in range(G)], axis=1))

    wqkv = np.asarray(f["L0_ffn_wqkv"], np.float32)
    wo = np.asarray(f["L0_ffn_wo"], np.float32)
    w1 = np.asarray(f["L0_ffn_w1"], np.float32)
    w2 = np.asarray(f["L0_ffn_w2"], np.float32)
    for l in range(2):
        for m in range(8):
            put(("qkv", l, m), np.concatenate(
                [wqkv[l][128 * k:128 * (k + 1), 128 * m:128 * (m + 1)]
                 for k in range(G)], 1))
        put(("v", l, 0), np.concatenate(
            [wqkv[l][128 * k:128 * (k + 1), 1024:1536] for k in range(4)], 1))
        put(("v", l, 1), np.concatenate(
            [wqkv[l][128 * k:128 * (k + 1), 1024:1536] for k in range(4, G)], 1))
        for m in range(8):
            tiles = []
            for h in range(HEADS):
                t = np.zeros((128, 128), np.float32)
                t[:64] = wo[l][64 * h:64 * (h + 1), 128 * m:128 * (m + 1)]
                tiles.append(t)
            put(("wo", l, m), np.concatenate(tiles, 1))
        for m in range(NW1):
            w = W1M[m]
            a_t = [w1[l][128 * k:128 * (k + 1), 128 * m:128 * m + w]
                   for k in range(G)]
            g_t = [w1[l][128 * k:128 * (k + 1), HID + 128 * m:HID + 128 * m + w]
                   for k in range(G)]
            put(("w1", l, m), np.concatenate(a_t + g_t, 1))
        for m in range(8):
            tiles = []
            for k in range(NW1):
                kk = min(128, HID - 128 * k)
                t = np.zeros((128, 128), np.float32)
                t[:kk] = w2[l][128 * k:128 * k + kk, 128 * m:128 * (m + 1)]
                tiles.append(t)
            put(("w2", l, m), np.concatenate(tiles, 1))
    WgT = np.ascontiguousarray(np.asarray(f["L0_attn_Wg"], np.float32).T)
    for m in range(8):
        put(("wg", m), np.concatenate(
            [WgT[128 * k:128 * (k + 1), 128 * m:128 * (m + 1)]
             for k in range(G)], 1))
    WomT = np.ascontiguousarray(np.asarray(f["L0_attn_Wo"], np.float32).T)
    tiles = []
    for m in range(8):
        t = np.zeros((128, 128), np.float32)
        t[:DST] = WomT[:, 128 * m:128 * (m + 1)]
        tiles.append(t)
    put(("wom",), np.concatenate(tiles, 1))
    return blob


def rope_tables(S):
    half = np.arange(0, DH, 2, dtype=np.float64)
    inv_freq = 1.0 / 10000.0 ** (half / DH)
    t = np.arange(S, dtype=np.float64)
    freqs = np.concatenate([t[:, None] * inv_freq] * 2, 1)
    base = np.concatenate([(half + 0.4 * DH) / (1.4 * DH)] * 2)
    cos, sin = np.cos(freqs), np.sin(freqs)
    pq = base[None, :] ** (((t % WIN) / (WIN // 2))[:, None])
    pk = base[None, :] ** ((-(t % WIN) / (WIN // 2))[:, None])

    def lay(tab):
        return np.concatenate([tab.T, tab.T], 0).astype(np.float32)

    return (lay(cos * pq), lay(sin * pq), lay(cos * pk), lay(sin * pk),
            base.astype(np.float32))


def const_misc(AT=None):
    P = np.zeros((128, 128), np.float32)
    for m in range(128):
        base = (m // 64) * 64
        dm = m % 64
        if dm < 32:
            P[base + dm + 32, m] = -1.0
        else:
            P[base + dm - 32, m] = 1.0
    triU = np.zeros((128, 128), np.float32)
    triL = np.zeros((128, 128), np.float32)
    for p in range(128):
        triU[p, p:] = 1.0
        triL[p, :p + 1] = 1.0
    at = np.zeros((128, 16), np.float32)
    if AT is not None:
        at[:16] = AT
    return np.concatenate([P, triU, triL, at], 1).astype(ml_dtypes.bfloat16)


CP_BSQ = 0
CP_QS = {}
_c = 1
for _l in range(2):
    for _k in range(4):
        CP_QS[(_l, _k)] = _c
        _c += 1
CP_AT = _c; _c += 16
CP_BS = _c; _c += 1
CP_BI = _c; _c += 1
CP_BO = _c; _c += 8
CP_BG = _c; _c += 8
CP_LNW = _c; _c += 8
CP_LNB = _c; _c += 8
CP_LG = _c; _c += 16
CP_LB = _c; _c += 16
CP_B1 = _c; _c += 2 * NW1
CP_B1G = _c; _c += 2 * NW1
CP_B2 = _c; _c += 16
CP_SEL = _c; _c += 128
CP_SIGN = _c; _c += 1
CPC = _c


def pack_cpat(f, base64):
    cp = np.zeros((128, CPC), np.float32)
    cp[:, CP_BSQ] = np.tile(base64 ** 2, 2)
    qs = np.asarray(f["L0_ffn_q_scale"], np.float32)
    ks = np.asarray(f["L0_ffn_k_scale"], np.float32)
    for l in range(2):
        cp[:, CP_QS[(l, 0)]] = np.tile(qs[l], 2)
        cp[:, CP_QS[(l, 1)]] = np.tile(np.concatenate([qs[l][32:], qs[l][:32]]), 2)
        cp[:, CP_QS[(l, 2)]] = np.tile(ks[l], 2)
        cp[:, CP_QS[(l, 3)]] = np.tile(np.concatenate([ks[l][32:], ks[l][:32]]), 2)
    cp[:DST, CP_AT:CP_AT + 16] = np.asarray(f["L0_attn_A"], np.float32).T
    cp[:DST, CP_BS] = np.asarray(f["L0_attn_bs"], np.float32)
    cp[:DST, CP_BI] = np.asarray(f["L0_attn_bi"], np.float32)
    for g in range(G):
        sl = slice(128 * g, 128 * (g + 1))
        cp[:, CP_BO + g] = np.asarray(f["L0_attn_bo"], np.float32)[sl]
        cp[:, CP_BG + g] = np.asarray(f["L0_attn_bg"], np.float32)[sl]
        cp[:, CP_LNW + g] = np.asarray(f["L0_attn_ln_w"], np.float32)[sl]
        cp[:, CP_LNB + g] = np.asarray(f["L0_attn_ln_b"], np.float32)[sl]
    for l in range(2):
        for g in range(G):
            sl = slice(128 * g, 128 * (g + 1))
            cp[:, CP_LG + 8 * l + g] = np.asarray(f["L0_ffn_ln_g"], np.float32)[l][sl]
            cp[:, CP_LB + 8 * l + g] = np.asarray(f["L0_ffn_ln_b"], np.float32)[l][sl]
        b1 = np.asarray(f["L0_ffn_b1"], np.float32)[l]
        for m in range(NW1):
            w = W1M[m]
            cp[:w, CP_B1 + NW1 * l + m] = b1[128 * m:128 * m + w]
            cp[:w, CP_B1G + NW1 * l + m] = b1[HID + 128 * m:HID + 128 * m + w]
        b2 = np.asarray(f["L0_ffn_b2"], np.float32)[l]
        for g in range(G):
            cp[:, CP_B2 + 8 * l + g] = b2[128 * g:128 * (g + 1)]
    cp[64, CP_SEL:CP_SEL + 64] = 1.0
    cp[65, CP_SEL + 64:CP_SEL + 128] = 1.0
    sgn = np.where(np.arange(128) % 64 < 32, -1.0, 1.0)
    cp[:, CP_SIGN] = sgn
    return cp


class Op:
    __slots__ = ("eng", "fn", "deps", "idx", "inc", "cum", "kind", "tag")

    def __init__(self, eng, fn, deps, idx, kind, tag):
        self.eng, self.fn, self.deps, self.idx = eng, fn, deps, idx
        self.inc, self.cum, self.kind, self.tag = 0, 0, kind, tag


class Em:
    def __init__(self):
        self.ops = []
        self.res = {}

    def op(self, eng, fn, r=(), w=(), dep=(), dma=False, tag="", force_inc=False):
        deps, seen = [], set()

        def add(o):
            if o is not None and id(o) not in seen:
                seen.add(id(o))
                deps.append(o)

        for res in r:
            e = self.res.get(res)
            if e and e[0] is not None:
                add(e[0])
        for res in w:
            e = self.res.get(res)
            if e:
                if e[0] is not None:
                    add(e[0])
                for rd in e[1]:
                    add(rd)
        for d in dep:
            add(d)
        o = Op(eng, fn, deps, len(self.ops), "d" if dma else "c", tag)
        if force_inc:
            o.inc = 1
        self.ops.append(o)
        for res in r:
            self.res.setdefault(res, [None, []])[1].append(o)
        for res in w:
            self.res[res] = [o, []]
        return o


ELIDE = True
STREAMS = [("tensor", "c"), ("vector", "c"), ("scalar", "c"),
           ("gpsimd", "c"), ("gpsimd", "d"), ("sync", "d")]


def emit_program(nc, em):
    for o in em.ops:
        if o.kind == "d":
            o.inc = 1
        for d in o.deps:
            if (d.eng, d.kind) != (o.eng, o.kind) or d.kind == "d":
                d.inc = 1
    cum = {s: 0 for s in STREAMS}
    for o in em.ops:
        s = (o.eng, o.kind)
        if o.inc:
            cum[s] += 16 if o.kind == "d" else 1
        o.cum = cum[s]
    totals = dict(cum)

    sem_cm = {s: nc.semaphore("sem_%s_%s" % s) for s in STREAMS}
    sem = {s: cm.__enter__() for s, cm in sem_cm.items()}

    know = {e: {s2: 0 for s2 in STREAMS} for e in
            ("tensor", "vector", "scalar", "gpsimd", "sync")}
    opclock = {}
    by_eng = {}
    for o in em.ops:
        by_eng.setdefault(o.eng, []).append(o)

    def run_engine(engname):
        def body(eng):
            kn = know[engname]
            for o in by_eng.get(engname, []):
                s_own = (o.eng, o.kind)
                for d in o.deps:
                    ds = (d.eng, d.kind)
                    if ds[0] == engname and ds[1] == "c":
                        continue
                    tgt = d.cum
                    if kn.get(ds, 0) >= tgt:
                        continue
                    eng.wait_ge(sem[ds], tgt)
                    kn[ds] = tgt
                    if ELIDE:
                        dk = opclock.get(id(d))
                        if dk:
                            for s2, v in dk.items():
                                if kn.get(s2, 0) < v:
                                    kn[s2] = v
                instr = o.fn(eng)
                if o.inc:
                    if instr is None:
                        raise RuntimeError("inc op returned no instr: " + o.tag)
                    instr.then_inc(sem[s_own], 16 if o.kind == "d" else 1)
                    snap = dict(kn)
                    snap[s_own] = o.cum
                    opclock[id(o)] = snap
        return body

    with nc.allow_low_precision(reason="bf16 operands intended"), \
            nc.Block() as block:
        block.tensor(run_engine("tensor"))
        block.vector(run_engine("vector"))
        block.scalar(run_engine("scalar"))
        block.gpsimd(run_engine("gpsimd"))

        def sync_body(eng):
            run_engine("sync")(eng)
            if totals[("gpsimd", "d")]:
                eng.wait_ge(sem[("gpsimd", "d")], totals[("gpsimd", "d")])
            if totals[("sync", "d")]:
                eng.wait_ge(sem[("sync", "d")], totals[("sync", "d")])
        block.sync(sync_body)

    for cm in sem_cm.values():
        cm.__exit__(None, None, None)


def build_nc(S, T, verbose=False, dbg=False):
    NT = S // T
    NP = T // 512
    NSLOT = (T + WIN) // 128
    KRING = T + WIN
    NJ = T // 128

    nc = bass.Bass("TRN2", target_bir_lowering=False)

    xT = nc.dram_tensor("xT", [D, S], F32, kind="ExternalInput").ap()
    wblob = nc.dram_tensor("wblob", [128, WCOLS], F32, kind="ExternalInput").ap()
    tabs = nc.dram_tensor("tabs", [128, 4 * S], F32, kind="ExternalInput").ap()
    cpat_in = nc.dram_tensor("cpin", [128, CPC], F32, kind="ExternalInput").ap()
    cmisc_in = nc.dram_tensor("cmin", [128, 400], BF16, kind="ExternalInput").ap()
    yout = nc.dram_tensor("y", [128, G * S], F32, kind="ExternalOutput").ap()
    wscr = nc.dram_tensor("wscr", [128, WCOLS], BF16, kind="Internal").ap()
    dbg_outs = {}

    AL = nc.alloc_sbuf_tensor

    SB_TOP = 229344
    ob_bytes = HEADS * T * 2
    OB_BASE = SB_TOP - ob_bytes
    arena_bytes = (G * T + 64) * 4
    ARENA_BASE = OB_BASE - arena_bytes
    arena = nc.alloc_sbuf_tensor_at("arena", [128, G * T + 64], F32,
                                    offset=ARENA_BASE).ap()
    xnar = AL("xnar", [128, G * (T + 1)], BF16).ap()
    xffn = AL("xffn", [128, G * (T + 1)], F32).ap()
    qk = AL("qk", [128, QG * T], BF16).ap()
    obf = nc.alloc_sbuf_tensor_at("obf", [128, HEADS * T], BF16,
                                  offset=OB_BASE).ap()
    shs = AL("shs", [128, 2560], F32).ap()
    expS = AL("expS", [128, 2 * 640], BF16).ap()
    knx = AL("knx", [128, 2 * 128], BF16).ap()
    tbuf = AL("tbuf", [128, 1024], F32).ap()
    WSLOT = 2816
    wring = AL("wring", [128, 2 * WSLOT], BF16).ap()
    sqt = AL("sqt", [128, 2 * 513], BF16).ap()
    uh = AL("uh", [128, T + 8], F32).ap()
    hbf = AL("hbf", [DST, T], BF16).ap()
    Hbuf = AL("Hbuf", [DST, 2 * (T + 1)], BF16).ap()
    scr = AL("scr", [128, 3 * 512], F32).ap()
    scrb = AL("scrb", [128, 2 * 512], BF16).ap()
    cpat = AL("cpat", [128, CPC], F32).ap()
    cmisc = AL("cmisc", [128, 400], BF16).ap()
    ones_f = AL("ones_f", [128, 128], F32).ap()
    ones_b = AL("ones_b", [128, 128], BF16).ap()
    blk2 = AL("blk2", [128, 2], BF16).ap()
    halos = AL("halos", [128, 32], F32).ap()
    dconst = AL("dconst", [128, 8], F32).ap()
    kpr = [AL("kpr%d" % l, [128, QG * KRING], BF16).ap() for l in range(2)]
    bvr = [AL("bvr%d" % l, [128, NSLOT * HEADS * 65], BF16).ap() for l in range(2)]

    h2p = nc.alloc_sbuf_tensor_at("h2p", [128, NW1 * 512], BF16,
                                  offset=ARENA_BASE).ap()
    x2 = nc.alloc_sbuf_tensor_at("x2b", [128, G * (T + 1)], BF16,
                                 offset=ARENA_BASE).ap()
    strips = nc.alloc_sbuf_tensor_at("strips", [128, 4 * T], F32,
                                     offset=OB_BASE).ap()
    recbf = AL("recbf", [128, 1024], BF16).ap()

    rem = nc.sbuf_bytes_remaining
    if verbose:
        print("SBUF bytes remaining:", rem)
    assert rem >= (SB_TOP - ARENA_BASE) + 512, "allocator near pinned slabs"

    psA = nc.alloc_psum_tensor("psA", [128, 1024], F32).ap()
    psB = nc.alloc_psum_tensor("psB", [128, 1024], F32).ap()
    psR = nc.alloc_psum_tensor("psR", [128, 1024], F32).ap()
    psX = nc.alloc_psum_tensor("psX", [128, 512], F32).ap()
    psS = nc.alloc_psum_tensor("psS", [128, 512], F32).ap()
    PRES = {"psA": 0, "psB": 2, "psR": 4, "psX": 6, "psS": 7}

    em = Em()

    def bres(name, lo, hi):
        base = PRES[name]
        return tuple("pb%d" % (base + b) for b in range(lo // 512, (hi + 511) // 512))

    def mm(e, out, lhsT, rhs, start, stop):
        return e.matmul(out, lhsT, rhs, start=start, stop=stop,
                        skip_group_check=True)

    def cpc(c, rows=128):
        return cpat[0:rows, c:c + 1]

    def xg(g, a, b):
        return arena[:, g * T + a:g * T + b]

    def xfd(g, a, b):
        base = g * (T + 1)
        return xffn[:, base + 1 + a:base + 1 + b]

    def xfs(g, a, b):
        base = g * (T + 1)
        o = 0 if g >= 4 else 1
        return xffn[:, base + o + a:base + o + b]

    def xnd(g, a, b):
        base = g * (T + 1)
        return xnar[:, base + 1 + a:base + 1 + b]

    def xns(g, a, b):
        base = g * (T + 1)
        o = 0 if g >= 4 else 1
        return xnar[:, base + o + a:base + o + b]

    def x2d(g, a, b):
        base = g * (T + 1)
        return x2[:, base + 1 + a:base + 1 + b]

    def x2s(g, a, b):
        base = g * (T + 1)
        o = 0 if g >= 4 else 1
        return x2[:, base + o + a:base + o + b]

    AR_ALL = ("arQ0", "arQ1", "arQ2", "arQ3")
    H2_RES = ("arQ0", "arQ1", "arQ2")
    X2_RES = ("arQ0", "arQ1")

    wslot = [0]
    first_touch = set()

    def wload(key, nslots=1):
        a, b = WPLAN[key]
        width = b - a
        slot = wslot[0] % 2
        wslot[0] += 1
        dst = wring[:, slot * WSLOT:slot * WSLOT + width]
        res_slot = "wr%d" % slot
        res_reg = "wreg%d" % a
        if key not in first_touch:
            first_touch.add(key)
            ld = em.op("gpsimd", lambda e, d=dst, s=wblob[:, a:b]: e.dma_start(d, s),
                       w=(res_slot,), dma=True, tag="wc%s" % (key,))
            em.op("gpsimd", lambda e, d=wscr[:, a:b], s=dst: e.dma_start(d, s),
                  r=(res_slot,), w=(res_reg,), dma=True, tag="ws%s" % (key,))
        else:
            ld = em.op("sync", lambda e, d=dst, s=wscr[:, a:b]: e.dma_start(d, s),
                       r=(res_reg,), w=(res_slot,), dma=True, tag="wl%s" % (key,))
        return dst, ld, res_slot

    scan = {"emitted": T, "udeps": [None, None]}

    def u_col(t):
        return uh[0:DST, t:t + 1]

    def H_col(t):
        return Hbuf[0:DST, 2 * t:2 * t + 1]

    pump_gate = [0]

    def scan_pump(n=1):
        if n == 1:
            pump_gate[0] += 1
            if pump_gate[0] % 2:
                return
        for _ in range(n):
            t = scan["emitted"]
            if t >= T:
                return
            udep = scan["udeps"][min(t // 512, NP - 1)]
            if udep is None:
                return
            o1 = em.op("tensor",
                       lambda e, t=t: mm(e, psS[0:DST, 0:1],
                                         cmisc[0:DST, 384:400],
                                         H_col(t), True, True),
                       r=("H", "cmisc") + bres("psS", 0, 1),
                       w=bres("psS", 0, 1), dep=[udep], tag="smm%d" % t)
            em.op("scalar",
                  lambda e, t=t: e.activation(H_col(t + 1), psS[0:DST, 0:1],
                                              AF.Tanh, bias=u_col(t)),
                  r=bres("psS", 0, 1) + ("u",), w=("H",), dep=[o1],
                  tag="stanh%d" % t)
            scan["emitted"] = t + 1

    R64 = 64

    def ln_derive(dep_ops, a_dst, b_dst, tag):
        a_ap, a_col, a_res = a_dst
        b_ap, b_col, b_res = b_dst
        o_mean = em.op("scalar",
                       lambda e: e.mul(shs[R64:R64 + 1, 0:512],
                                       psA[R64:R64 + 1, 0:512], 1.0 / D),
                       r=bres("psA", 0, 512), w=("shsm",), dep=dep_ops,
                       tag=tag + "mean")
        o_m2 = em.op("scalar",
                     lambda e: e.mul(shs[R64:R64 + 1, 512:1024],
                                     psA[R64:R64 + 1, 512:1024], 1.0 / D),
                     r=bres("psA", 512, 1024), w=("shsm2",), dep=dep_ops,
                     tag=tag + "m2")
        o_mm = em.op("vector",
                     lambda e: e.tensor_mul(shs[R64:R64 + 1, 1024:1536],
                                            shs[R64:R64 + 1, 0:512],
                                            shs[R64:R64 + 1, 0:512]),
                     r=("shsm",), w=("shs1",), dep=[o_mean], tag=tag + "mm")
        o_var = em.op("vector",
                      lambda e: e.tensor_sub(shs[R64:R64 + 1, 1024:1536],
                                             shs[R64:R64 + 1, 512:1024],
                                             shs[R64:R64 + 1, 1024:1536]),
                      r=("shsm2", "shs1"), w=("shs1",), dep=[o_m2, o_mm],
                      tag=tag + "var")
        o_sq = em.op("scalar",
                     lambda e: e.activation(shs[R64:R64 + 1, 1024:1536],
                                            shs[R64:R64 + 1, 1024:1536],
                                            AF.Sqrt, bias=dconst[R64:R64 + 1, 1:2]),
                     r=("shs1", "dconst"), w=("shs1",), dep=[o_var],
                     tag=tag + "sqrt")
        o_a = em.op("vector",
                    lambda e: e.reciprocal(a_ap[R64:R64 + 1, a_col:a_col + 512],
                                           shs[R64:R64 + 1, 1024:1536]),
                    r=("shs1",), w=(a_res,), dep=[o_sq], tag=tag + "a")
        o_ma = em.op("vector",
                     lambda e: e.tensor_mul(shs[R64:R64 + 1, 1024:1536],
                                            shs[R64:R64 + 1, 0:512],
                                            a_ap[R64:R64 + 1, a_col:a_col + 512]),
                     r=("shsm", a_res), w=("shs1",), dep=[o_a], tag=tag + "ma")
        o_b = em.op("scalar",
                    lambda e: e.mul(b_ap[R64:R64 + 1, b_col:b_col + 512],
                                    shs[R64:R64 + 1, 1024:1536], -1.0),
                    r=("shs1",), w=(b_res,), dep=[o_ma], tag=tag + "b")
        return o_a, o_b, o_mean

    def bcast64(dst_name, lo, src_ap, src_col, src_res, dep, tag="bc"):
        ps = {"psA": psA, "psB": psB, "psR": psR, "psX": psX}[dst_name]
        o = em.op("tensor",
                  lambda e: mm(e, ps[0:128, lo:lo + 512],
                               ones_f[R64:R64 + 1, 0:128],
                               src_ap[R64:R64 + 1, src_col:src_col + 512],
                               True, True),
                  r=(src_res, "onesf") + bres(dst_name, lo, lo + 512),
                  w=bres(dst_name, lo, lo + 512), dep=dep, tag=tag)
        return o

    def dump(name, ap, res, dep=()):
        if not dbg:
            return
        shp = list(ap.shape)
        dt = ap.dtype
        t = nc.dram_tensor("dbg_" + name, shp, dt, kind="ExternalOutput").ap()
        dbg_outs[name] = True
        em.op("gpsimd", lambda e, t=t, ap=ap: e.dma_start(t, ap),
              r=res, dma=True, force_inc=True, tag="dump" + name)

    em.op("sync", lambda e: e.dma_start(cpat, cpat_in), w=("cpat",), dma=True)
    em.op("sync", lambda e: e.dma_start(cmisc, cmisc_in), w=("cmisc",), dma=True)
    em.op("vector", lambda e: e.memset(ones_f, 1.0), w=("onesf",))
    em.op("vector", lambda e: e.memset(ones_b, 1.0), w=("onesb",))
    em.op("vector", lambda e: e.memset(blk2[0:64, 0:1], 1.0), w=("blk2",))
    em.op("vector", lambda e: e.memset(blk2[64:128, 0:1], 0.0), w=("blk2",))
    em.op("vector", lambda e: e.memset(blk2[0:64, 1:2], 0.0), w=("blk2",))
    em.op("vector", lambda e: e.memset(blk2[64:128, 1:2], 1.0), w=("blk2",))
    em.op("vector", lambda e: e.memset(halos, 0.0), w=("hB", "hC", "hD"))
    em.op("vector", lambda e: e.memset(H_col(0), 0.0), w=("H",))
    em.op("vector", lambda e: e.memset(dconst[:, 1:2], LNEPS), w=("dconst",))
    em.op("vector", lambda e: e.memset(dconst[:, 2:3], QKSCALE), w=("dconst",))
    em.op("vector",
          lambda e: e.tensor_add(dconst[0:DST, 0:1], cpat[0:DST, CP_BS:CP_BS + 1],
                                 cpat[0:DST, CP_BI:CP_BI + 1]),
          r=("cpat",), w=("dconst",))
    for _i, (_l, _k) in enumerate([(0, 1), (0, 3), (1, 1), (1, 3)]):
        em.op("vector",
              lambda e, _i=_i, _l=_l, _k=_k: e.tensor_mul(
                  dconst[:, 3 + _i:4 + _i], cpat[:, CP_QS[(_l, _k)]:CP_QS[(_l, _k)] + 1],
                  cpat[:, CP_SIGN:CP_SIGN + 1]),
              r=("cpat",), w=("dconst",), tag="sgn")
    for l in range(2):
        em.op("vector",
              lambda e, l=l: e.memset(
                  bvr[l].rearrange("p (s h c) -> p s h c", h=HEADS, c=65)[:, :, :, 64:65],
                  1.0),
              w=tuple("bv%d_%d" % (l, s) for s in range(NSLOT)))
    for g in range(G):
        em.op("vector", lambda e, g=g: e.memset(
            xnar[:, g * (T + 1):g * (T + 1) + 1], 0.0), w=("xa%d" % g,))
        em.op("vector", lambda e, g=g: e.memset(
            x2[:, g * (T + 1):g * (T + 1) + 1], 0.0), w=X2_RES)

    for c in range(NT):
        t0 = c * T
        scan["emitted"] = 0
        scan["udeps"] = [None, None]

        xin = em.op("sync",
                    lambda e, t0=t0: e.dma_start(
                        arena[:, 0:G * T].rearrange("p (g t) -> p g t", g=G),
                        xT[:, t0:t0 + T].rearrange("(g p) t -> p g t", p=128)),
                    w=AR_ALL, dma=True, tag="xload%d" % c)

        if c > 0:
            em.op("vector",
                  lambda e: e.tensor_copy(
                      xnar.rearrange("p (g t) -> p g t", g=G + 0, t=T + 1)[:, :, 0:1],
                      halos[:, 24:32].rearrange("p (g o) -> p g o", o=1)),
                  r=("hA",), w=tuple("xa%d" % g for g in range(G)),
                  tag="xnh%d" % c)
            em.op("vector", lambda e: e.tensor_copy(H_col(0), H_col(T)),
                  w=("H",), tag="hcar%d" % c)

        for g in range(G):
            em.op("scalar", lambda e, g=g: e.copy(xnd(g, 0, T), xg(g, 0, T)),
                  r=("arQ%d" % (g // 2),), w=("xa%d" % g,), dep=[xin],
                  tag="xbf%d" % g)

        ab1 = []
        for p in range(NP):
            a = p * 512
            prev = []
            for g in range(G):
                sl = g % 2
                o_sq = em.op("gpsimd",
                             lambda e, g=g, a=a, sl=sl:
                             e.tensor_mul(scrb[:, sl * 512:sl * 512 + 512],
                                          xnd(g, a, a + 512), xnd(g, a, a + 512)),
                             r=("xa%d" % g,), w=("scrb%d" % sl,), tag="sq")
                o1 = em.op("tensor",
                           lambda e, g=g, a=a, first=(g == 0):
                           mm(e, psA[R64:R64 + 1, 0:512], ones_b[:, 0:1],
                              xnd(g, a, a + 512), first, g == G - 1),
                           r=("xa%d" % g, "onesb") + (bres("psA", 0, 512) if g == 0 else ()),
                           w=bres("psA", 0, 512) if g == 0 else (),
                           dep=prev, tag="sm")
                prev = [o1]
                o2 = em.op("tensor",
                           lambda e, sl=sl, first=(g == 0), g=g:
                           mm(e, psA[R64:R64 + 1, 512:1024], ones_b[:, 0:1],
                              scrb[:, sl * 512:sl * 512 + 512], first, g == G - 1),
                           r=("scrb%d" % sl, "onesb") + (bres("psA", 512, 1024) if g == 0 else ()),
                           w=bres("psA", 512, 1024) if g == 0 else (),
                           dep=[o_sq] + prev, tag="sms")
                prev = [o2]
            ab1.append(ln_derive(prev, (strips, a, "stA"),
                                 (strips, T + a, "stB"), "ln1_%d_" % p))

        xn_done = []
        for p in range(NP):
            a = p * 512
            o_a, o_b, _ = ab1[p]
            bc_a = bcast64("psX", 0, strips, a, "stA", [o_a], tag="bca")
            bc_b = bcast64("psR", 0, strips, T + a, "stB", [o_b], tag="bcb")
            for g in range(G):
                sl = g % 2
                o1 = em.op("vector",
                           lambda e, g=g, a=a, sl=sl:
                           e.tensor_mul(scr[:, sl * 512:sl * 512 + 512],
                                        xg(g, a, a + 512), psX[0:128, 0:512]),
                           r=("arQ%d" % (g // 2),) + bres("psX", 0, 512),
                           w=("scr%d" % sl,), dep=[bc_a, xin], tag="xn1")
                o2 = em.op("vector",
                           lambda e, g=g, a=a, sl=sl:
                           e.tensor_add(xfd(g, a, a + 512),
                                        scr[:, sl * 512:sl * 512 + 512],
                                        psR[0:128, 0:512]),
                           r=("scr%d" % sl,) + bres("psR", 0, 512),
                           w=("xf%d" % g,), dep=[bc_b, o1], tag="xn2")
                o3 = em.op("scalar",
                           lambda e, g=g, a=a: e.copy(xnd(g, a, a + 512),
                                                      xfd(g, a, a + 512)),
                           r=("xf%d" % g,), w=("xa%d" % g,), dep=[o2], tag="xnc")
                xn_done.append(o3)

        if c == 0:
            dump("xar", arena, AR_ALL)
            dump("xnbf", xnar, tuple("xa%d" % g for g in range(G)))
            dump("xffn0", xffn, tuple("xf%d" % g for g in range(G)))
            dump("strips", strips, ("stA", "stB"))

        em.op("vector",
              lambda e: e.tensor_copy(
                  halos[:, 24:32].rearrange("p (g o) -> p g o", o=1),
                  xnar.rearrange("p (g t) -> p g t", g=G)[:, :, T:T + 1]),
              r=tuple("xa%d" % g for g in range(G)), w=("hA",),
              dep=xn_done[-1:], tag="hsA")

        ab2 = []
        for p in range(NP):
            a = p * 512
            prev = []
            for g in range(G):
                sl = g % 2
                o_sq = em.op("gpsimd",
                             lambda e, g=g, a=a, sl=sl:
                             e.tensor_mul(scrb[:, sl * 512:sl * 512 + 512],
                                          xnd(g, a, a + 512), xnd(g, a, a + 512)),
                             r=("xa%d" % g,), w=("scrb%d" % sl,), tag="sq2")
                o1 = em.op("tensor",
                           lambda e, g=g, a=a, first=(g == 0):
                           mm(e, psA[R64:R64 + 1, 0:512], ones_b[:, 0:1],
                              xnd(g, a, a + 512), first, g == G - 1),
                           r=("xa%d" % g, "onesb") + (bres("psA", 0, 512) if g == 0 else ()),
                           w=bres("psA", 0, 512) if g == 0 else (),
                           dep=prev, tag="s2")
                prev = [o1]
                o2 = em.op("tensor",
                           lambda e, sl=sl, first=(g == 0), g=g:
                           mm(e, psA[R64:R64 + 1, 512:1024], ones_b[:, 0:1],
                              scrb[:, sl * 512:sl * 512 + 512], first, g == G - 1),
                           r=("scrb%d" % sl, "onesb") + (bres("psA", 512, 1024) if g == 0 else ()),
                           w=bres("psA", 512, 1024) if g == 0 else (),
                           dep=[o_sq] + prev, tag="ss2")
                prev = [o2]
            o_r2, o_nm2r2, _ = ln_derive(prev, (strips, 2 * T + a, "stA2"),
                                         (strips, 3 * T + a, "stB2"),
                                         "ln2_%d_" % p)
            o_a, o_b, _ = ab1[p]
            o_br = em.op("vector",
                         lambda e, a=a: e.tensor_mul(
                             shs[R64:R64 + 1, 1024:1536],
                             strips[R64:R64 + 1, T + a:T + a + 512],
                             strips[R64:R64 + 1, 2 * T + a:2 * T + a + 512]),
                         r=("stA2", "stB", "shs1"), w=("shs1",),
                         dep=[o_r2, o_b], tag="br")
            o_A2 = em.op("vector",
                         lambda e, a=a: e.tensor_mul(
                             strips[R64:R64 + 1, 2 * T + a:2 * T + a + 512],
                             strips[R64:R64 + 1, 2 * T + a:2 * T + a + 512],
                             strips[R64:R64 + 1, a:a + 512]),
                         r=("stA2", "stA"), w=("stA2",),
                         dep=[o_br, o_a], tag="A2")
            o_B2 = em.op("vector",
                         lambda e, a=a: e.tensor_add(
                             strips[R64:R64 + 1, 3 * T + a:3 * T + a + 512],
                             strips[R64:R64 + 1, 3 * T + a:3 * T + a + 512],
                             shs[R64:R64 + 1, 1024:1536]),
                         r=("stB2", "shs1"), w=("stB2",),
                         dep=[o_nm2r2, o_br], tag="B2")
            ab2.append((o_A2, o_B2))

        wu, wu_ld, wu_res = wload(("u",))
        for p in range(NP):
            a = p * 512
            o_A2, o_B2 = ab2[p]
            bcA = bcast64("psX", 0, strips, 2 * T + a, "stA2", [o_A2], tag="bcA2")
            bcB = bcast64("psR", 0, strips, 3 * T + a, "stB2", [o_B2], tag="bcB2")
            prev_mm = []
            for g in range(G):
                sl = g % 2
                o1 = em.op("vector",
                           lambda e, g=g, a=a, sl=sl:
                           e.tensor_mul(scr[:, sl * 512:sl * 512 + 512],
                                        xg(g, a, a + 512), psX[0:128, 0:512]),
                           r=("arQ%d" % (g // 2),) + bres("psX", 0, 512),
                           w=("scr%d" % sl,), dep=[bcA, xin], tag="u1")
                o2 = em.op("vector",
                           lambda e, g=g, sl=sl:
                           e.tensor_add(scr[:, sl * 512:sl * 512 + 512],
                                        scr[:, sl * 512:sl * 512 + 512],
                                        psR[0:128, 0:512]),
                           r=("scr%d" % sl,) + bres("psR", 0, 512),
                           w=("scr%d" % sl,), dep=[bcB, o1], tag="u2")
                o3 = em.op("scalar",
                           lambda e, g=g, sl=sl:
                           e.activation(scrb[:, sl * 512:sl * 512 + 512],
                                        scr[:, sl * 512:sl * 512 + 512],
                                        AF.Identity, bias=cpc(CP_LNB + g),
                                        scale=cpc(CP_LNW + g)),
                           r=("scr%d" % sl, "cpat"), w=("scrb%d" % sl,),
                           dep=[o2], tag="u3")
                m1 = em.op("tensor",
                           lambda e, g=g, sl=sl, first=(g == 0), wu=wu:
                           mm(e, psB[0:DST, 0:512], wu[:, g * DST:(g + 1) * DST],
                              scrb[:, sl * 512:sl * 512 + 512], first, False),
                           r=("scrb%d" % sl, wu_res) + (bres("psB", 0, 512) if g == 0 else ()),
                           w=bres("psB", 0, 512) if g == 0 else (),
                           dep=[o3, wu_ld] + prev_mm, tag="umm")
                prev_mm = [m1]
                m2 = em.op("tensor",
                           lambda e, g=g, sl=sl, wu=wu:
                           mm(e, psB[0:DST, 0:512],
                              wu[:, G * DST + g * DST:G * DST + (g + 1) * DST],
                              scrb[:, sl * 512:sl * 512 + 512], False, g == G - 1),
                           r=("scrb%d" % sl, wu_res),
                           dep=[o3, wu_ld] + prev_mm, tag="umm2")
                prev_mm = [m2]
            uo = em.op("scalar",
                       lambda e, a=a: e.activation(uh[0:DST, a:a + 512],
                                                   psB[0:DST, 0:512],
                                                   AF.Identity,
                                                   bias=dconst[0:DST, 0:1]),
                       r=bres("psB", 0, 512) + ("dconst",), w=("u",),
                       dep=prev_mm, tag="uev%d" % p)
            scan["udeps"][p] = uo

        if c == 0:
            dump("u", uh, ("u",), dep=[x for x in scan["udeps"] if x])

        for l in range(2):
            lin_s = xns if l == 0 else x2s
            lres = (lambda g: "xa%d" % g) if l == 0 else (lambda g, _r=X2_RES: _r)

            def lrtup(g):
                r = lres(g)
                return (r,) if isinstance(r, str) else tuple(r)

            def qkv_m(m, pump=False):
                wt, wld, wres = wload(("qkv", l, m))
                ps, pn = (psA, "psA") if m % 2 == 0 else (psB, "psB")
                prev = []
                for k in range(G):
                    if pump and k in (3, 6):
                        ilq_pump(1)
                    for cc2 in range(NP):
                        scan_pump(1)
                        prev.append(em.op(
                            "tensor",
                            lambda e, k=k, cc2=cc2, ps=ps, wt=wt, lin_s=lin_s:
                            mm(e, ps[:, cc2 * 512:cc2 * 512 + 512],
                               wt[:, k * 128:(k + 1) * 128],
                               lin_s(k, cc2 * 512, cc2 * 512 + 512),
                               k == 0, k == G - 1),
                            r=lrtup(k) + (wres,) + (bres(pn, 0, 1024) if (k == 0 and cc2 == 0) else ()),
                            w=bres(pn, 0, 1024) if (k == 0 and cc2 == 0) else (),
                            dep=[wld] + prev[-1:], tag="qkv"))
                q = m % 4
                em.op("scalar",
                      lambda e, q=q, ps=ps: e.copy(qk[:, q * T:q * T + T],
                                                   ps[:, 0:T]),
                      r=bres(pn, 0, 1024), w=("qk%d" % q,), dep=prev[-1:],
                      tag="qkev")

            def rope_swap(q):
                for (d0, d1, s0, s1) in ((0, 32, 32, 64), (32, 64, 0, 32),
                                         (64, 96, 96, 128), (96, 128, 64, 96)):
                    em.op("sync",
                          lambda e, q=q, d0=d0, d1=d1, s0=s0, s1=s1:
                          e.dma_start(scrb[d0:d1, 0:T],
                                      qk[s0:s1, q * T:q * T + T]),
                          r=("qk%d" % q,), w=("scrb0", "scrb1"), dma=True,
                          tag="swdma")

            def rope_pre(is_k, q, p):
                cS = CP_QS[(l, 2 if is_k else 0)]
                sgn_col = 3 + 2 * l + (1 if is_k else 0)
                tabbase = (2 * S if is_k else 0)
                a = p * 512
                st = {}
                st["tl"] = em.op("sync",
                           lambda e, a=a, tabbase=tabbase, t0=t0:
                           e.dma_start(
                               tbuf.rearrange("p (u t) -> p u t", u=2),
                               tabs.rearrange("p (u s) -> p u s", u=4)
                               [:, (tabbase // S):(tabbase // S) + 2,
                                t0 + a:t0 + a + 512]),
                           w=("tbuf",), dma=True, tag="tab")
                o_sq = em.op("gpsimd",
                             lambda e, q=q, a=a:
                             e.tensor_mul(sqt[:, 0:512],
                                          qk[:, q * T + a:q * T + a + 512],
                                          qk[:, q * T + a:q * T + a + 512]),
                             r=("qk%d" % q,), w=("sqtA",), tag="qsq")
                o_ss = em.op("tensor",
                             lambda e: mm(e, psX[R64:R64 + 2, 0:512],
                                          blk2, sqt[:, 0:512], True, True),
                             r=("sqtA", "blk2") + bres("psX", 0, 512),
                             w=bres("psX", 0, 512), dep=[o_sq], tag="ssq")
                o_nrm = em.op("scalar",
                              lambda e: e.activation(shs[R64:R64 + 2, 0:512],
                                                     psX[R64:R64 + 2, 0:512],
                                                     AF.Sqrt),
                              r=bres("psX", 0, 512), w=("shsm",),
                              dep=[o_ss], tag="nsqrt")
                o_mx = em.op("vector",
                             lambda e: e.tensor_scalar_max(
                                 shs[R64:R64 + 2, 0:512],
                                 shs[R64:R64 + 2, 0:512], 1e-12),
                             r=("shsm",), w=("shsm",), dep=[o_nrm])
                st["rc"] = em.op("vector",
                              lambda e: e.reciprocal(shs[R64:R64 + 2, 0:512],
                                                     shs[R64:R64 + 2, 0:512]),
                              r=("shsm",), w=("shsm",), dep=[o_mx], tag="nrc")
                st["args"] = (is_k, q, p, cS, sgn_col)
                return st

            def rope_post(st):
                is_k, q, p, cS, sgn_col = st["args"]
                a = p * 512
                tl = st["tl"]
                o_fbc = em.op("tensor",
                              lambda e: mm(e, psX[0:128, 0:512],
                                           cpat[R64:R64 + 2, CP_SEL:CP_SEL + 128],
                                           shs[R64:R64 + 2, 0:512],
                                           True, True),
                              r=("shsm", "cpat") + bres("psX", 0, 512),
                              w=bres("psX", 0, 512), dep=[st["rc"]], tag="fbc")
                o_t1 = em.op("vector",
                             lambda e, q=q, a=a, cS=cS:
                             e.scalar_tensor_tensor(
                                 scr[:, 0:512],
                                 qk[:, q * T + a:q * T + a + 512],
                                 cpc(cS), tbuf[:, 0:512],
                                 ALU.mult, ALU.mult),
                             r=("qk%d" % q, "tbuf", "cpat"),
                             w=("scr0",), dep=[tl], tag="t1")
                o_t2 = em.op("vector",
                             lambda e, a=a, sgn_col=sgn_col:
                             e.scalar_tensor_tensor(
                                 scr[:, 512:1024],
                                 scrb[:, a:a + 512],
                                 dconst[:, sgn_col:sgn_col + 1],
                                 tbuf[:, 512:1024],
                                 ALU.mult, ALU.mult),
                             r=("scrb0", "scrb1", "dconst", "tbuf"),
                             w=("scr1",), dep=[tl], tag="t2")
                o_pre = em.op("vector",
                              lambda e: e.tensor_add(scr[:, 0:512],
                                                     scr[:, 0:512],
                                                     scr[:, 512:1024]),
                              r=("scr0", "scr1"), w=("scr0",),
                              dep=[o_t1, o_t2], tag="pre")
                if is_k:
                    rc0 = (t0 + a) % KRING
                    dst = kpr[l][:, q * KRING + rc0:q * KRING + rc0 + 512]
                    wres_k = tuple(
                        "kp%d_%d" % (l, ((t0 + a) // 128 + i) % NSLOT)
                        for i in range(4))
                    em.op("vector",
                          lambda e, dst=dst: e.tensor_mul(
                              dst, scr[:, 0:512], psX[0:128, 0:512]),
                          r=("scr0",) + bres("psX", 0, 512),
                          w=wres_k, dep=[o_pre, o_fbc], tag="kout")
                else:
                    em.op("vector",
                          lambda e, q=q, a=a: e.tensor_mul(
                              qk[:, q * T + a:q * T + a + 512],
                              scr[:, 0:512], psX[0:128, 0:512]),
                          r=("scr0",) + bres("psX", 0, 512),
                          w=("qk%d" % q,), dep=[o_pre, o_fbc], tag="qout")

            ilq = []

            def ilq_pump(n=1):
                for _ in range(n):
                    if ilq:
                        ilq.pop(0)()

            def ilq_drain():
                while ilq:
                    ilq.pop(0)()

            def enqueue_rope(is_k, q):
                def pre0(q=q, is_k=is_k):
                    rope_swap(q)
                    enqueue_rope.state[(is_k, q, 0)] = rope_pre(is_k, q, 0)
                def post0(q=q, is_k=is_k):
                    rope_post(enqueue_rope.state.pop((is_k, q, 0)))
                def pre1(q=q, is_k=is_k):
                    enqueue_rope.state[(is_k, q, 1)] = rope_pre(is_k, q, 1)
                def post1(q=q, is_k=is_k):
                    rope_post(enqueue_rope.state.pop((is_k, q, 1)))
                ilq.extend([pre0, post0, pre1, post1])
            enqueue_rope.state = {}

            def v_tb(tb, wv):
                (wtvA, wldvA, wresvA), (wtvB, wldvB, wresvB) = wv
                slot = (t0 // 128 + tb) % NSLOT
                ps, pn = (psA, "psA") if tb % 2 == 0 else (psB, "psB")
                prev = []
                for k in range(G):
                    scan_pump(1)
                    wtv = wtvA if k < 4 else wtvB
                    wresv = wresvA if k < 4 else wresvB
                    wldv = wldvA if k < 4 else wldvB
                    prev.append(em.op(
                        "tensor",
                        lambda e, k=k, tb=tb, ps=ps, wtv=wtv, lin_s=lin_s:
                        mm(e, ps[:, 0:512],
                           lin_s(k, tb * 128, tb * 128 + 128),
                           wtv[:, (k % 4) * 512:(k % 4) * 512 + 512],
                           k == 0, k == G - 1),
                        r=lrtup(k) + (wresv,) + (bres(pn, 0, 512) if k == 0 else ()),
                        w=bres(pn, 0, 512) if k == 0 else (),
                        dep=[wldv] + prev[-1:], tag="vmm"))
                em.op("vector",
                      lambda e, slot=slot, ps=ps, l=l:
                      e.tensor_copy(
                          bvr[l][:, slot * HEADS * 65:(slot + 1) * HEADS * 65]
                          .rearrange("p (h z) -> p h z", z=65)[:, :, 0:64],
                          ps[:, 0:512].rearrange("p (h z) -> p h z", z=64)),
                      r=bres(pn, 0, 512), w=("bv%d_%d" % (l, slot),),
                      dep=prev[-1:], tag="vev")

            for m in range(4, 8):
                qkv_m(m)
            for q in range(QG):
                enqueue_rope(True, q)
            wv = (wload(("v", l, 0)), wload(("v", l, 1)))
            for tb in range(T // 128):
                v_tb(tb, wv)
                ilq_pump(1)
            for m in range(0, 4):
                qkv_m(m, pump=True)
                if m >= 1:
                    enqueue_rope(False, m - 1)
                ilq_pump(1)
            enqueue_rope(False, 3)

            Jlo = max(0, t0 // 128 - 4)
            Jhi = (t0 + T) // 128 - 1
            pend_drain = []

            def attn_head(h):
                hp = h % 2
                q = h // 2
                av_last = []
                av_virgin = {4: True, 5: True}
                pend_av = []
                for J in range(Jlo, Jhi + 1):
                    jp = J % 2
                    simps, simn = (psA, "psA") if jp == 0 else (psB, "psB")
                    r = J % 4
                    sn = 512 - 128 * r
                    a = max(128 * J, t0) - 128 * J
                    b = min(128 * J + 640, t0 + T) - 128 * J
                    slot = J % NSLOT
                    rc0 = (128 * J) % KRING
                    kres = "kp%d_%d" % (l, slot)
                    need_next = b > sn
                    if need_next:
                        o_knx = em.op("vector",
                                      lambda e, q=q, rc0=rc0, jp=jp, l=l:
                                      e.tensor_scalar_mul(
                                          knx[:, jp * 128:jp * 128 + 128],
                                          kpr[l][:, q * KRING + rc0:q * KRING + rc0 + 128],
                                          cpc(CP_BSQ)),
                                      r=(kres, "cpat"), w=("knx%d" % jp,),
                                      tag="knx")
                    cuts = sorted({a, b, min(max(a, sn), b), min(max(a, 512), b)})
                    sim_ops = []
                    for ci in range(len(cuts) - 1):
                        p0, p1 = cuts[ci], cuts[ci + 1]
                        if p1 <= p0:
                            continue
                        use_next = p0 >= sn
                        scan_pump(1)
                        if use_next:
                            lhs = knx[64 * hp:64 * hp + 64, jp * 128:jp * 128 + 128]
                            lr = ("knx%d" % jp,)
                            ldep = [o_knx]
                        else:
                            lhs = kpr[l][64 * hp:64 * hp + 64,
                                         q * KRING + rc0:q * KRING + rc0 + 128]
                            lr = (kres,)
                            ldep = []
                        sim_ops.append(em.op(
                            "tensor",
                            lambda e, p0=p0, p1=p1, lhs=lhs, simps=simps, J=J, q=q,
                            h=h, t0=t0:
                            mm(e, simps[:, p0:p1], lhs,
                               qk[64 * (h % 2):64 * (h % 2) + 64,
                                  q * T + 128 * J + p0 - t0:q * T + 128 * J + p1 - t0],
                               True, True),
                            r=lr + ("qk%d" % q,) + bres(simn, p0, p1),
                            w=bres(simn, p0, p1), dep=ldep, tag="sim"))
                    o_exp = em.op("scalar",
                                  lambda e, a=a, b=b, jp=jp, simps=simps:
                                  e.activation(expS[:, jp * 640 + a:jp * 640 + b],
                                               simps[:, a:b], AF.Exp,
                                               scale=QKSCALE),
                                  r=bres(simn, a, b), w=("ex%d" % jp,),
                                  dep=sim_ops, tag="exp")
                    mdeps = [o_exp]
                    mask_ranges = []
                    u0, u1 = a, min(b, 128)
                    if u1 > u0:
                        mask_ranges.append((u0, u1))
                        mdeps.append(em.op(
                            "gpsimd",
                            lambda e, u0=u0, u1=u1, jp=jp:
                            e.tensor_mul(expS[:, jp * 640 + u0:jp * 640 + u1],
                                         expS[:, jp * 640 + u0:jp * 640 + u1],
                                         cmisc[:, 128 + u0:128 + u1]),
                            r=("ex%d" % jp, "cmisc"), w=("ex%d" % jp,),
                            dep=[o_exp], tag="mU"))
                    l0, l1 = max(a, 512), b
                    if l1 > l0:
                        mask_ranges.append((l0, l1))
                        mdeps.append(em.op(
                            "gpsimd",
                            lambda e, l0=l0, l1=l1, jp=jp:
                            e.tensor_mul(expS[:, jp * 640 + l0:jp * 640 + l1],
                                         expS[:, jp * 640 + l0:jp * 640 + l1],
                                         cmisc[:, 256 + l0 - 512:256 + l1 - 512]),
                            r=("ex%d" % jp, "cmisc"), w=("ex%d" % jp,),
                            dep=[o_exp], tag="mL"))
                    if dbg and c == 0 and l == 0 and h == 0 and J in (0, 1):
                        dump("expS%d" % J, expS, ("ex%d" % jp,), )

                    ilo = 128 * J + a
                    ihi = 128 * J + b
                    newlo = 128 * J + 512
                    avcuts = sorted({ilo, ihi, min(max(ilo, newlo), ihi),
                                     min(max(ilo, t0 + 512), ihi)})
                    av_emits = []
                    for ci in range(len(avcuts) - 1):
                        i0, i1 = avcuts[ci], avcuts[ci + 1]
                        if i1 <= i0:
                            continue
                        bank = 4 + (i0 - t0) // 512
                        st = av_virgin[bank]
                        av_virgin[bank] = False
                        scan_pump(1)
                        c0m, c1m = i0 - 128 * J, i1 - 128 * J
                        if any(c1m > m0 and c0m < m1 for m0, m1 in mask_ranges):
                            pdeps = list(mdeps)
                        else:
                            pdeps = [o_exp]
                        av_emits.append((i0, i1, st, jp, J, slot, pdeps))
                    if J == Jlo + 1 and pend_drain:
                        pend_drain.pop(0)()
                    if (J - Jlo) % 3 == 2:
                        ilq_pump(1)
                    pend_av.append(av_emits)
                    if len(pend_av) > globals().get("AVLAG", 1):
                        for i0, i1, st, jp, Jx, slot, md in pend_av.pop(0):
                            av_last.append(em.op(
                                "tensor",
                                lambda e, i0=i0, i1=i1, st=st, jp=jp, Jx=Jx,
                                slot=slot, l=l, h=h, t0=t0:
                                mm(e, psR[0:65, i0 - t0:i1 - t0],
                                   bvr[l][:, slot * HEADS * 65 + h * 65:
                                          slot * HEADS * 65 + h * 65 + 65],
                                   expS[:, jp * 640 + i0 - 128 * Jx:jp * 640 + i1 - 128 * Jx],
                                   st, True),
                                r=("bv%d_%d" % (l, slot), "ex%d" % jp)
                                + bres("psR", i0 - t0, i1 - t0),
                                w=bres("psR", i0 - t0, i1 - t0),
                                dep=md, tag="av"))
                for i0, i1, st, jp, Jx, slot, md in (pend_av.pop(0) if pend_av else []):
                    av_last.append(em.op(
                        "tensor",
                        lambda e, i0=i0, i1=i1, st=st, jp=jp, Jx=Jx,
                        slot=slot, l=l, h=h, t0=t0:
                        mm(e, psR[0:65, i0 - t0:i1 - t0],
                           bvr[l][:, slot * HEADS * 65 + h * 65:
                                  slot * HEADS * 65 + h * 65 + 65],
                           expS[:, jp * 640 + i0 - 128 * Jx:jp * 640 + i1 - 128 * Jx],
                           st, True),
                        r=("bv%d_%d" % (l, slot), "ex%d" % jp)
                        + bres("psR", i0 - t0, i1 - t0),
                        w=bres("psR", i0 - t0, i1 - t0),
                        dep=md, tag="av"))

                def emit_drain(h=h, av_last=av_last):
                    o_rc = em.op("vector",
                                 lambda e: e.reciprocal(recbf[R64:R64 + 1, 0:T],
                                                        psR[R64:R64 + 1, 0:T]),
                                 r=bres("psR", 0, T), w=("rec",), dep=av_last,
                                 tag="drec")
                    for p in range(NP):
                        a = p * 512
                        o_bc = em.op("tensor",
                                     lambda e, a=a: mm(e, psX[0:64, 0:512],
                                                       ones_b[R64:R64 + 1, 0:64],
                                                       recbf[R64:R64 + 1, a:a + 512],
                                                       True, True),
                                     r=("rec", "onesb") + bres("psX", 0, 512),
                                     w=bres("psX", 0, 512), dep=[o_rc], tag="dbc")
                        o_cf = em.op("scalar",
                                     lambda e, a=a: e.copy(shs[0:64, a:a + 512],
                                                           psX[0:64, 0:512]),
                                     r=bres("psX", 0, 512), w=("fbc%d" % p,),
                                     dep=[o_bc], tag="dcf")
                        em.op("vector",
                              lambda e, a=a, h=h: e.tensor_mul(
                                  obf[0:64, h * T + a:h * T + a + 512],
                                  psR[0:64, a:a + 512], shs[0:64, a:a + 512]),
                              r=bres("psR", a, a + 512) + ("fbc%d" % p,),
                              w=("ob%d" % h,), dep=[o_cf] + av_last, tag="dmul")
                pend_drain.append(emit_drain)

            for q in range(QG):
                while len(ilq) > 4 * (QG - 1 - q):
                    ilq_pump(1)
                attn_head(2 * q)
                attn_head(2 * q + 1)

            if c == 0:
                dump("kpr%d" % l, kpr[l],
                     tuple("kp%d_%d" % (l, s) for s in range(NSLOT)))
                dump("qp%d" % l, qk, tuple("qk%d" % i for i in range(4)))
                dump("bvr%d" % l, bvr[l],
                     tuple("bv%d_%d" % (l, s) for s in range(NSLOT)))
                dump("obf%d" % l, obf, tuple("ob%d" % h for h in range(HEADS)))

            while pend_drain:
                pend_drain.pop(0)()

            for m in range(8):
                wt, wld, wres = wload(("wo", l, m))
                for cc2 in range(NP):
                    ps, pn = (psA, "psA") if (m * NP + cc2) % 2 == 0 else (psB, "psB")
                    prev = []
                    for h in range(HEADS):
                        scan_pump(1)
                        prev.append(em.op(
                            "tensor",
                            lambda e, h=h, cc2=cc2, ps=ps, wt=wt:
                            mm(e, ps[:, 0:512], wt[0:64, h * 128:(h + 1) * 128],
                               obf[0:64, h * T + cc2 * 512:h * T + cc2 * 512 + 512],
                               h == 0, h == HEADS - 1),
                            r=("ob%d" % h, wres) + (bres(pn, 0, 512) if h == 0 else ()),
                            w=bres(pn, 0, 512) if h == 0 else (),
                            dep=[wld] + prev[-1:], tag="womm"))
                    em.op("vector",
                          lambda e, m=m, cc2=cc2, ps=ps:
                          e.tensor_add(xfd(m, cc2 * 512, cc2 * 512 + 512),
                                       ps[:, 0:512],
                                       xfd(m, cc2 * 512, cc2 * 512 + 512)),
                          r=bres(pn, 0, 512) + ("xf%d" % m,), w=("xf%d" % m,),
                          dep=prev[-1:], tag="woev")

            if c == 0:
                dump("xwo%d" % l, xffn, tuple("xf%d" % g for g in range(G)))

            hres = "hB" if l == 0 else "hD"
            hcol = 0 if l == 0 else 16
            o_hr = em.op("vector",
                         lambda e, hcol=hcol: e.tensor_copy(
                             xffn.rearrange("p (g t) -> p g t", g=G)[:, :, 0:1],
                             halos[:, hcol:hcol + 8].rearrange("p (g o) -> p g o", o=1)),
                         r=(hres,), w=tuple("xf%d" % g for g in range(G)),
                         tag="hrest")
            o_hs = em.op("vector",
                         lambda e, hcol=hcol: e.tensor_copy(
                             halos[:, hcol:hcol + 8].rearrange("p (g o) -> p g o", o=1),
                             xffn.rearrange("p (g t) -> p g t", g=G)[:, :, T:T + 1]),
                         r=tuple("xf%d" % g for g in range(G)), w=(hres,),
                         tag="hsave")

            for p in range(NP):
                a = p * 512
                prev = []
                for g in range(G):
                    base = g * (T + 1)
                    o_c = em.op("scalar",
                                lambda e, base=base, a=a:
                                e.copy(sqt[:, 0:513],
                                       xffn[:, base + a:base + a + 513]),
                                r=("xf%d" % g,), w=("sqtA",), dep=[o_hr],
                                tag="l2c")
                    o_s = em.op("scalar",
                                lambda e, base=base, a=a:
                                e.activation(sqt[:, 513:1026],
                                             xffn[:, base + a:base + a + 513],
                                             AF.Square),
                                r=("xf%d" % g,), w=("sqtB",), dep=[o_hr],
                                tag="l2s")
                    off = 0 if g >= 4 else 1
                    o1 = em.op("tensor",
                               lambda e, off=off, first=(g == 0), g=g:
                               mm(e, psA[R64:R64 + 1, 0:512], ones_b[:, 0:1],
                                  sqt[:, off:off + 512], first, g == G - 1),
                               r=("sqtA", "onesb") + (bres("psA", 0, 512) if g == 0 else ()),
                               w=bres("psA", 0, 512) if g == 0 else (),
                               dep=[o_c] + prev, tag="l2m")
                    prev = [o1]
                    o2 = em.op("tensor",
                               lambda e, off=off, first=(g == 0), g=g:
                               mm(e, psA[R64:R64 + 1, 512:1024], ones_b[:, 0:1],
                                  sqt[:, 513 + off:513 + off + 512], first,
                                  g == G - 1),
                               r=("sqtB", "onesb") + (bres("psA", 512, 1024) if g == 0 else ()),
                               w=bres("psA", 512, 1024) if g == 0 else (),
                               dep=[o_s] + prev, tag="l2ms")
                    prev = [o2]
                o_a, o_b, _ = ln_derive(prev, (shs, 1536, "l2a"),
                                        (shs, 2048, "l2b"), "l2d_%d_" % p)
                bc_a = bcast64("psX", 0, shs, 1536, "l2a", [o_a], tag="l2bca")
                bc_b = bcast64("psR", 0, shs, 2048, "l2b", [o_b], tag="l2bcb")
                for g in range(G):
                    sl = g % 2
                    o1 = em.op("vector",
                               lambda e, g=g, a=a, sl=sl:
                               e.tensor_mul(scr[:, sl * 512:sl * 512 + 512],
                                            xfs(g, a, a + 512), psX[0:128, 0:512]),
                               r=("xf%d" % g,) + bres("psX", 0, 512),
                               w=("scr%d" % sl,), dep=[bc_a], tag="xh1")
                    o2 = em.op("vector",
                               lambda e, g=g, sl=sl:
                               e.tensor_add(scr[:, sl * 512:sl * 512 + 512],
                                            scr[:, sl * 512:sl * 512 + 512],
                                            psR[0:128, 0:512]),
                               r=("scr%d" % sl,) + bres("psR", 0, 512),
                               w=("scr%d" % sl,), dep=[bc_b, o1], tag="xh2")
                    em.op("scalar",
                          lambda e, g=g, a=a, sl=sl, l=l:
                          e.activation(xnd(g, a, a + 512),
                                       scr[:, sl * 512:sl * 512 + 512],
                                       AF.Identity, bias=cpc(CP_LB + 8 * l + g),
                                       scale=cpc(CP_LG + 8 * l + g)),
                          r=("scr%d" % sl, "cpat"), w=("xa%d" % g,),
                          dep=[o2], tag="xh3")

            if c == 0:
                dump("xh%d" % l, xnar, tuple("xa%d" % g for g in range(G)))

            for cc2 in range(NP):
                a = cc2 * 512
                for m in range(NW1):
                    w = W1M[m]
                    wt, wld, wres = wload(("w1", l, m))
                    prevA, prevB = [], []
                    for k in range(G):
                        scan_pump(1)
                        prevA.append(em.op(
                            "tensor",
                            lambda e, k=k, w=w, a=a, wt=wt:
                            mm(e, psA[0:w, 0:512], wt[:, k * w:(k + 1) * w],
                               xnd(k, a, a + 512), k == 0, k == G - 1),
                            r=("xa%d" % k, wres) + (bres("psA", 0, 512) if k == 0 else ()),
                            w=bres("psA", 0, 512) if k == 0 else (),
                            dep=[wld] + prevA[-1:], tag="w1a"))
                        scan_pump(1)
                        prevB.append(em.op(
                            "tensor",
                            lambda e, k=k, w=w, a=a, wt=wt:
                            mm(e, psB[0:w, 0:512],
                               wt[:, (G + k) * w:(G + k + 1) * w],
                               xnd(k, a, a + 512), k == 0, k == G - 1),
                            r=("xa%d" % k, wres) + (bres("psB", 0, 512) if k == 0 else ()),
                            w=bres("psB", 0, 512) if k == 0 else (),
                            dep=[wld] + prevB[-1:], tag="w1g"))
                    o_g = em.op("scalar",
                                lambda e, w=w, m=m, l=l:
                                e.activation(scr[:, 1024:1024 + 512][0:w, :],
                                             psB[0:w, 0:512], AF.Gelu,
                                             bias=cpc(CP_B1G + NW1 * l + m, w)),
                                r=bres("psB", 0, 512) + ("cpat",), w=("scr2",),
                                dep=prevB[-1:], tag="gelu")
                    em.op("vector",
                          lambda e, w=w, m=m, l=l:
                          e.scalar_tensor_tensor(
                              h2p[0:w, m * 512:m * 512 + 512],
                              psA[0:w, 0:512], cpc(CP_B1 + NW1 * l + m, w),
                              scr[0:w, 1024:1024 + 512], ALU.add, ALU.mult),
                          r=bres("psA", 0, 512) + ("scr2", "cpat"),
                          w=H2_RES + ("h2t%d" % m,), dep=[o_g] + prevA[-1:],
                          tag="h2")
                for mo in range(8):
                    wt, wld, wres = wload(("w2", l, mo))
                    prev = []
                    for k in range(NW1):
                        kk = W1M[k]
                        scan_pump(1)
                        prev.append(em.op(
                            "tensor",
                            lambda e, k=k, kk=kk, wt=wt:
                            mm(e, psR[0:128, 0:512], wt[0:kk, k * 128:(k + 1) * 128],
                               h2p[0:kk, k * 512:k * 512 + 512],
                               k == 0, k == NW1 - 1),
                            r=("h2t%d" % k, wres) + (bres("psR", 0, 512) if k == 0 else ()),
                            w=bres("psR", 0, 512) if k == 0 else (),
                            dep=[wld] + prev[-1:], tag="w2mm"))
                    em.op("vector",
                          lambda e, mo=mo, a=a, l=l:
                          e.scalar_tensor_tensor(
                              xfd(mo, a, a + 512), psR[0:128, 0:512],
                              cpc(CP_B2 + 8 * l + mo),
                              xfd(mo, a, a + 512), ALU.add, ALU.add),
                          r=bres("psR", 0, 512) + ("cpat", "xf%d" % mo),
                          w=("xf%d" % mo,), dep=prev[-1:], tag="w2ev")

            if l == 0:
                o_hr2 = em.op("vector",
                              lambda e: e.tensor_copy(
                                  xffn.rearrange("p (g t) -> p g t", g=G)[:, :, 0:1],
                                  halos[:, 8:16].rearrange("p (g o) -> p g o", o=1)),
                              r=("hC",), w=tuple("xf%d" % g for g in range(G)),
                              tag="hrC")
                em.op("vector",
                      lambda e: e.tensor_copy(
                          halos[:, 8:16].rearrange("p (g o) -> p g o", o=1),
                          xffn.rearrange("p (g t) -> p g t", g=G)[:, :, T:T + 1]),
                      r=tuple("xf%d" % g for g in range(G)), w=("hC",),
                      tag="hsC")
                for g in range(G):
                    em.op("scalar",
                          lambda e, g=g: e.copy(
                              x2[:, g * (T + 1):(g + 1) * (T + 1)],
                              xffn[:, g * (T + 1):(g + 1) * (T + 1)]),
                          r=("xf%d" % g,), w=X2_RES, dep=[o_hr2], tag="x2c")

        if c == 0:
            dump("xw2", xffn, tuple("xf%d" % g for g in range(G)))
            dump("H", Hbuf, ("H",))

        scan_pump(T)
        for g in range(G):
            em.op("scalar",
                  lambda e, g=g: e.activation(xfd(g, 0, T), xfd(g, 0, T),
                                              AF.Silu),
                  r=("xf%d" % g,), w=("xf%d" % g,), tag="silu")
        h2rd = [em.res.get("h2t%d" % m, [None, []]) for m in range(NW1)]
        h2deps = [o for e2 in h2rd for o in ([e2[0]] if e2[0] else []) + e2[1]]
        xin2 = em.op("sync",
                     lambda e, t0=t0: e.dma_start(
                         arena[:, 0:G * T].rearrange("p (g t) -> p g t", g=G),
                         xT[:, t0:t0 + T].rearrange("(g p) t -> p g t", p=128)),
                     w=AR_ALL, dma=True, dep=h2deps, tag="xre%d" % c)
        for p in range(NP):
            a = p * 512
            bcA = bcast64("psX", 0, strips, 2 * T + a, "stA2", [], tag="cbcA")
            bcB = bcast64("psR", 512, strips, 3 * T + a, "stB2", [], tag="cbcB")
            for g in range(G):
                sl = g % 2
                o1 = em.op("vector",
                           lambda e, g=g, a=a, sl=sl:
                           e.tensor_mul(scr[:, sl * 512:sl * 512 + 512],
                                        xg(g, a, a + 512), psX[0:128, 0:512]),
                           r=("arQ%d" % (g // 2),) + bres("psX", 0, 512),
                           w=("scr%d" % sl,), dep=[bcA, xin2], tag="c1")
                o2 = em.op("vector",
                           lambda e, g=g, sl=sl:
                           e.tensor_add(scr[:, sl * 512:sl * 512 + 512],
                                        scr[:, sl * 512:sl * 512 + 512],
                                        psR[0:128, 512:1024]),
                           r=("scr%d" % sl,) + bres("psR", 512, 1024),
                           w=("scr%d" % sl,), dep=[bcB, o1], tag="c2")
                em.op("scalar",
                      lambda e, g=g, a=a, sl=sl:
                      e.activation(xnd(g, a, a + 512),
                                   scr[:, sl * 512:sl * 512 + 512],
                                   AF.Identity, bias=cpc(CP_LNB + g),
                                   scale=cpc(CP_LNW + g)),
                      r=("scr%d" % sl, "cpat"), w=("xa%d" % g,), dep=[o2],
                      tag="c3")
        o_hb = em.op("scalar", lambda e: e.copy(
                         hbf, Hbuf.rearrange("p (t u) -> p t u", u=2)[0:DST, 1:T + 1, 0:1]),
                     r=("H",), w=("hbf",), tag="hcast")
        for g in range(G):
            wt, wld, wres = wload(("wg", g))
            womt, womld, womres = wload(("wom",))
            for p in range(NP):
                a = p * 512
                prev = []
                for k in range(G):
                    scan_pump(1)
                    prev.append(em.op(
                        "tensor",
                        lambda e, k=k, a=a, wt=wt: mm(e, psA[0:128, 0:512],
                                               wt[:, k * 128:(k + 1) * 128],
                                               xnd(k, a, a + 512),
                                               k == 0, k == G - 1),
                        r=("xa%d" % k, wres) + (bres("psA", 0, 512) if k == 0 else ()),
                        w=bres("psA", 0, 512) if k == 0 else (),
                        dep=[wld] + prev[-1:], tag="gmm"))
                o_om = em.op("tensor",
                             lambda e, g=g, a=a, womt=womt:
                             mm(e, psB[0:128, 0:512],
                                womt[0:DST, g * 128:(g + 1) * 128],
                                hbf[:, a:a + 512], True, True),
                             r=("hbf", womres) + bres("psB", 0, 512),
                             w=bres("psB", 0, 512), dep=[womld, o_hb],
                             tag="ommm")
                o_sg = em.op("scalar",
                             lambda e, g=g: e.activation(scr[:, 0:512],
                                                         psA[0:128, 0:512],
                                                         AF.Sigmoid,
                                                         bias=cpc(CP_BG + g)),
                             r=bres("psA", 0, 512) + ("cpat",), w=("scr0",),
                             dep=prev[-1:], tag="sig")
                o_s = em.op("vector",
                            lambda e, g=g: e.scalar_tensor_tensor(
                                scr[:, 512:1024], psB[0:128, 0:512],
                                cpc(CP_BO + g), scr[:, 0:512],
                                ALU.add, ALU.mult),
                            r=bres("psB", 0, 512) + ("cpat", "scr0"),
                            w=("scr1",), dep=[o_om, o_sg], tag="sg")
                bcA2 = bcast64("psX", 0, strips, a, "stA", [], tag="fbca")
                bcB2 = bcast64("psR", 512, strips, T + a, "stB", [], tag="fbcb")
                o_t = em.op("vector",
                            lambda e, g=g, a=a: e.tensor_mul(
                                scr[:, 1024:1536], xg(g, a, a + 512),
                                psX[0:128, 0:512]),
                            r=("arQ%d" % (g // 2),) + bres("psX", 0, 512),
                            w=("scr2",), dep=[bcA2, xin2], tag="ft")
                o_v = em.op("vector",
                            lambda e: e.tensor_add(scr[:, 1024:1536],
                                                   scr[:, 1024:1536],
                                                   psR[0:128, 512:1024]),
                            r=("scr2",) + bres("psR", 512, 1024),
                            w=("scr2",), dep=[bcB2, o_t], tag="fv")
                o_w = em.op("vector",
                            lambda e: e.tensor_add(scr[:, 1024:1536],
                                                   scr[:, 1024:1536],
                                                   scr[:, 512:1024]),
                            r=("scr2", "scr1"), w=("scr2",), dep=[o_v, o_s],
                            tag="fw")
                o_z = em.op("vector",
                            lambda e, g=g, a=a: e.tensor_mul(
                                scr[:, 1024:1536], scr[:, 1024:1536],
                                xfd(g, a, a + 512)),
                            r=("scr2", "xf%d" % g), w=("scr2",), dep=[o_w],
                            tag="fz")
                o_f = em.op("vector",
                            lambda e, g=g, a=a: e.tensor_add(
                                xfd(g, a, a + 512), scr[:, 1024:1536],
                                xg(g, a, a + 512)),
                            r=("scr2", "arQ%d" % (g // 2)),
                            w=("xf%d" % g,), dep=[o_z, xin2], tag="ff")
                em.op("gpsimd",
                      lambda e, g=g, a=a, t0=t0: e.dma_start(
                          yout[:, g * S + t0 + a:g * S + t0 + a + 512],
                          xfd(g, a, a + 512)),
                      r=("xf%d" % g,), dma=True, force_inc=True,
                      dep=[o_f], tag="yout")

    emit_program(nc, em)
    return nc


def _host_prep(inputs, S):
    f = inputs
    wblob = pack_wblob(f)
    TQc, TQs, TKc, TKs, base64 = rope_tables(S)
    tabs = np.concatenate([TQc, TQs, TKc, TKs], 1)
    cpat = pack_cpat(f, base64)
    cmisc = const_misc(np.asarray(inputs["L0_attn_A"], np.float32).T)
    x = np.asarray(f["L0_attn_x"], np.float32)
    in_maps = []
    for b in range(NCORE):
        in_maps.append({
            "xT": np.ascontiguousarray(x[b].T),
            "wblob": wblob,
            "tabs": tabs,
            "cpin": cpat,
            "cmin": cmisc,
        })
    return in_maps


def run(inputs, S=S_FULL, T=T_CHUNK, verbose=False, dbg=False):
    in_maps = _host_prep(inputs, S)
    nc = build_nc(S, T, verbose=verbose, dbg=dbg)
    res = run_bass_kernel_spmd(nc, in_maps, core_ids=list(range(NCORE)))
    outs = []
    for b in range(NCORE):
        y = res.results[b]["y"]
        y = y.reshape(128, G, S).transpose(1, 0, 2).reshape(D, S)
        outs.append(y.T)
    out = np.stack(outs, 0).astype(np.float32)
    if dbg:
        return out, [{k: v for k, v in res.results[b].items() if k.startswith("dbg_")}
                     for b in range(NCORE)]
    return out


def kernel(**inputs):
    return run(inputs, S=S_FULL, T=T_CHUNK)


# version 06
# speedup vs baseline: 1.0313x; 1.0008x over previous
import numpy as np
import ml_dtypes
import concourse.bass as bass
import concourse.mybir as mybir
from concourse.bass_utils import run_bass_kernel_spmd

F32 = mybir.dt.float32
BF16 = mybir.dt.bfloat16
AF = mybir.ActivationFunctionType
ALU = mybir.AluOpType

D = 1024
G = 8
DST = 16
HEADS = 8
DH = 64
INNER = 512
QG = 4
WIN = 512
HID = 2730
QKSCALE = 8.0
LNEPS = 1e-5
NCORE = 8

S_FULL = 8192
T_CHUNK = 1024

W1M = [128] * 21 + [42]
NW1 = len(W1M)

def build_wplan():
    plan = {}
    col = [0]

    def grp(key, width):
        a = col[0]
        plan[key] = (a, a + width)
        col[0] += width

    grp(("u",), 2 * G * DST)
    for l in range(2):
        for m in range(8):
            grp(("qkv", l, m), G * 128)
        for h4 in range(4):
            grp(("v", l, h4), 2 * 512)
        for m in range(8):
            grp(("wo", l, m), HEADS * 128)
        for m in range(NW1):
            grp(("w1a", l, m), G * W1M[m])
            grp(("w1g", l, m), G * W1M[m])
        for m in range(8):
            grp(("w2", l, m, 0), 11 * 128)
            grp(("w2", l, m, 1), 11 * 128)
    for m in range(8):
        grp(("wg", m), G * 128)
    grp(("wom",), 8 * 128)
    return plan, col[0]


WPLAN, WCOLS = build_wplan()


def pack_wblob(f):
    blob = np.zeros((128, WCOLS), np.float32)

    def put(key, mat):
        a, b = WPLAN[key]
        assert mat.shape[1] == b - a and mat.shape[0] <= 128, (key, mat.shape)
        blob[: mat.shape[0], a:b] = mat

    WsT = np.ascontiguousarray(np.asarray(f["L0_attn_Ws"], np.float32).T)
    WiT = np.ascontiguousarray(np.asarray(f["L0_attn_Wi"], np.float32).T)
    put(("u",), np.concatenate(
        [WsT[128 * g:128 * (g + 1)] for g in range(G)]
        + [WiT[128 * g:128 * (g + 1)] for g in range(G)], axis=1))

    wqkv = np.asarray(f["L0_ffn_wqkv"], np.float32)
    wo = np.asarray(f["L0_ffn_wo"], np.float32)
    w1 = np.asarray(f["L0_ffn_w1"], np.float32)
    w2 = np.asarray(f["L0_ffn_w2"], np.float32)
    for l in range(2):
        for m in range(8):
            put(("qkv", l, m), np.concatenate(
                [wqkv[l][128 * k:128 * (k + 1), 128 * m:128 * (m + 1)]
                 for k in range(G)], 1))
        for h4 in range(4):
            put(("v", l, h4), np.concatenate(
                [wqkv[l][128 * k:128 * (k + 1), 1024:1536]
                 for k in range(2 * h4, 2 * h4 + 2)], 1))
        for m in range(8):
            tiles = []
            for h in range(HEADS):
                t = np.zeros((128, 128), np.float32)
                t[:64] = wo[l][64 * h:64 * (h + 1), 128 * m:128 * (m + 1)]
                tiles.append(t)
            put(("wo", l, m), np.concatenate(tiles, 1))
        for m in range(NW1):
            w = W1M[m]
            put(("w1a", l, m), np.concatenate(
                [w1[l][128 * k:128 * (k + 1), 128 * m:128 * m + w]
                 for k in range(G)], 1))
            put(("w1g", l, m), np.concatenate(
                [w1[l][128 * k:128 * (k + 1), HID + 128 * m:HID + 128 * m + w]
                 for k in range(G)], 1))
        for m in range(8):
            for half in range(2):
                tiles = []
                for k in range(11 * half, 11 * half + 11):
                    kk = min(128, HID - 128 * k)
                    t = np.zeros((128, 128), np.float32)
                    t[:kk] = w2[l][128 * k:128 * k + kk, 128 * m:128 * (m + 1)]
                    tiles.append(t)
                put(("w2", l, m, half), np.concatenate(tiles, 1))
    WgT = np.ascontiguousarray(np.asarray(f["L0_attn_Wg"], np.float32).T)
    for m in range(8):
        put(("wg", m), np.concatenate(
            [WgT[128 * k:128 * (k + 1), 128 * m:128 * (m + 1)]
             for k in range(G)], 1))
    WomT = np.ascontiguousarray(np.asarray(f["L0_attn_Wo"], np.float32).T)
    tiles = []
    for m in range(8):
        t = np.zeros((128, 128), np.float32)
        t[:DST] = WomT[:, 128 * m:128 * (m + 1)]
        tiles.append(t)
    put(("wom",), np.concatenate(tiles, 1))
    return blob


def rope_tables(S):
    half = np.arange(0, DH, 2, dtype=np.float64)
    inv_freq = 1.0 / 10000.0 ** (half / DH)
    t = np.arange(S, dtype=np.float64)
    freqs = np.concatenate([t[:, None] * inv_freq] * 2, 1)
    base = np.concatenate([(half + 0.4 * DH) / (1.4 * DH)] * 2)
    cos, sin = np.cos(freqs), np.sin(freqs)
    pq = base[None, :] ** (((t % WIN) / (WIN // 2))[:, None])
    pk = base[None, :] ** ((-(t % WIN) / (WIN // 2))[:, None])

    def lay(tab):
        return np.concatenate([tab.T, tab.T], 0).astype(np.float32)

    return (lay(cos * pq), lay(sin * pq), lay(cos * pk), lay(sin * pk),
            base.astype(np.float32))


def const_misc(AT=None):
    P = np.zeros((128, 128), np.float32)
    for m in range(128):
        base = (m // 64) * 64
        dm = m % 64
        if dm < 32:
            P[base + dm + 32, m] = -1.0
        else:
            P[base + dm - 32, m] = 1.0
    triU = np.zeros((128, 128), np.float32)
    triL = np.zeros((128, 128), np.float32)
    for p in range(128):
        triU[p, p:] = 1.0
        triL[p, :p + 1] = 1.0
    at = np.zeros((128, 16), np.float32)
    if AT is not None:
        at[:16] = AT
    return np.concatenate([P, triU, triL, at], 1).astype(ml_dtypes.bfloat16)


CP_BSQ = 0
CP_QS = {}
_c = 1
for _l in range(2):
    for _k in range(4):
        CP_QS[(_l, _k)] = _c
        _c += 1
CP_AT = _c; _c += 16
CP_BS = _c; _c += 1
CP_BI = _c; _c += 1
CP_BO = _c; _c += 8
CP_BG = _c; _c += 8
CP_LNW = _c; _c += 8
CP_LNB = _c; _c += 8
CP_LG = _c; _c += 16
CP_LB = _c; _c += 16
CP_B1 = _c; _c += 2 * NW1
CP_B1G = _c; _c += 2 * NW1
CP_B2 = _c; _c += 16
CP_SEL = _c; _c += 128
CP_SIGN = _c; _c += 1
CPC = _c


def pack_cpat(f, base64):
    cp = np.zeros((128, CPC), np.float32)
    cp[:, CP_BSQ] = np.tile(base64 ** 2, 2)
    qs = np.asarray(f["L0_ffn_q_scale"], np.float32)
    ks = np.asarray(f["L0_ffn_k_scale"], np.float32)
    for l in range(2):
        cp[:, CP_QS[(l, 0)]] = np.tile(qs[l], 2)
        cp[:, CP_QS[(l, 1)]] = np.tile(np.concatenate([qs[l][32:], qs[l][:32]]), 2)
        cp[:, CP_QS[(l, 2)]] = np.tile(ks[l], 2)
        cp[:, CP_QS[(l, 3)]] = np.tile(np.concatenate([ks[l][32:], ks[l][:32]]), 2)
    cp[:DST, CP_AT:CP_AT + 16] = np.asarray(f["L0_attn_A"], np.float32).T
    cp[:DST, CP_BS] = np.asarray(f["L0_attn_bs"], np.float32)
    cp[:DST, CP_BI] = np.asarray(f["L0_attn_bi"], np.float32)
    for g in range(G):
        sl = slice(128 * g, 128 * (g + 1))
        cp[:, CP_BO + g] = np.asarray(f["L0_attn_bo"], np.float32)[sl]
        cp[:, CP_BG + g] = np.asarray(f["L0_attn_bg"], np.float32)[sl]
        cp[:, CP_LNW + g] = np.asarray(f["L0_attn_ln_w"], np.float32)[sl]
        cp[:, CP_LNB + g] = np.asarray(f["L0_attn_ln_b"], np.float32)[sl]
    for l in range(2):
        for g in range(G):
            sl = slice(128 * g, 128 * (g + 1))
            cp[:, CP_LG + 8 * l + g] = np.asarray(f["L0_ffn_ln_g"], np.float32)[l][sl]
            cp[:, CP_LB + 8 * l + g] = np.asarray(f["L0_ffn_ln_b"], np.float32)[l][sl]
        b1 = np.asarray(f["L0_ffn_b1"], np.float32)[l]
        for m in range(NW1):
            w = W1M[m]
            cp[:w, CP_B1 + NW1 * l + m] = b1[128 * m:128 * m + w]
            cp[:w, CP_B1G + NW1 * l + m] = b1[HID + 128 * m:HID + 128 * m + w]
        b2 = np.asarray(f["L0_ffn_b2"], np.float32)[l]
        for g in range(G):
            cp[:, CP_B2 + 8 * l + g] = b2[128 * g:128 * (g + 1)]
    cp[64, CP_SEL:CP_SEL + 64] = 1.0
    cp[65, CP_SEL + 64:CP_SEL + 128] = 1.0
    sgn = np.where(np.arange(128) % 64 < 32, -1.0, 1.0)
    cp[:, CP_SIGN] = sgn
    return cp


class Op:
    __slots__ = ("eng", "fn", "deps", "idx", "inc", "cum", "kind", "tag")

    def __init__(self, eng, fn, deps, idx, kind, tag):
        self.eng, self.fn, self.deps, self.idx = eng, fn, deps, idx
        self.inc, self.cum, self.kind, self.tag = 0, 0, kind, tag


class Em:
    def __init__(self):
        self.ops = []
        self.res = {}

    def op(self, eng, fn, r=(), w=(), dep=(), dma=False, tag="", force_inc=False):
        deps, seen = [], set()

        def add(o):
            if o is not None and id(o) not in seen:
                seen.add(id(o))
                deps.append(o)

        for res in r:
            e = self.res.get(res)
            if e and e[0] is not None:
                add(e[0])
        for res in w:
            e = self.res.get(res)
            if e:
                if e[0] is not None:
                    add(e[0])
                for rd in e[1]:
                    add(rd)
        for d in dep:
            add(d)
        o = Op(eng, fn, deps, len(self.ops), "d" if dma else "c", tag)
        if force_inc:
            o.inc = 1
        self.ops.append(o)
        for res in r:
            self.res.setdefault(res, [None, []])[1].append(o)
        for res in w:
            self.res[res] = [o, []]
        return o


ELIDE = True
STREAMS = [("tensor", "c"), ("vector", "c"), ("scalar", "c"),
           ("gpsimd", "c"), ("gpsimd", "d"), ("sync", "d")]


def emit_program(nc, em):
    for o in em.ops:
        if o.kind == "d":
            o.inc = 1
        for d in o.deps:
            if (d.eng, d.kind) != (o.eng, o.kind) or d.kind == "d":
                d.inc = 1
    cum = {s: 0 for s in STREAMS}
    for o in em.ops:
        s = (o.eng, o.kind)
        if o.inc:
            cum[s] += 16 if o.kind == "d" else 1
        o.cum = cum[s]
    totals = dict(cum)

    sem_cm = {s: nc.semaphore("sem_%s_%s" % s) for s in STREAMS}
    sem = {s: cm.__enter__() for s, cm in sem_cm.items()}

    know = {e: {s2: 0 for s2 in STREAMS} for e in
            ("tensor", "vector", "scalar", "gpsimd", "sync")}
    opclock = {}
    by_eng = {}
    for o in em.ops:
        by_eng.setdefault(o.eng, []).append(o)

    def run_engine(engname):
        def body(eng):
            kn = know[engname]
            for o in by_eng.get(engname, []):
                s_own = (o.eng, o.kind)
                for d in o.deps:
                    ds = (d.eng, d.kind)
                    if ds[0] == engname and ds[1] == "c":
                        continue
                    tgt = d.cum
                    if kn.get(ds, 0) >= tgt:
                        continue
                    eng.wait_ge(sem[ds], tgt)
                    kn[ds] = tgt
                    if ELIDE:
                        dk = opclock.get(id(d))
                        if dk:
                            for s2, v in dk.items():
                                if kn.get(s2, 0) < v:
                                    kn[s2] = v
                instr = o.fn(eng)
                if o.inc:
                    if instr is None:
                        raise RuntimeError("inc op returned no instr: " + o.tag)
                    instr.then_inc(sem[s_own], 16 if o.kind == "d" else 1)
                    snap = dict(kn)
                    snap[s_own] = o.cum
                    opclock[id(o)] = snap
        return body

    with nc.allow_low_precision(reason="bf16 operands intended"), \
            nc.Block() as block:
        block.tensor(run_engine("tensor"))
        block.vector(run_engine("vector"))
        block.scalar(run_engine("scalar"))
        block.gpsimd(run_engine("gpsimd"))

        def sync_body(eng):
            run_engine("sync")(eng)
            if totals[("gpsimd", "d")]:
                eng.wait_ge(sem[("gpsimd", "d")], totals[("gpsimd", "d")])
            if totals[("sync", "d")]:
                eng.wait_ge(sem[("sync", "d")], totals[("sync", "d")])
        block.sync(sync_body)

    for cm in sem_cm.values():
        cm.__exit__(None, None, None)


def build_nc(S, T, verbose=False, dbg=False):
    NT = S // T
    NP = T // 512
    NSLOT = (T + WIN) // 128
    KRING = T + WIN
    NJ = T // 128

    nc = bass.Bass("TRN2", target_bir_lowering=False)

    xT = nc.dram_tensor("xT", [D, S], F32, kind="ExternalInput").ap()
    wblob = nc.dram_tensor("wblob", [128, WCOLS], F32, kind="ExternalInput").ap()
    tabs = nc.dram_tensor("tabs", [128, 4 * S], F32, kind="ExternalInput").ap()
    cpat_in = nc.dram_tensor("cpin", [128, CPC], F32, kind="ExternalInput").ap()
    cmisc_in = nc.dram_tensor("cmin", [128, 400], BF16, kind="ExternalInput").ap()
    yout = nc.dram_tensor("y", [128, G * S], F32, kind="ExternalOutput").ap()
    wscr = nc.dram_tensor("wscr", [128, WCOLS], BF16, kind="Internal").ap()
    dbg_outs = {}

    AL = nc.alloc_sbuf_tensor

    SB_TOP = 229344
    ob_bytes = HEADS * T * 2
    OB_BASE = SB_TOP - ob_bytes
    arena_bytes = (G * T + 64) * 4
    ARENA_BASE = OB_BASE - arena_bytes
    arena = nc.alloc_sbuf_tensor_at("arena", [128, G * T + 64], F32,
                                    offset=ARENA_BASE).ap()
    xnar = AL("xnar", [128, G * (T + 1)], BF16).ap()
    xffn = AL("xffn", [128, G * (T + 1)], F32).ap()
    qk = AL("qk", [128, QG * T], BF16).ap()
    obf = nc.alloc_sbuf_tensor_at("obf", [128, HEADS * T], BF16,
                                  offset=OB_BASE).ap()
    shs = AL("shs", [128, 2560], F32).ap()
    expS = AL("expS", [128, 2 * 640], BF16).ap()
    knx = AL("knx", [128, 2 * 128], BF16).ap()
    tbuf = AL("tbuf", [128, 1024], F32).ap()
    WSLOT = 1408
    NWS = 4
    wring = AL("wring", [128, NWS * WSLOT], BF16).ap()
    sqt = AL("sqt", [128, 2 * 513], BF16).ap()
    uh = AL("uh", [128, T + 8], F32).ap()
    hbf = AL("hbf", [DST, T], BF16).ap()
    Hbuf = AL("Hbuf", [DST, 2 * (T + 1)], BF16).ap()
    scr = AL("scr", [128, 3 * 512], F32).ap()
    scrb = AL("scrb", [128, 2 * 512], BF16).ap()
    cpat = AL("cpat", [128, CPC], F32).ap()
    cmisc = AL("cmisc", [128, 400], BF16).ap()
    ones_f = AL("ones_f", [128, 128], F32).ap()
    ones_b = AL("ones_b", [128, 128], BF16).ap()
    blk2 = AL("blk2", [128, 2], BF16).ap()
    halos = AL("halos", [128, 32], F32).ap()
    dconst = AL("dconst", [128, 8], F32).ap()
    kpr = [AL("kpr%d" % l, [128, QG * KRING], BF16).ap() for l in range(2)]
    bvr = [AL("bvr%d" % l, [128, NSLOT * HEADS * 65], BF16).ap() for l in range(2)]

    h2p = nc.alloc_sbuf_tensor_at("h2p", [128, NW1 * 512], BF16,
                                  offset=ARENA_BASE).ap()
    x2 = nc.alloc_sbuf_tensor_at("x2b", [128, G * (T + 1)], BF16,
                                 offset=ARENA_BASE).ap()
    strips = nc.alloc_sbuf_tensor_at("strips", [128, 4 * T], F32,
                                     offset=OB_BASE).ap()
    recbf = AL("recbf", [128, 1024], BF16).ap()

    rem = nc.sbuf_bytes_remaining
    if verbose:
        print("SBUF bytes remaining:", rem)
    assert rem >= (SB_TOP - ARENA_BASE) + 512, "allocator near pinned slabs"

    psA = nc.alloc_psum_tensor("psA", [128, 1024], F32).ap()
    psB = nc.alloc_psum_tensor("psB", [128, 1024], F32).ap()
    psR = nc.alloc_psum_tensor("psR", [128, 1024], F32).ap()
    psX = nc.alloc_psum_tensor("psX", [128, 512], F32).ap()
    psS = nc.alloc_psum_tensor("psS", [128, 512], F32).ap()
    PRES = {"psA": 0, "psB": 2, "psR": 4, "psX": 6, "psS": 7}

    em = Em()

    def bres(name, lo, hi):
        base = PRES[name]
        return tuple("pb%d" % (base + b) for b in range(lo // 512, (hi + 511) // 512))

    def mm(e, out, lhsT, rhs, start, stop):
        return e.matmul(out, lhsT, rhs, start=start, stop=stop,
                        skip_group_check=True)

    def cpc(c, rows=128):
        return cpat[0:rows, c:c + 1]

    def xg(g, a, b):
        return arena[:, g * T + a:g * T + b]

    def xfd(g, a, b):
        base = g * (T + 1)
        return xffn[:, base + 1 + a:base + 1 + b]

    def xfs(g, a, b):
        base = g * (T + 1)
        o = 0 if g >= 4 else 1
        return xffn[:, base + o + a:base + o + b]

    def xnd(g, a, b):
        base = g * (T + 1)
        return xnar[:, base + 1 + a:base + 1 + b]

    def xns(g, a, b):
        base = g * (T + 1)
        o = 0 if g >= 4 else 1
        return xnar[:, base + o + a:base + o + b]

    def x2d(g, a, b):
        base = g * (T + 1)
        return x2[:, base + 1 + a:base + 1 + b]

    def x2s(g, a, b):
        base = g * (T + 1)
        o = 0 if g >= 4 else 1
        return x2[:, base + o + a:base + o + b]

    AR_ALL = ("arQ0", "arQ1", "arQ2", "arQ3")
    H2_RES = ("arQ0", "arQ1", "arQ2")
    X2_RES = ("arQ0", "arQ1")

    wslot = [0]
    first_touch = set()

    def wload(key, nslots=1):
        a, b = WPLAN[key]
        width = b - a
        slot = wslot[0] % NWS
        wslot[0] += 1
        dst = wring[:, slot * WSLOT:slot * WSLOT + width]
        res_slot = "wr%d" % slot
        res_reg = "wreg%d" % a
        if key not in first_touch:
            first_touch.add(key)
            ld = em.op("gpsimd", lambda e, d=dst, s=wblob[:, a:b]: e.dma_start(d, s),
                       w=(res_slot,), dma=True, tag="wc%s" % (key,))
            em.op("gpsimd", lambda e, d=wscr[:, a:b], s=dst: e.dma_start(d, s),
                  r=(res_slot,), w=(res_reg,), dma=True, tag="ws%s" % (key,))
        else:
            ld = em.op("sync", lambda e, d=dst, s=wscr[:, a:b]: e.dma_start(d, s),
                       r=(res_reg,), w=(res_slot,), dma=True, tag="wl%s" % (key,))
        return dst, ld, res_slot

    scan = {"emitted": T, "udeps": [None, None]}

    def u_col(t):
        return uh[0:DST, t:t + 1]

    def H_col(t):
        return Hbuf[0:DST, 2 * t:2 * t + 1]

    pump_gate = [0]

    def scan_pump(n=1):
        if n == 1:
            pump_gate[0] += 1
            if pump_gate[0] % 2:
                return
        for _ in range(n):
            t = scan["emitted"]
            if t >= T:
                return
            udep = scan["udeps"][min(t // 512, NP - 1)]
            if udep is None:
                return
            o1 = em.op("tensor",
                       lambda e, t=t: mm(e, psS[0:DST, 0:1],
                                         cmisc[0:DST, 384:400],
                                         H_col(t), True, True),
                       r=("H", "cmisc") + bres("psS", 0, 1),
                       w=bres("psS", 0, 1), dep=[udep], tag="smm%d" % t)
            em.op("scalar",
                  lambda e, t=t: e.activation(H_col(t + 1), psS[0:DST, 0:1],
                                              AF.Tanh, bias=u_col(t)),
                  r=bres("psS", 0, 1) + ("u",), w=("H",), dep=[o1],
                  tag="stanh%d" % t)
            scan["emitted"] = t + 1

    R64 = 64

    def ln_derive(dep_ops, a_dst, b_dst, tag):
        a_ap, a_col, a_res = a_dst
        b_ap, b_col, b_res = b_dst
        o_mean = em.op("scalar",
                       lambda e: e.mul(shs[R64:R64 + 1, 0:512],
                                       psA[R64:R64 + 1, 0:512], 1.0 / D),
                       r=bres("psA", 0, 512), w=("shsm",), dep=dep_ops,
                       tag=tag + "mean")
        o_m2 = em.op("scalar",
                     lambda e: e.mul(shs[R64:R64 + 1, 512:1024],
                                     psA[R64:R64 + 1, 512:1024], 1.0 / D),
                     r=bres("psA", 512, 1024), w=("shsm2",), dep=dep_ops,
                     tag=tag + "m2")
        o_mm = em.op("vector",
                     lambda e: e.tensor_mul(shs[R64:R64 + 1, 1024:1536],
                                            shs[R64:R64 + 1, 0:512],
                                            shs[R64:R64 + 1, 0:512]),
                     r=("shsm",), w=("shs1",), dep=[o_mean], tag=tag + "mm")
        o_var = em.op("vector",
                      lambda e: e.tensor_sub(shs[R64:R64 + 1, 1024:1536],
                                             shs[R64:R64 + 1, 512:1024],
                                             shs[R64:R64 + 1, 1024:1536]),
                      r=("shsm2", "shs1"), w=("shs1",), dep=[o_m2, o_mm],
                      tag=tag + "var")
        o_sq = em.op("scalar",
                     lambda e: e.activation(shs[R64:R64 + 1, 1024:1536],
                                            shs[R64:R64 + 1, 1024:1536],
                                            AF.Sqrt, bias=dconst[R64:R64 + 1, 1:2]),
                     r=("shs1", "dconst"), w=("shs1",), dep=[o_var],
                     tag=tag + "sqrt")
        o_a = em.op("vector",
                    lambda e: e.reciprocal(a_ap[R64:R64 + 1, a_col:a_col + 512],
                                           shs[R64:R64 + 1, 1024:1536]),
                    r=("shs1",), w=(a_res,), dep=[o_sq], tag=tag + "a")
        o_ma = em.op("vector",
                     lambda e: e.tensor_mul(shs[R64:R64 + 1, 1024:1536],
                                            shs[R64:R64 + 1, 0:512],
                                            a_ap[R64:R64 + 1, a_col:a_col + 512]),
                     r=("shsm", a_res), w=("shs1",), dep=[o_a], tag=tag + "ma")
        o_b = em.op("scalar",
                    lambda e: e.mul(b_ap[R64:R64 + 1, b_col:b_col + 512],
                                    shs[R64:R64 + 1, 1024:1536], -1.0),
                    r=("shs1",), w=(b_res,), dep=[o_ma], tag=tag + "b")
        return o_a, o_b, o_mean

    def bcast64(dst_name, lo, src_ap, src_col, src_res, dep, tag="bc"):
        ps = {"psA": psA, "psB": psB, "psR": psR, "psX": psX}[dst_name]
        o = em.op("tensor",
                  lambda e: mm(e, ps[0:128, lo:lo + 512],
                               ones_f[R64:R64 + 1, 0:128],
                               src_ap[R64:R64 + 1, src_col:src_col + 512],
                               True, True),
                  r=(src_res, "onesf") + bres(dst_name, lo, lo + 512),
                  w=bres(dst_name, lo, lo + 512), dep=dep, tag=tag)
        return o

    def dump(name, ap, res, dep=()):
        if not dbg:
            return
        shp = list(ap.shape)
        dt = ap.dtype
        t = nc.dram_tensor("dbg_" + name, shp, dt, kind="ExternalOutput").ap()
        dbg_outs[name] = True
        em.op("gpsimd", lambda e, t=t, ap=ap: e.dma_start(t, ap),
              r=res, dma=True, force_inc=True, tag="dump" + name)

    em.op("sync", lambda e: e.dma_start(cpat, cpat_in), w=("cpat",), dma=True)
    em.op("sync", lambda e: e.dma_start(cmisc, cmisc_in), w=("cmisc",), dma=True)
    em.op("vector", lambda e: e.memset(ones_f, 1.0), w=("onesf",))
    em.op("vector", lambda e: e.memset(ones_b, 1.0), w=("onesb",))
    em.op("vector", lambda e: e.memset(blk2[0:64, 0:1], 1.0), w=("blk2",))
    em.op("vector", lambda e: e.memset(blk2[64:128, 0:1], 0.0), w=("blk2",))
    em.op("vector", lambda e: e.memset(blk2[0:64, 1:2], 0.0), w=("blk2",))
    em.op("vector", lambda e: e.memset(blk2[64:128, 1:2], 1.0), w=("blk2",))
    em.op("vector", lambda e: e.memset(halos, 0.0), w=("hB", "hC", "hD"))
    em.op("vector", lambda e: e.memset(H_col(0), 0.0), w=("H",))
    em.op("vector", lambda e: e.memset(dconst[:, 1:2], LNEPS), w=("dconst",))
    em.op("vector", lambda e: e.memset(dconst[:, 2:3], QKSCALE), w=("dconst",))
    em.op("vector",
          lambda e: e.tensor_add(dconst[0:DST, 0:1], cpat[0:DST, CP_BS:CP_BS + 1],
                                 cpat[0:DST, CP_BI:CP_BI + 1]),
          r=("cpat",), w=("dconst",))
    for _i, (_l, _k) in enumerate([(0, 1), (0, 3), (1, 1), (1, 3)]):
        em.op("vector",
              lambda e, _i=_i, _l=_l, _k=_k: e.tensor_mul(
                  dconst[:, 3 + _i:4 + _i], cpat[:, CP_QS[(_l, _k)]:CP_QS[(_l, _k)] + 1],
                  cpat[:, CP_SIGN:CP_SIGN + 1]),
              r=("cpat",), w=("dconst",), tag="sgn")
    for l in range(2):
        em.op("vector",
              lambda e, l=l: e.memset(
                  bvr[l].rearrange("p (s h c) -> p s h c", h=HEADS, c=65)[:, :, :, 64:65],
                  1.0),
              w=tuple("bv%d_%d" % (l, s) for s in range(NSLOT)))
    for g in range(G):
        em.op("vector", lambda e, g=g: e.memset(
            xnar[:, g * (T + 1):g * (T + 1) + 1], 0.0), w=("xa%d" % g,))
        em.op("vector", lambda e, g=g: e.memset(
            x2[:, g * (T + 1):g * (T + 1) + 1], 0.0), w=X2_RES)

    for c in range(NT):
        t0 = c * T
        scan["emitted"] = 0
        scan["udeps"] = [None, None]

        xin = em.op("sync",
                    lambda e, t0=t0: e.dma_start(
                        arena[:, 0:G * T].rearrange("p (g t) -> p g t", g=G),
                        xT[:, t0:t0 + T].rearrange("(g p) t -> p g t", p=128)),
                    w=AR_ALL, dma=True, tag="xload%d" % c)

        if c > 0:
            em.op("vector",
                  lambda e: e.tensor_copy(
                      xnar.rearrange("p (g t) -> p g t", g=G + 0, t=T + 1)[:, :, 0:1],
                      halos[:, 24:32].rearrange("p (g o) -> p g o", o=1)),
                  r=("hA",), w=tuple("xa%d" % g for g in range(G)),
                  tag="xnh%d" % c)
            em.op("vector", lambda e: e.tensor_copy(H_col(0), H_col(T)),
                  w=("H",), tag="hcar%d" % c)

        for g in range(G):
            em.op("scalar", lambda e, g=g: e.copy(xnd(g, 0, T), xg(g, 0, T)),
                  r=("arQ%d" % (g // 2),), w=("xa%d" % g,), dep=[xin],
                  tag="xbf%d" % g)

        ab1 = []
        for p in range(NP):
            a = p * 512
            prev = []
            for g in range(G):
                sl = g % 2
                o_sq = em.op("gpsimd",
                             lambda e, g=g, a=a, sl=sl:
                             e.tensor_mul(scrb[:, sl * 512:sl * 512 + 512],
                                          xnd(g, a, a + 512), xnd(g, a, a + 512)),
                             r=("xa%d" % g,), w=("scrb%d" % sl,), tag="sq")
                o1 = em.op("tensor",
                           lambda e, g=g, a=a, first=(g == 0):
                           mm(e, psA[R64:R64 + 1, 0:512], ones_b[:, 0:1],
                              xnd(g, a, a + 512), first, g == G - 1),
                           r=("xa%d" % g, "onesb") + (bres("psA", 0, 512) if g == 0 else ()),
                           w=bres("psA", 0, 512) if g == 0 else (),
                           dep=prev, tag="sm")
                prev = [o1]
                o2 = em.op("tensor",
                           lambda e, sl=sl, first=(g == 0), g=g:
                           mm(e, psA[R64:R64 + 1, 512:1024], ones_b[:, 0:1],
                              scrb[:, sl * 512:sl * 512 + 512], first, g == G - 1),
                           r=("scrb%d" % sl, "onesb") + (bres("psA", 512, 1024) if g == 0 else ()),
                           w=bres("psA", 512, 1024) if g == 0 else (),
                           dep=[o_sq] + prev, tag="sms")
                prev = [o2]
            ab1.append(ln_derive(prev, (strips, a, "stA"),
                                 (strips, T + a, "stB"), "ln1_%d_" % p))

        xn_done = []
        for p in range(NP):
            a = p * 512
            o_a, o_b, _ = ab1[p]
            bc_a = bcast64("psX", 0, strips, a, "stA", [o_a], tag="bca")
            bc_b = bcast64("psR", 0, strips, T + a, "stB", [o_b], tag="bcb")
            for g in range(G):
                sl = g % 2
                o1 = em.op("vector",
                           lambda e, g=g, a=a, sl=sl:
                           e.tensor_mul(scr[:, sl * 512:sl * 512 + 512],
                                        xg(g, a, a + 512), psX[0:128, 0:512]),
                           r=("arQ%d" % (g // 2),) + bres("psX", 0, 512),
                           w=("scr%d" % sl,), dep=[bc_a, xin], tag="xn1")
                o2 = em.op("vector",
                           lambda e, g=g, a=a, sl=sl:
                           e.tensor_add(xfd(g, a, a + 512),
                                        scr[:, sl * 512:sl * 512 + 512],
                                        psR[0:128, 0:512]),
                           r=("scr%d" % sl,) + bres("psR", 0, 512),
                           w=("xf%d" % g,), dep=[bc_b, o1], tag="xn2")
                o3 = em.op("scalar",
                           lambda e, g=g, a=a: e.copy(xnd(g, a, a + 512),
                                                      xfd(g, a, a + 512)),
                           r=("xf%d" % g,), w=("xa%d" % g,), dep=[o2], tag="xnc")
                xn_done.append(o3)

        if c == 0:
            dump("xar", arena, AR_ALL)
            dump("xnbf", xnar, tuple("xa%d" % g for g in range(G)))
            dump("xffn0", xffn, tuple("xf%d" % g for g in range(G)))
            dump("strips", strips, ("stA", "stB"))

        em.op("vector",
              lambda e: e.tensor_copy(
                  halos[:, 24:32].rearrange("p (g o) -> p g o", o=1),
                  xnar.rearrange("p (g t) -> p g t", g=G)[:, :, T:T + 1]),
              r=tuple("xa%d" % g for g in range(G)), w=("hA",),
              dep=xn_done[-1:], tag="hsA")

        ab2 = []
        for p in range(NP):
            a = p * 512
            prev = []
            for g in range(G):
                sl = g % 2
                o_sq = em.op("gpsimd",
                             lambda e, g=g, a=a, sl=sl:
                             e.tensor_mul(scrb[:, sl * 512:sl * 512 + 512],
                                          xnd(g, a, a + 512), xnd(g, a, a + 512)),
                             r=("xa%d" % g,), w=("scrb%d" % sl,), tag="sq2")
                o1 = em.op("tensor",
                           lambda e, g=g, a=a, first=(g == 0):
                           mm(e, psA[R64:R64 + 1, 0:512], ones_b[:, 0:1],
                              xnd(g, a, a + 512), first, g == G - 1),
                           r=("xa%d" % g, "onesb") + (bres("psA", 0, 512) if g == 0 else ()),
                           w=bres("psA", 0, 512) if g == 0 else (),
                           dep=prev, tag="s2")
                prev = [o1]
                o2 = em.op("tensor",
                           lambda e, sl=sl, first=(g == 0), g=g:
                           mm(e, psA[R64:R64 + 1, 512:1024], ones_b[:, 0:1],
                              scrb[:, sl * 512:sl * 512 + 512], first, g == G - 1),
                           r=("scrb%d" % sl, "onesb") + (bres("psA", 512, 1024) if g == 0 else ()),
                           w=bres("psA", 512, 1024) if g == 0 else (),
                           dep=[o_sq] + prev, tag="ss2")
                prev = [o2]
            o_r2, o_nm2r2, _ = ln_derive(prev, (strips, 2 * T + a, "stA2"),
                                         (strips, 3 * T + a, "stB2"),
                                         "ln2_%d_" % p)
            o_a, o_b, _ = ab1[p]
            o_br = em.op("vector",
                         lambda e, a=a: e.tensor_mul(
                             shs[R64:R64 + 1, 1024:1536],
                             strips[R64:R64 + 1, T + a:T + a + 512],
                             strips[R64:R64 + 1, 2 * T + a:2 * T + a + 512]),
                         r=("stA2", "stB", "shs1"), w=("shs1",),
                         dep=[o_r2, o_b], tag="br")
            o_A2 = em.op("vector",
                         lambda e, a=a: e.tensor_mul(
                             strips[R64:R64 + 1, 2 * T + a:2 * T + a + 512],
                             strips[R64:R64 + 1, 2 * T + a:2 * T + a + 512],
                             strips[R64:R64 + 1, a:a + 512]),
                         r=("stA2", "stA"), w=("stA2",),
                         dep=[o_br, o_a], tag="A2")
            o_B2 = em.op("vector",
                         lambda e, a=a: e.tensor_add(
                             strips[R64:R64 + 1, 3 * T + a:3 * T + a + 512],
                             strips[R64:R64 + 1, 3 * T + a:3 * T + a + 512],
                             shs[R64:R64 + 1, 1024:1536]),
                         r=("stB2", "shs1"), w=("stB2",),
                         dep=[o_nm2r2, o_br], tag="B2")
            ab2.append((o_A2, o_B2))

        wu, wu_ld, wu_res = wload(("u",))
        for p in range(NP):
            a = p * 512
            o_A2, o_B2 = ab2[p]
            bcA = bcast64("psX", 0, strips, 2 * T + a, "stA2", [o_A2], tag="bcA2")
            bcB = bcast64("psR", 0, strips, 3 * T + a, "stB2", [o_B2], tag="bcB2")
            prev_mm = []
            for g in range(G):
                sl = g % 2
                o1 = em.op("vector",
                           lambda e, g=g, a=a, sl=sl:
                           e.tensor_mul(scr[:, sl * 512:sl * 512 + 512],
                                        xg(g, a, a + 512), psX[0:128, 0:512]),
                           r=("arQ%d" % (g // 2),) + bres("psX", 0, 512),
                           w=("scr%d" % sl,), dep=[bcA, xin], tag="u1")
                o2 = em.op("vector",
                           lambda e, g=g, sl=sl:
                           e.tensor_add(scr[:, sl * 512:sl * 512 + 512],
                                        scr[:, sl * 512:sl * 512 + 512],
                                        psR[0:128, 0:512]),
                           r=("scr%d" % sl,) + bres("psR", 0, 512),
                           w=("scr%d" % sl,), dep=[bcB, o1], tag="u2")
                o3 = em.op("scalar",
                           lambda e, g=g, sl=sl:
                           e.activation(scrb[:, sl * 512:sl * 512 + 512],
                                        scr[:, sl * 512:sl * 512 + 512],
                                        AF.Identity, bias=cpc(CP_LNB + g),
                                        scale=cpc(CP_LNW + g)),
                           r=("scr%d" % sl, "cpat"), w=("scrb%d" % sl,),
                           dep=[o2], tag="u3")
                m1 = em.op("tensor",
                           lambda e, g=g, sl=sl, first=(g == 0), wu=wu:
                           mm(e, psB[0:DST, 0:512], wu[:, g * DST:(g + 1) * DST],
                              scrb[:, sl * 512:sl * 512 + 512], first, False),
                           r=("scrb%d" % sl, wu_res) + (bres("psB", 0, 512) if g == 0 else ()),
                           w=bres("psB", 0, 512) if g == 0 else (),
                           dep=[o3, wu_ld] + prev_mm, tag="umm")
                prev_mm = [m1]
                m2 = em.op("tensor",
                           lambda e, g=g, sl=sl, wu=wu:
                           mm(e, psB[0:DST, 0:512],
                              wu[:, G * DST + g * DST:G * DST + (g + 1) * DST],
                              scrb[:, sl * 512:sl * 512 + 512], False, g == G - 1),
                           r=("scrb%d" % sl, wu_res),
                           dep=[o3, wu_ld] + prev_mm, tag="umm2")
                prev_mm = [m2]
            uo = em.op("scalar",
                       lambda e, a=a: e.activation(uh[0:DST, a:a + 512],
                                                   psB[0:DST, 0:512],
                                                   AF.Identity,
                                                   bias=dconst[0:DST, 0:1]),
                       r=bres("psB", 0, 512) + ("dconst",), w=("u",),
                       dep=prev_mm, tag="uev%d" % p)
            scan["udeps"][p] = uo

        if c == 0:
            dump("u", uh, ("u",), dep=[x for x in scan["udeps"] if x])

        for l in range(2):
            lin_s = xns if l == 0 else x2s
            lres = (lambda g: "xa%d" % g) if l == 0 else (lambda g, _r=X2_RES: _r)

            def lrtup(g):
                r = lres(g)
                return (r,) if isinstance(r, str) else tuple(r)

            def qkv_m(m, pump=False):
                wt, wld, wres = wload(("qkv", l, m))
                ps, pn = (psA, "psA") if m % 2 == 0 else (psB, "psB")
                prev = []
                for k in range(G):
                    if pump and k in (3, 6):
                        ilq_pump(1)
                    for cc2 in range(NP):
                        scan_pump(1)
                        prev.append(em.op(
                            "tensor",
                            lambda e, k=k, cc2=cc2, ps=ps, wt=wt, lin_s=lin_s:
                            mm(e, ps[:, cc2 * 512:cc2 * 512 + 512],
                               wt[:, k * 128:(k + 1) * 128],
                               lin_s(k, cc2 * 512, cc2 * 512 + 512),
                               k == 0, k == G - 1),
                            r=lrtup(k) + (wres,) + (bres(pn, 0, 1024) if (k == 0 and cc2 == 0) else ()),
                            w=bres(pn, 0, 1024) if (k == 0 and cc2 == 0) else (),
                            dep=[wld] + prev[-1:], tag="qkv"))
                q = m % 4
                em.op("scalar",
                      lambda e, q=q, ps=ps: e.copy(qk[:, q * T:q * T + T],
                                                   ps[:, 0:T]),
                      r=bres(pn, 0, 1024), w=("qk%d" % q,), dep=prev[-1:],
                      tag="qkev")

            def rope_swap(q):
                for (d0, d1, s0, s1) in ((0, 32, 32, 64), (32, 64, 0, 32),
                                         (64, 96, 96, 128), (96, 128, 64, 96)):
                    em.op("sync",
                          lambda e, q=q, d0=d0, d1=d1, s0=s0, s1=s1:
                          e.dma_start(scrb[d0:d1, 0:T],
                                      qk[s0:s1, q * T:q * T + T]),
                          r=("qk%d" % q,), w=("scrb0", "scrb1"), dma=True,
                          tag="swdma")

            def rope_pre(is_k, q, p):
                cS = CP_QS[(l, 2 if is_k else 0)]
                sgn_col = 3 + 2 * l + (1 if is_k else 0)
                tabbase = (2 * S if is_k else 0)
                a = p * 512
                st = {}
                st["tl"] = em.op("sync",
                           lambda e, a=a, tabbase=tabbase, t0=t0:
                           e.dma_start(
                               tbuf.rearrange("p (u t) -> p u t", u=2),
                               tabs.rearrange("p (u s) -> p u s", u=4)
                               [:, (tabbase // S):(tabbase // S) + 2,
                                t0 + a:t0 + a + 512]),
                           w=("tbuf",), dma=True, tag="tab")
                o_sq = em.op("gpsimd",
                             lambda e, q=q, a=a:
                             e.tensor_mul(sqt[:, 0:512],
                                          qk[:, q * T + a:q * T + a + 512],
                                          qk[:, q * T + a:q * T + a + 512]),
                             r=("qk%d" % q,), w=("sqtA",), tag="qsq")
                o_ss = em.op("tensor",
                             lambda e: mm(e, psX[R64:R64 + 2, 0:512],
                                          blk2, sqt[:, 0:512], True, True),
                             r=("sqtA", "blk2") + bres("psX", 0, 512),
                             w=bres("psX", 0, 512), dep=[o_sq], tag="ssq")
                o_nrm = em.op("scalar",
                              lambda e: e.activation(shs[R64:R64 + 2, 0:512],
                                                     psX[R64:R64 + 2, 0:512],
                                                     AF.Sqrt),
                              r=bres("psX", 0, 512), w=("shsm",),
                              dep=[o_ss], tag="nsqrt")
                o_mx = em.op("vector",
                             lambda e: e.tensor_scalar_max(
                                 shs[R64:R64 + 2, 0:512],
                                 shs[R64:R64 + 2, 0:512], 1e-12),
                             r=("shsm",), w=("shsm",), dep=[o_nrm])
                st["rc"] = em.op("vector",
                              lambda e: e.reciprocal(shs[R64:R64 + 2, 0:512],
                                                     shs[R64:R64 + 2, 0:512]),
                              r=("shsm",), w=("shsm",), dep=[o_mx], tag="nrc")
                st["args"] = (is_k, q, p, cS, sgn_col)
                return st

            def rope_post(st):
                is_k, q, p, cS, sgn_col = st["args"]
                a = p * 512
                tl = st["tl"]
                o_fbc = em.op("tensor",
                              lambda e: mm(e, psX[0:128, 0:512],
                                           cpat[R64:R64 + 2, CP_SEL:CP_SEL + 128],
                                           shs[R64:R64 + 2, 0:512],
                                           True, True),
                              r=("shsm", "cpat") + bres("psX", 0, 512),
                              w=bres("psX", 0, 512), dep=[st["rc"]], tag="fbc")
                o_t1 = em.op("vector",
                             lambda e, q=q, a=a, cS=cS:
                             e.scalar_tensor_tensor(
                                 scr[:, 0:512],
                                 qk[:, q * T + a:q * T + a + 512],
                                 cpc(cS), tbuf[:, 0:512],
                                 ALU.mult, ALU.mult),
                             r=("qk%d" % q, "tbuf", "cpat"),
                             w=("scr0",), dep=[tl], tag="t1")
                o_t2 = em.op("vector",
                             lambda e, a=a, sgn_col=sgn_col:
                             e.scalar_tensor_tensor(
                                 scr[:, 512:1024],
                                 scrb[:, a:a + 512],
                                 dconst[:, sgn_col:sgn_col + 1],
                                 tbuf[:, 512:1024],
                                 ALU.mult, ALU.mult),
                             r=("scrb0", "scrb1", "dconst", "tbuf"),
                             w=("scr1",), dep=[tl], tag="t2")
                o_pre = em.op("vector",
                              lambda e: e.tensor_add(scr[:, 0:512],
                                                     scr[:, 0:512],
                                                     scr[:, 512:1024]),
                              r=("scr0", "scr1"), w=("scr0",),
                              dep=[o_t1, o_t2], tag="pre")
                if is_k:
                    rc0 = (t0 + a) % KRING
                    dst = kpr[l][:, q * KRING + rc0:q * KRING + rc0 + 512]
                    wres_k = tuple(
                        "kp%d_%d" % (l, ((t0 + a) // 128 + i) % NSLOT)
                        for i in range(4))
                    em.op("vector",
                          lambda e, dst=dst: e.tensor_mul(
                              dst, scr[:, 0:512], psX[0:128, 0:512]),
                          r=("scr0",) + bres("psX", 0, 512),
                          w=wres_k, dep=[o_pre, o_fbc], tag="kout")
                else:
                    em.op("vector",
                          lambda e, q=q, a=a: e.tensor_mul(
                              qk[:, q * T + a:q * T + a + 512],
                              scr[:, 0:512], psX[0:128, 0:512]),
                          r=("scr0",) + bres("psX", 0, 512),
                          w=("qk%d" % q,), dep=[o_pre, o_fbc], tag="qout")

            ilq = []

            def ilq_pump(n=1):
                for _ in range(n):
                    if ilq:
                        ilq.pop(0)()

            def ilq_drain():
                while ilq:
                    ilq.pop(0)()

            def enqueue_rope(is_k, q):
                def pre0(q=q, is_k=is_k):
                    rope_swap(q)
                    enqueue_rope.state[(is_k, q, 0)] = rope_pre(is_k, q, 0)
                def post0(q=q, is_k=is_k):
                    rope_post(enqueue_rope.state.pop((is_k, q, 0)))
                def pre1(q=q, is_k=is_k):
                    enqueue_rope.state[(is_k, q, 1)] = rope_pre(is_k, q, 1)
                def post1(q=q, is_k=is_k):
                    rope_post(enqueue_rope.state.pop((is_k, q, 1)))
                ilq.extend([pre0, post0, pre1, post1])
            enqueue_rope.state = {}

            def v_tb(tb, wv):
                slot = (t0 // 128 + tb) % NSLOT
                ps, pn = (psA, "psA") if tb % 2 == 0 else (psB, "psB")
                prev = []
                for k in range(G):
                    scan_pump(1)
                    wtv, wldv, wresv = wv[k // 2]
                    prev.append(em.op(
                        "tensor",
                        lambda e, k=k, tb=tb, ps=ps, wtv=wtv, lin_s=lin_s:
                        mm(e, ps[:, 0:512],
                           lin_s(k, tb * 128, tb * 128 + 128),
                           wtv[:, (k % 2) * 512:(k % 2) * 512 + 512],
                           k == 0, k == G - 1),
                        r=lrtup(k) + (wresv,) + (bres(pn, 0, 512) if k == 0 else ()),
                        w=bres(pn, 0, 512) if k == 0 else (),
                        dep=[wldv] + prev[-1:], tag="vmm"))
                em.op("vector",
                      lambda e, slot=slot, ps=ps, l=l:
                      e.tensor_copy(
                          bvr[l][:, slot * HEADS * 65:(slot + 1) * HEADS * 65]
                          .rearrange("p (h z) -> p h z", z=65)[:, :, 0:64],
                          ps[:, 0:512].rearrange("p (h z) -> p h z", z=64)),
                      r=bres(pn, 0, 512), w=("bv%d_%d" % (l, slot),),
                      dep=prev[-1:], tag="vev")

            for m in range(4, 8):
                qkv_m(m)
            for q in range(QG):
                enqueue_rope(True, q)
            wv = [wload(("v", l, h4)) for h4 in range(4)]
            for tb in range(T // 128):
                v_tb(tb, wv)
                ilq_pump(1)
            for m in range(0, 4):
                qkv_m(m, pump=True)
                if m >= 1:
                    enqueue_rope(False, m - 1)
                ilq_pump(1)
            enqueue_rope(False, 3)

            Jlo = max(0, t0 // 128 - 4)
            Jhi = (t0 + T) // 128 - 1
            pend_drain = []

            def attn_head(h):
                hp = h % 2
                q = h // 2
                av_last = []
                av_virgin = {4: True, 5: True}
                pend_av = []
                for J in range(Jlo, Jhi + 1):
                    jp = J % 2
                    simps, simn = (psA, "psA") if jp == 0 else (psB, "psB")
                    r = J % 4
                    sn = 512 - 128 * r
                    a = max(128 * J, t0) - 128 * J
                    b = min(128 * J + 640, t0 + T) - 128 * J
                    slot = J % NSLOT
                    rc0 = (128 * J) % KRING
                    kres = "kp%d_%d" % (l, slot)
                    need_next = b > sn
                    if need_next:
                        o_knx = em.op("vector",
                                      lambda e, q=q, rc0=rc0, jp=jp, l=l:
                                      e.tensor_scalar_mul(
                                          knx[:, jp * 128:jp * 128 + 128],
                                          kpr[l][:, q * KRING + rc0:q * KRING + rc0 + 128],
                                          cpc(CP_BSQ)),
                                      r=(kres, "cpat"), w=("knx%d" % jp,),
                                      tag="knx")
                    cuts = sorted({a, b, min(max(a, sn), b), min(max(a, 512), b)})
                    sim_ops = []
                    for ci in range(len(cuts) - 1):
                        p0, p1 = cuts[ci], cuts[ci + 1]
                        if p1 <= p0:
                            continue
                        use_next = p0 >= sn
                        scan_pump(1)
                        if use_next:
                            lhs = knx[64 * hp:64 * hp + 64, jp * 128:jp * 128 + 128]
                            lr = ("knx%d" % jp,)
                            ldep = [o_knx]
                        else:
                            lhs = kpr[l][64 * hp:64 * hp + 64,
                                         q * KRING + rc0:q * KRING + rc0 + 128]
                            lr = (kres,)
                            ldep = []
                        sim_ops.append(em.op(
                            "tensor",
                            lambda e, p0=p0, p1=p1, lhs=lhs, simps=simps, J=J, q=q,
                            h=h, t0=t0:
                            mm(e, simps[:, p0:p1], lhs,
                               qk[64 * (h % 2):64 * (h % 2) + 64,
                                  q * T + 128 * J + p0 - t0:q * T + 128 * J + p1 - t0],
                               True, True),
                            r=lr + ("qk%d" % q,) + bres(simn, p0, p1),
                            w=bres(simn, p0, p1), dep=ldep, tag="sim"))
                    o_exp = em.op("scalar",
                                  lambda e, a=a, b=b, jp=jp, simps=simps:
                                  e.activation(expS[:, jp * 640 + a:jp * 640 + b],
                                               simps[:, a:b], AF.Exp,
                                               scale=QKSCALE),
                                  r=bres(simn, a, b), w=("ex%d" % jp,),
                                  dep=sim_ops, tag="exp")
                    mdeps = [o_exp]
                    mask_ranges = []
                    u0, u1 = a, min(b, 128)
                    if u1 > u0:
                        mask_ranges.append((u0, u1))
                        mdeps.append(em.op(
                            "gpsimd",
                            lambda e, u0=u0, u1=u1, jp=jp:
                            e.tensor_mul(expS[:, jp * 640 + u0:jp * 640 + u1],
                                         expS[:, jp * 640 + u0:jp * 640 + u1],
                                         cmisc[:, 128 + u0:128 + u1]),
                            r=("ex%d" % jp, "cmisc"), w=("ex%d" % jp,),
                            dep=[o_exp], tag="mU"))
                    l0, l1 = max(a, 512), b
                    if l1 > l0:
                        mask_ranges.append((l0, l1))
                        mdeps.append(em.op(
                            "gpsimd",
                            lambda e, l0=l0, l1=l1, jp=jp:
                            e.tensor_mul(expS[:, jp * 640 + l0:jp * 640 + l1],
                                         expS[:, jp * 640 + l0:jp * 640 + l1],
                                         cmisc[:, 256 + l0 - 512:256 + l1 - 512]),
                            r=("ex%d" % jp, "cmisc"), w=("ex%d" % jp,),
                            dep=[o_exp], tag="mL"))
                    if dbg and c == 0 and l == 0 and h == 0 and J in (0, 1):
                        dump("expS%d" % J, expS, ("ex%d" % jp,), )

                    ilo = 128 * J + a
                    ihi = 128 * J + b
                    newlo = 128 * J + 512
                    avcuts = sorted({ilo, ihi, min(max(ilo, newlo), ihi),
                                     min(max(ilo, t0 + 512), ihi)})
                    av_emits = []
                    for ci in range(len(avcuts) - 1):
                        i0, i1 = avcuts[ci], avcuts[ci + 1]
                        if i1 <= i0:
                            continue
                        bank = 4 + (i0 - t0) // 512
                        st = av_virgin[bank]
                        av_virgin[bank] = False
                        scan_pump(1)
                        c0m, c1m = i0 - 128 * J, i1 - 128 * J
                        if any(c1m > m0 and c0m < m1 for m0, m1 in mask_ranges):
                            pdeps = list(mdeps)
                        else:
                            pdeps = [o_exp]
                        av_emits.append((i0, i1, st, jp, J, slot, pdeps))
                    if J == Jlo + 1 and pend_drain:
                        pend_drain.pop(0)()
                    if (J - Jlo) % 3 == 2:
                        ilq_pump(1)
                    pend_av.append(av_emits)
                    if len(pend_av) > globals().get("AVLAG", 1):
                        for i0, i1, st, jp, Jx, slot, md in pend_av.pop(0):
                            av_last.append(em.op(
                                "tensor",
                                lambda e, i0=i0, i1=i1, st=st, jp=jp, Jx=Jx,
                                slot=slot, l=l, h=h, t0=t0:
                                mm(e, psR[0:65, i0 - t0:i1 - t0],
                                   bvr[l][:, slot * HEADS * 65 + h * 65:
                                          slot * HEADS * 65 + h * 65 + 65],
                                   expS[:, jp * 640 + i0 - 128 * Jx:jp * 640 + i1 - 128 * Jx],
                                   st, True),
                                r=("bv%d_%d" % (l, slot), "ex%d" % jp)
                                + bres("psR", i0 - t0, i1 - t0),
                                w=bres("psR", i0 - t0, i1 - t0),
                                dep=md, tag="av"))
                for i0, i1, st, jp, Jx, slot, md in (pend_av.pop(0) if pend_av else []):
                    av_last.append(em.op(
                        "tensor",
                        lambda e, i0=i0, i1=i1, st=st, jp=jp, Jx=Jx,
                        slot=slot, l=l, h=h, t0=t0:
                        mm(e, psR[0:65, i0 - t0:i1 - t0],
                           bvr[l][:, slot * HEADS * 65 + h * 65:
                                  slot * HEADS * 65 + h * 65 + 65],
                           expS[:, jp * 640 + i0 - 128 * Jx:jp * 640 + i1 - 128 * Jx],
                           st, True),
                        r=("bv%d_%d" % (l, slot), "ex%d" % jp)
                        + bres("psR", i0 - t0, i1 - t0),
                        w=bres("psR", i0 - t0, i1 - t0),
                        dep=md, tag="av"))

                def emit_drain(h=h, av_last=av_last):
                    o_rc = em.op("vector",
                                 lambda e: e.reciprocal(recbf[R64:R64 + 1, 0:T],
                                                        psR[R64:R64 + 1, 0:T]),
                                 r=bres("psR", 0, T), w=("rec",), dep=av_last,
                                 tag="drec")
                    for p in range(NP):
                        a = p * 512
                        o_bc = em.op("tensor",
                                     lambda e, a=a: mm(e, psX[0:64, 0:512],
                                                       ones_b[R64:R64 + 1, 0:64],
                                                       recbf[R64:R64 + 1, a:a + 512],
                                                       True, True),
                                     r=("rec", "onesb") + bres("psX", 0, 512),
                                     w=bres("psX", 0, 512), dep=[o_rc], tag="dbc")
                        o_cf = em.op("scalar",
                                     lambda e, a=a: e.copy(shs[0:64, a:a + 512],
                                                           psX[0:64, 0:512]),
                                     r=bres("psX", 0, 512), w=("fbc%d" % p,),
                                     dep=[o_bc], tag="dcf")
                        em.op("vector",
                              lambda e, a=a, h=h: e.tensor_mul(
                                  obf[0:64, h * T + a:h * T + a + 512],
                                  psR[0:64, a:a + 512], shs[0:64, a:a + 512]),
                              r=bres("psR", a, a + 512) + ("fbc%d" % p,),
                              w=("ob%d" % h,), dep=[o_cf] + av_last, tag="dmul")
                pend_drain.append(emit_drain)

            for q in range(QG):
                while len(ilq) > 4 * (QG - 1 - q):
                    ilq_pump(1)
                attn_head(2 * q)
                attn_head(2 * q + 1)

            if c == 0:
                dump("kpr%d" % l, kpr[l],
                     tuple("kp%d_%d" % (l, s) for s in range(NSLOT)))
                dump("qp%d" % l, qk, tuple("qk%d" % i for i in range(4)))
                dump("bvr%d" % l, bvr[l],
                     tuple("bv%d_%d" % (l, s) for s in range(NSLOT)))
                dump("obf%d" % l, obf, tuple("ob%d" % h for h in range(HEADS)))

            while pend_drain:
                pend_drain.pop(0)()

            for m in range(8):
                wt, wld, wres = wload(("wo", l, m))
                for cc2 in range(NP):
                    ps, pn = (psA, "psA") if (m * NP + cc2) % 2 == 0 else (psB, "psB")
                    prev = []
                    for h in range(HEADS):
                        scan_pump(1)
                        prev.append(em.op(
                            "tensor",
                            lambda e, h=h, cc2=cc2, ps=ps, wt=wt:
                            mm(e, ps[:, 0:512], wt[0:64, h * 128:(h + 1) * 128],
                               obf[0:64, h * T + cc2 * 512:h * T + cc2 * 512 + 512],
                               h == 0, h == HEADS - 1),
                            r=("ob%d" % h, wres) + (bres(pn, 0, 512) if h == 0 else ()),
                            w=bres(pn, 0, 512) if h == 0 else (),
                            dep=[wld] + prev[-1:], tag="womm"))
                    em.op("vector",
                          lambda e, m=m, cc2=cc2, ps=ps:
                          e.tensor_add(xfd(m, cc2 * 512, cc2 * 512 + 512),
                                       ps[:, 0:512],
                                       xfd(m, cc2 * 512, cc2 * 512 + 512)),
                          r=bres(pn, 0, 512) + ("xf%d" % m,), w=("xf%d" % m,),
                          dep=prev[-1:], tag="woev")

            if c == 0:
                dump("xwo%d" % l, xffn, tuple("xf%d" % g for g in range(G)))

            hres = "hB" if l == 0 else "hD"
            hcol = 0 if l == 0 else 16
            o_hr = em.op("vector",
                         lambda e, hcol=hcol: e.tensor_copy(
                             xffn.rearrange("p (g t) -> p g t", g=G)[:, :, 0:1],
                             halos[:, hcol:hcol + 8].rearrange("p (g o) -> p g o", o=1)),
                         r=(hres,), w=tuple("xf%d" % g for g in range(G)),
                         tag="hrest")
            o_hs = em.op("vector",
                         lambda e, hcol=hcol: e.tensor_copy(
                             halos[:, hcol:hcol + 8].rearrange("p (g o) -> p g o", o=1),
                             xffn.rearrange("p (g t) -> p g t", g=G)[:, :, T:T + 1]),
                         r=tuple("xf%d" % g for g in range(G)), w=(hres,),
                         tag="hsave")

            for p in range(NP):
                a = p * 512
                prev = []
                for g in range(G):
                    base = g * (T + 1)
                    o_c = em.op("scalar",
                                lambda e, base=base, a=a:
                                e.copy(sqt[:, 0:513],
                                       xffn[:, base + a:base + a + 513]),
                                r=("xf%d" % g,), w=("sqtA",), dep=[o_hr],
                                tag="l2c")
                    o_s = em.op("scalar",
                                lambda e, base=base, a=a:
                                e.activation(sqt[:, 513:1026],
                                             xffn[:, base + a:base + a + 513],
                                             AF.Square),
                                r=("xf%d" % g,), w=("sqtB",), dep=[o_hr],
                                tag="l2s")
                    off = 0 if g >= 4 else 1
                    o1 = em.op("tensor",
                               lambda e, off=off, first=(g == 0), g=g:
                               mm(e, psA[R64:R64 + 1, 0:512], ones_b[:, 0:1],
                                  sqt[:, off:off + 512], first, g == G - 1),
                               r=("sqtA", "onesb") + (bres("psA", 0, 512) if g == 0 else ()),
                               w=bres("psA", 0, 512) if g == 0 else (),
                               dep=[o_c] + prev, tag="l2m")
                    prev = [o1]
                    o2 = em.op("tensor",
                               lambda e, off=off, first=(g == 0), g=g:
                               mm(e, psA[R64:R64 + 1, 512:1024], ones_b[:, 0:1],
                                  sqt[:, 513 + off:513 + off + 512], first,
                                  g == G - 1),
                               r=("sqtB", "onesb") + (bres("psA", 512, 1024) if g == 0 else ()),
                               w=bres("psA", 512, 1024) if g == 0 else (),
                               dep=[o_s] + prev, tag="l2ms")
                    prev = [o2]
                o_a, o_b, _ = ln_derive(prev, (shs, 1536, "l2a"),
                                        (shs, 2048, "l2b"), "l2d_%d_" % p)
                bc_a = bcast64("psX", 0, shs, 1536, "l2a", [o_a], tag="l2bca")
                bc_b = bcast64("psR", 0, shs, 2048, "l2b", [o_b], tag="l2bcb")
                for g in range(G):
                    sl = g % 2
                    o1 = em.op("vector",
                               lambda e, g=g, a=a, sl=sl:
                               e.tensor_mul(scr[:, sl * 512:sl * 512 + 512],
                                            xfs(g, a, a + 512), psX[0:128, 0:512]),
                               r=("xf%d" % g,) + bres("psX", 0, 512),
                               w=("scr%d" % sl,), dep=[bc_a], tag="xh1")
                    o2 = em.op("vector",
                               lambda e, g=g, sl=sl:
                               e.tensor_add(scr[:, sl * 512:sl * 512 + 512],
                                            scr[:, sl * 512:sl * 512 + 512],
                                            psR[0:128, 0:512]),
                               r=("scr%d" % sl,) + bres("psR", 0, 512),
                               w=("scr%d" % sl,), dep=[bc_b, o1], tag="xh2")
                    em.op("scalar",
                          lambda e, g=g, a=a, sl=sl, l=l:
                          e.activation(xnd(g, a, a + 512),
                                       scr[:, sl * 512:sl * 512 + 512],
                                       AF.Identity, bias=cpc(CP_LB + 8 * l + g),
                                       scale=cpc(CP_LG + 8 * l + g)),
                          r=("scr%d" % sl, "cpat"), w=("xa%d" % g,),
                          dep=[o2], tag="xh3")

            if c == 0:
                dump("xh%d" % l, xnar, tuple("xa%d" % g for g in range(G)))

            for cc2 in range(NP):
                a = cc2 * 512
                for m in range(NW1):
                    w = W1M[m]
                    wtA, wldA, wresA = wload(("w1a", l, m))
                    wtB, wldB, wresB = wload(("w1g", l, m))
                    prevA, prevB = [], []
                    for k in range(G):
                        scan_pump(1)
                        prevA.append(em.op(
                            "tensor",
                            lambda e, k=k, w=w, a=a, wt=wtA:
                            mm(e, psA[0:w, 0:512], wt[:, k * w:(k + 1) * w],
                               xnd(k, a, a + 512), k == 0, k == G - 1),
                            r=("xa%d" % k, wresA) + (bres("psA", 0, 512) if k == 0 else ()),
                            w=bres("psA", 0, 512) if k == 0 else (),
                            dep=[wldA] + prevA[-1:], tag="w1a"))
                        scan_pump(1)
                        prevB.append(em.op(
                            "tensor",
                            lambda e, k=k, w=w, a=a, wt=wtB:
                            mm(e, psB[0:w, 0:512],
                               wt[:, k * w:(k + 1) * w],
                               xnd(k, a, a + 512), k == 0, k == G - 1),
                            r=("xa%d" % k, wresB) + (bres("psB", 0, 512) if k == 0 else ()),
                            w=bres("psB", 0, 512) if k == 0 else (),
                            dep=[wldB] + prevB[-1:], tag="w1g"))
                    o_g = em.op("scalar",
                                lambda e, w=w, m=m, l=l:
                                e.activation(scr[:, 1024:1024 + 512][0:w, :],
                                             psB[0:w, 0:512], AF.Gelu,
                                             bias=cpc(CP_B1G + NW1 * l + m, w)),
                                r=bres("psB", 0, 512) + ("cpat",), w=("scr2",),
                                dep=prevB[-1:], tag="gelu")
                    em.op("vector",
                          lambda e, w=w, m=m, l=l:
                          e.scalar_tensor_tensor(
                              h2p[0:w, m * 512:m * 512 + 512],
                              psA[0:w, 0:512], cpc(CP_B1 + NW1 * l + m, w),
                              scr[0:w, 1024:1024 + 512], ALU.add, ALU.mult),
                          r=bres("psA", 0, 512) + ("scr2", "cpat"),
                          w=H2_RES + ("h2t%d" % m,), dep=[o_g] + prevA[-1:],
                          tag="h2")
                for mo in range(8):
                    wh = [wload(("w2", l, mo, 0)), wload(("w2", l, mo, 1))]
                    prev = []
                    for k in range(NW1):
                        kk = W1M[k]
                        scan_pump(1)
                        wt, wld, wres = wh[k // 11]
                        prev.append(em.op(
                            "tensor",
                            lambda e, k=k, kk=kk, wt=wt:
                            mm(e, psR[0:128, 0:512],
                               wt[0:kk, (k % 11) * 128:(k % 11) * 128 + 128],
                               h2p[0:kk, k * 512:k * 512 + 512],
                               k == 0, k == NW1 - 1),
                            r=("h2t%d" % k, wres) + (bres("psR", 0, 512) if k == 0 else ()),
                            w=bres("psR", 0, 512) if k == 0 else (),
                            dep=[wld] + prev[-1:], tag="w2mm"))
                    em.op("vector",
                          lambda e, mo=mo, a=a, l=l:
                          e.scalar_tensor_tensor(
                              xfd(mo, a, a + 512), psR[0:128, 0:512],
                              cpc(CP_B2 + 8 * l + mo),
                              xfd(mo, a, a + 512), ALU.add, ALU.add),
                          r=bres("psR", 0, 512) + ("cpat", "xf%d" % mo),
                          w=("xf%d" % mo,), dep=prev[-1:], tag="w2ev")

            if l == 0:
                o_hr2 = em.op("vector",
                              lambda e: e.tensor_copy(
                                  xffn.rearrange("p (g t) -> p g t", g=G)[:, :, 0:1],
                                  halos[:, 8:16].rearrange("p (g o) -> p g o", o=1)),
                              r=("hC",), w=tuple("xf%d" % g for g in range(G)),
                              tag="hrC")
                em.op("vector",
                      lambda e: e.tensor_copy(
                          halos[:, 8:16].rearrange("p (g o) -> p g o", o=1),
                          xffn.rearrange("p (g t) -> p g t", g=G)[:, :, T:T + 1]),
                      r=tuple("xf%d" % g for g in range(G)), w=("hC",),
                      tag="hsC")
                for g in range(G):
                    em.op("scalar",
                          lambda e, g=g: e.copy(
                              x2[:, g * (T + 1):(g + 1) * (T + 1)],
                              xffn[:, g * (T + 1):(g + 1) * (T + 1)]),
                          r=("xf%d" % g,), w=X2_RES, dep=[o_hr2], tag="x2c")

        if c == 0:
            dump("xw2", xffn, tuple("xf%d" % g for g in range(G)))
            dump("H", Hbuf, ("H",))

        scan_pump(T)
        for g in range(G):
            em.op("scalar",
                  lambda e, g=g: e.activation(xfd(g, 0, T), xfd(g, 0, T),
                                              AF.Silu),
                  r=("xf%d" % g,), w=("xf%d" % g,), tag="silu")
        h2rd = [em.res.get("h2t%d" % m, [None, []]) for m in range(NW1)]
        h2deps = [o for e2 in h2rd for o in ([e2[0]] if e2[0] else []) + e2[1]]
        xin2 = em.op("sync",
                     lambda e, t0=t0: e.dma_start(
                         arena[:, 0:G * T].rearrange("p (g t) -> p g t", g=G),
                         xT[:, t0:t0 + T].rearrange("(g p) t -> p g t", p=128)),
                     w=AR_ALL, dma=True, dep=h2deps, tag="xre%d" % c)
        for p in range(NP):
            a = p * 512
            bcA = bcast64("psX", 0, strips, 2 * T + a, "stA2", [], tag="cbcA")
            bcB = bcast64("psR", 512, strips, 3 * T + a, "stB2", [], tag="cbcB")
            for g in range(G):
                sl = g % 2
                o1 = em.op("vector",
                           lambda e, g=g, a=a, sl=sl:
                           e.tensor_mul(scr[:, sl * 512:sl * 512 + 512],
                                        xg(g, a, a + 512), psX[0:128, 0:512]),
                           r=("arQ%d" % (g // 2),) + bres("psX", 0, 512),
                           w=("scr%d" % sl,), dep=[bcA, xin2], tag="c1")
                o2 = em.op("vector",
                           lambda e, g=g, sl=sl:
                           e.tensor_add(scr[:, sl * 512:sl * 512 + 512],
                                        scr[:, sl * 512:sl * 512 + 512],
                                        psR[0:128, 512:1024]),
                           r=("scr%d" % sl,) + bres("psR", 512, 1024),
                           w=("scr%d" % sl,), dep=[bcB, o1], tag="c2")
                em.op("scalar",
                      lambda e, g=g, a=a, sl=sl:
                      e.activation(xnd(g, a, a + 512),
                                   scr[:, sl * 512:sl * 512 + 512],
                                   AF.Identity, bias=cpc(CP_LNB + g),
                                   scale=cpc(CP_LNW + g)),
                      r=("scr%d" % sl, "cpat"), w=("xa%d" % g,), dep=[o2],
                      tag="c3")
        o_hb = em.op("scalar", lambda e: e.copy(
                         hbf, Hbuf.rearrange("p (t u) -> p t u", u=2)[0:DST, 1:T + 1, 0:1]),
                     r=("H",), w=("hbf",), tag="hcast")
        for g in range(G):
            wt, wld, wres = wload(("wg", g))
            womt, womld, womres = wload(("wom",))
            for p in range(NP):
                a = p * 512
                prev = []
                for k in range(G):
                    scan_pump(1)
                    prev.append(em.op(
                        "tensor",
                        lambda e, k=k, a=a, wt=wt: mm(e, psA[0:128, 0:512],
                                               wt[:, k * 128:(k + 1) * 128],
                                               xnd(k, a, a + 512),
                                               k == 0, k == G - 1),
                        r=("xa%d" % k, wres) + (bres("psA", 0, 512) if k == 0 else ()),
                        w=bres("psA", 0, 512) if k == 0 else (),
                        dep=[wld] + prev[-1:], tag="gmm"))
                o_om = em.op("tensor",
                             lambda e, g=g, a=a, womt=womt:
                             mm(e, psB[0:128, 0:512],
                                womt[0:DST, g * 128:(g + 1) * 128],
                                hbf[:, a:a + 512], True, True),
                             r=("hbf", womres) + bres("psB", 0, 512),
                             w=bres("psB", 0, 512), dep=[womld, o_hb],
                             tag="ommm")
                o_sg = em.op("scalar",
                             lambda e, g=g: e.activation(scr[:, 0:512],
                                                         psA[0:128, 0:512],
                                                         AF.Sigmoid,
                                                         bias=cpc(CP_BG + g)),
                             r=bres("psA", 0, 512) + ("cpat",), w=("scr0",),
                             dep=prev[-1:], tag="sig")
                o_s = em.op("vector",
                            lambda e, g=g: e.scalar_tensor_tensor(
                                scr[:, 512:1024], psB[0:128, 0:512],
                                cpc(CP_BO + g), scr[:, 0:512],
                                ALU.add, ALU.mult),
                            r=bres("psB", 0, 512) + ("cpat", "scr0"),
                            w=("scr1",), dep=[o_om, o_sg], tag="sg")
                bcA2 = bcast64("psX", 0, strips, a, "stA", [], tag="fbca")
                bcB2 = bcast64("psR", 512, strips, T + a, "stB", [], tag="fbcb")
                o_t = em.op("vector",
                            lambda e, g=g, a=a: e.tensor_mul(
                                scr[:, 1024:1536], xg(g, a, a + 512),
                                psX[0:128, 0:512]),
                            r=("arQ%d" % (g // 2),) + bres("psX", 0, 512),
                            w=("scr2",), dep=[bcA2, xin2], tag="ft")
                o_v = em.op("vector",
                            lambda e: e.tensor_add(scr[:, 1024:1536],
                                                   scr[:, 1024:1536],
                                                   psR[0:128, 512:1024]),
                            r=("scr2",) + bres("psR", 512, 1024),
                            w=("scr2",), dep=[bcB2, o_t], tag="fv")
                o_w = em.op("vector",
                            lambda e: e.tensor_add(scr[:, 1024:1536],
                                                   scr[:, 1024:1536],
                                                   scr[:, 512:1024]),
                            r=("scr2", "scr1"), w=("scr2",), dep=[o_v, o_s],
                            tag="fw")
                o_z = em.op("vector",
                            lambda e, g=g, a=a: e.tensor_mul(
                                scr[:, 1024:1536], scr[:, 1024:1536],
                                xfd(g, a, a + 512)),
                            r=("scr2", "xf%d" % g), w=("scr2",), dep=[o_w],
                            tag="fz")
                o_f = em.op("vector",
                            lambda e, g=g, a=a: e.tensor_add(
                                xfd(g, a, a + 512), scr[:, 1024:1536],
                                xg(g, a, a + 512)),
                            r=("scr2", "arQ%d" % (g // 2)),
                            w=("xf%d" % g,), dep=[o_z, xin2], tag="ff")
                em.op("gpsimd",
                      lambda e, g=g, a=a, t0=t0: e.dma_start(
                          yout[:, g * S + t0 + a:g * S + t0 + a + 512],
                          xfd(g, a, a + 512)),
                      r=("xf%d" % g,), dma=True, force_inc=True,
                      dep=[o_f], tag="yout")

    emit_program(nc, em)
    return nc


def _host_prep(inputs, S):
    f = inputs
    wblob = pack_wblob(f)
    TQc, TQs, TKc, TKs, base64 = rope_tables(S)
    tabs = np.concatenate([TQc, TQs, TKc, TKs], 1)
    cpat = pack_cpat(f, base64)
    cmisc = const_misc(np.asarray(inputs["L0_attn_A"], np.float32).T)
    x = np.asarray(f["L0_attn_x"], np.float32)
    in_maps = []
    for b in range(NCORE):
        in_maps.append({
            "xT": np.ascontiguousarray(x[b].T),
            "wblob": wblob,
            "tabs": tabs,
            "cpin": cpat,
            "cmin": cmisc,
        })
    return in_maps


def run(inputs, S=S_FULL, T=T_CHUNK, verbose=False, dbg=False):
    in_maps = _host_prep(inputs, S)
    nc = build_nc(S, T, verbose=verbose, dbg=dbg)
    res = run_bass_kernel_spmd(nc, in_maps, core_ids=list(range(NCORE)))
    outs = []
    for b in range(NCORE):
        y = res.results[b]["y"]
        y = y.reshape(128, G, S).transpose(1, 0, 2).reshape(D, S)
        outs.append(y.T)
    out = np.stack(outs, 0).astype(np.float32)
    if dbg:
        return out, [{k: v for k, v in res.results[b].items() if k.startswith("dbg_")}
                     for b in range(NCORE)]
    return out


def kernel(**inputs):
    return run(inputs, S=S_FULL, T=T_CHUNK)


# version 07
# speedup vs baseline: 1.1455x; 1.1107x over previous
import numpy as np
import ml_dtypes
import concourse.bass as bass
import concourse.mybir as mybir
from concourse.bass_utils import run_bass_kernel_spmd

F32 = mybir.dt.float32
BF16 = mybir.dt.bfloat16
AF = mybir.ActivationFunctionType
ALU = mybir.AluOpType

D = 1024
G = 8
DST = 16
HEADS = 8
DH = 64
INNER = 512
QG = 4
WIN = 512
HID = 2730
QKSCALE = 8.0
LNEPS = 1e-5
NCORE = 8

S_FULL = 8192
T_CHUNK = 1024

W1M = [128] * 21 + [42]
NW1 = len(W1M)

def build_wplan():
    plan = {}
    col = [0]

    def grp(key, width):
        a = col[0]
        plan[key] = (a, a + width)
        col[0] += width

    grp(("u",), 2 * G * DST)
    for l in range(2):
        for m in range(8):
            grp(("qkv", l, m), G * 128)
        for h4 in range(4):
            grp(("v", l, h4), 2 * 512)
        for m in range(8):
            grp(("wo", l, m), HEADS * 128)
        for m in range(NW1):
            grp(("w1a", l, m), G * W1M[m])
            grp(("w1g", l, m), G * W1M[m])
        for m in range(8):
            grp(("w2", l, m, 0), 11 * 128)
            grp(("w2", l, m, 1), 11 * 128)
    for m in range(8):
        grp(("wg", m), G * 128)
    grp(("wom",), 8 * 128)
    return plan, col[0]


WPLAN, WCOLS = build_wplan()


def pack_wblob(f):
    blob = np.zeros((128, WCOLS), np.float32)

    def put(key, mat):
        a, b = WPLAN[key]
        assert mat.shape[1] == b - a and mat.shape[0] <= 128, (key, mat.shape)
        blob[: mat.shape[0], a:b] = mat

    WsT = np.ascontiguousarray(np.asarray(f["L0_attn_Ws"], np.float32).T)
    WiT = np.ascontiguousarray(np.asarray(f["L0_attn_Wi"], np.float32).T)
    put(("u",), np.concatenate(
        [WsT[128 * g:128 * (g + 1)] for g in range(G)]
        + [WiT[128 * g:128 * (g + 1)] for g in range(G)], axis=1))

    wqkv = np.asarray(f["L0_ffn_wqkv"], np.float32)
    wo = np.asarray(f["L0_ffn_wo"], np.float32)
    w1 = np.asarray(f["L0_ffn_w1"], np.float32)
    w2 = np.asarray(f["L0_ffn_w2"], np.float32)
    for l in range(2):
        for m in range(8):
            put(("qkv", l, m), np.concatenate(
                [wqkv[l][128 * k:128 * (k + 1), 128 * m:128 * (m + 1)]
                 for k in range(G)], 1))
        for h4 in range(4):
            put(("v", l, h4), np.concatenate(
                [wqkv[l][128 * k:128 * (k + 1), 1024:1536]
                 for k in range(2 * h4, 2 * h4 + 2)], 1))
        for m in range(8):
            tiles = []
            for h in range(HEADS):
                t = np.zeros((128, 128), np.float32)
                t[:64] = wo[l][64 * h:64 * (h + 1), 128 * m:128 * (m + 1)]
                tiles.append(t)
            put(("wo", l, m), np.concatenate(tiles, 1))
        for m in range(NW1):
            w = W1M[m]
            put(("w1a", l, m), np.concatenate(
                [w1[l][128 * k:128 * (k + 1), 128 * m:128 * m + w]
                 for k in range(G)], 1))
            put(("w1g", l, m), np.concatenate(
                [w1[l][128 * k:128 * (k + 1), HID + 128 * m:HID + 128 * m + w]
                 for k in range(G)], 1))
        for m in range(8):
            for half in range(2):
                tiles = []
                for k in range(11 * half, 11 * half + 11):
                    kk = min(128, HID - 128 * k)
                    t = np.zeros((128, 128), np.float32)
                    t[:kk] = w2[l][128 * k:128 * k + kk, 128 * m:128 * (m + 1)]
                    tiles.append(t)
                put(("w2", l, m, half), np.concatenate(tiles, 1))
    WgT = np.ascontiguousarray(np.asarray(f["L0_attn_Wg"], np.float32).T)
    for m in range(8):
        put(("wg", m), np.concatenate(
            [WgT[128 * k:128 * (k + 1), 128 * m:128 * (m + 1)]
             for k in range(G)], 1))
    WomT = np.ascontiguousarray(np.asarray(f["L0_attn_Wo"], np.float32).T)
    tiles = []
    for m in range(8):
        t = np.zeros((128, 128), np.float32)
        t[:DST] = WomT[:, 128 * m:128 * (m + 1)]
        tiles.append(t)
    put(("wom",), np.concatenate(tiles, 1))
    return blob


def rope_tables(S):
    half = np.arange(0, DH, 2, dtype=np.float64)
    inv_freq = 1.0 / 10000.0 ** (half / DH)
    t = np.arange(S, dtype=np.float64)
    freqs = np.concatenate([t[:, None] * inv_freq] * 2, 1)
    base = np.concatenate([(half + 0.4 * DH) / (1.4 * DH)] * 2)
    cos, sin = np.cos(freqs), np.sin(freqs)
    pq = base[None, :] ** (((t % WIN) / (WIN // 2))[:, None])
    pk = base[None, :] ** ((-(t % WIN) / (WIN // 2))[:, None])

    def lay(tab):
        return np.concatenate([tab.T, tab.T], 0).astype(np.float32)

    return (lay(cos * pq), lay(sin * pq), lay(cos * pk), lay(sin * pk),
            base.astype(np.float32))


def const_misc(AT=None):
    P = np.zeros((128, 128), np.float32)
    for m in range(128):
        base = (m // 64) * 64
        dm = m % 64
        if dm < 32:
            P[base + dm + 32, m] = -1.0
        else:
            P[base + dm - 32, m] = 1.0
    triU = np.zeros((128, 128), np.float32)
    triL = np.zeros((128, 128), np.float32)
    for p in range(128):
        triU[p, p:] = 1.0
        triL[p, :p + 1] = 1.0
    at = np.zeros((128, 16), np.float32)
    if AT is not None:
        at[:16] = AT
    return np.concatenate([P, triU, triL, at], 1).astype(ml_dtypes.bfloat16)


CP_BSQ = 0
CP_QS = {}
_c = 1
for _l in range(2):
    for _k in range(4):
        CP_QS[(_l, _k)] = _c
        _c += 1
CP_AT = _c; _c += 16
CP_BS = _c; _c += 1
CP_BI = _c; _c += 1
CP_BO = _c; _c += 8
CP_BG = _c; _c += 8
CP_LNW = _c; _c += 8
CP_LNB = _c; _c += 8
CP_LG = _c; _c += 16
CP_LB = _c; _c += 16
CP_B1 = _c; _c += 2 * NW1
CP_B1G = _c; _c += 2 * NW1
CP_B2 = _c; _c += 16
CP_SEL = _c; _c += 128
CP_SIGN = _c; _c += 1
CPC = _c


def pack_cpat(f, base64):
    cp = np.zeros((128, CPC), np.float32)
    cp[:, CP_BSQ] = np.tile(base64 ** 2, 2)
    qs = np.asarray(f["L0_ffn_q_scale"], np.float32)
    ks = np.asarray(f["L0_ffn_k_scale"], np.float32)
    for l in range(2):
        cp[:, CP_QS[(l, 0)]] = np.tile(qs[l], 2)
        cp[:, CP_QS[(l, 1)]] = np.tile(np.concatenate([qs[l][32:], qs[l][:32]]), 2)
        cp[:, CP_QS[(l, 2)]] = np.tile(ks[l], 2)
        cp[:, CP_QS[(l, 3)]] = np.tile(np.concatenate([ks[l][32:], ks[l][:32]]), 2)
    cp[:DST, CP_AT:CP_AT + 16] = np.asarray(f["L0_attn_A"], np.float32).T
    cp[:DST, CP_BS] = np.asarray(f["L0_attn_bs"], np.float32)
    cp[:DST, CP_BI] = np.asarray(f["L0_attn_bi"], np.float32)
    for g in range(G):
        sl = slice(128 * g, 128 * (g + 1))
        cp[:, CP_BO + g] = np.asarray(f["L0_attn_bo"], np.float32)[sl]
        cp[:, CP_BG + g] = np.asarray(f["L0_attn_bg"], np.float32)[sl]
        cp[:, CP_LNW + g] = np.asarray(f["L0_attn_ln_w"], np.float32)[sl]
        cp[:, CP_LNB + g] = np.asarray(f["L0_attn_ln_b"], np.float32)[sl]
    for l in range(2):
        for g in range(G):
            sl = slice(128 * g, 128 * (g + 1))
            cp[:, CP_LG + 8 * l + g] = np.asarray(f["L0_ffn_ln_g"], np.float32)[l][sl]
            cp[:, CP_LB + 8 * l + g] = np.asarray(f["L0_ffn_ln_b"], np.float32)[l][sl]
        b1 = np.asarray(f["L0_ffn_b1"], np.float32)[l]
        for m in range(NW1):
            w = W1M[m]
            cp[:w, CP_B1 + NW1 * l + m] = b1[128 * m:128 * m + w]
            cp[:w, CP_B1G + NW1 * l + m] = b1[HID + 128 * m:HID + 128 * m + w]
        b2 = np.asarray(f["L0_ffn_b2"], np.float32)[l]
        for g in range(G):
            cp[:, CP_B2 + 8 * l + g] = b2[128 * g:128 * (g + 1)]
    cp[64, CP_SEL:CP_SEL + 64] = 1.0
    cp[65, CP_SEL + 64:CP_SEL + 128] = 1.0
    sgn = np.where(np.arange(128) % 64 < 32, -1.0, 1.0)
    cp[:, CP_SIGN] = sgn
    return cp


class Op:
    __slots__ = ("eng", "fn", "deps", "idx", "inc", "cum", "kind", "tag")

    def __init__(self, eng, fn, deps, idx, kind, tag):
        self.eng, self.fn, self.deps, self.idx = eng, fn, deps, idx
        self.inc, self.cum, self.kind, self.tag = 0, 0, kind, tag


class Em:
    def __init__(self):
        self.ops = []
        self.res = {}

    def op(self, eng, fn, r=(), w=(), dep=(), dma=False, tag="", force_inc=False):
        deps, seen = [], set()

        def add(o):
            if o is not None and id(o) not in seen:
                seen.add(id(o))
                deps.append(o)

        for res in r:
            e = self.res.get(res)
            if e and e[0] is not None:
                add(e[0])
        for res in w:
            e = self.res.get(res)
            if e:
                if e[0] is not None:
                    add(e[0])
                for rd in e[1]:
                    add(rd)
        for d in dep:
            add(d)
        o = Op(eng, fn, deps, len(self.ops), "d" if dma else "c", tag)
        if force_inc:
            o.inc = 1
        self.ops.append(o)
        for res in r:
            self.res.setdefault(res, [None, []])[1].append(o)
        for res in w:
            self.res[res] = [o, []]
        return o


ELIDE = True
STREAMS = [("tensor", "c"), ("vector", "c"), ("scalar", "c"),
           ("gpsimd", "c"), ("gpsimd", "d"), ("sync", "d")]


def emit_program(nc, em):
    for o in em.ops:
        if o.kind == "d":
            o.inc = 1
        for d in o.deps:
            if (d.eng, d.kind) != (o.eng, o.kind) or d.kind == "d":
                d.inc = 1
    cum = {s: 0 for s in STREAMS}
    for o in em.ops:
        s = (o.eng, o.kind)
        if o.inc:
            cum[s] += 16 if o.kind == "d" else 1
        o.cum = cum[s]
    totals = dict(cum)

    sem_cm = {s: nc.semaphore("sem_%s_%s" % s) for s in STREAMS}
    sem = {s: cm.__enter__() for s, cm in sem_cm.items()}

    know = {e: {s2: 0 for s2 in STREAMS} for e in
            ("tensor", "vector", "scalar", "gpsimd", "sync")}
    opclock = {}
    by_eng = {}
    for o in em.ops:
        by_eng.setdefault(o.eng, []).append(o)

    def run_engine(engname):
        def body(eng):
            kn = know[engname]
            for o in by_eng.get(engname, []):
                s_own = (o.eng, o.kind)
                for d in o.deps:
                    ds = (d.eng, d.kind)
                    if ds[0] == engname and ds[1] == "c":
                        continue
                    tgt = d.cum
                    if kn.get(ds, 0) >= tgt:
                        continue
                    eng.wait_ge(sem[ds], tgt)
                    kn[ds] = tgt
                    if ELIDE:
                        dk = opclock.get(id(d))
                        if dk:
                            for s2, v in dk.items():
                                if kn.get(s2, 0) < v:
                                    kn[s2] = v
                instr = o.fn(eng)
                if o.inc:
                    if instr is None:
                        raise RuntimeError("inc op returned no instr: " + o.tag)
                    instr.then_inc(sem[s_own], 16 if o.kind == "d" else 1)
                    snap = dict(kn)
                    snap[s_own] = o.cum
                    opclock[id(o)] = snap
        return body

    with nc.allow_low_precision(reason="bf16 operands intended"), \
            nc.Block() as block:
        block.tensor(run_engine("tensor"))
        block.vector(run_engine("vector"))
        block.scalar(run_engine("scalar"))
        block.gpsimd(run_engine("gpsimd"))

        def sync_body(eng):
            run_engine("sync")(eng)
            if totals[("gpsimd", "d")]:
                eng.wait_ge(sem[("gpsimd", "d")], totals[("gpsimd", "d")])
            if totals[("sync", "d")]:
                eng.wait_ge(sem[("sync", "d")], totals[("sync", "d")])
        block.sync(sync_body)

    for cm in sem_cm.values():
        cm.__exit__(None, None, None)


def build_nc(S, T, verbose=False, dbg=False):
    NT = S // T
    NP = T // 512
    NSLOT = (T + WIN) // 128
    KRING = T + WIN
    NJ = T // 128

    nc = bass.Bass("TRN2", target_bir_lowering=False)

    xT = nc.dram_tensor("xT", [D, S], F32, kind="ExternalInput").ap()
    wblob = nc.dram_tensor("wblob", [128, WCOLS], F32, kind="ExternalInput").ap()
    tabs = nc.dram_tensor("tabs", [128, 4 * S], F32, kind="ExternalInput").ap()
    cpat_in = nc.dram_tensor("cpin", [128, CPC], F32, kind="ExternalInput").ap()
    cmisc_in = nc.dram_tensor("cmin", [128, 400], BF16, kind="ExternalInput").ap()
    yout = nc.dram_tensor("y", [128, G * S], F32, kind="ExternalOutput").ap()
    wscr = nc.dram_tensor("wscr", [128, WCOLS], BF16, kind="Internal").ap()
    dbg_outs = {}

    AL = nc.alloc_sbuf_tensor

    SB_TOP = 229344
    ob_bytes = HEADS * T * 2
    OB_BASE = SB_TOP - ob_bytes
    arena_bytes = (G * T + 64) * 4
    ARENA_BASE = OB_BASE - arena_bytes
    arena = nc.alloc_sbuf_tensor_at("arena", [128, G * T + 64], F32,
                                    offset=ARENA_BASE).ap()
    xnar = AL("xnar", [128, G * (T + 1)], BF16).ap()
    xffn = AL("xffn", [128, G * (T + 1)], F32).ap()
    qk = AL("qk", [128, QG * T], BF16).ap()
    obf = nc.alloc_sbuf_tensor_at("obf", [128, HEADS * T], BF16,
                                  offset=OB_BASE).ap()
    shs = AL("shs", [128, 2560], F32).ap()
    expS = AL("expS", [128, 2 * 640], BF16).ap()
    knx = AL("knx", [128, 2 * 128], BF16).ap()
    tbuf = AL("tbuf", [128, 1024], F32).ap()
    WSLOT = 1408
    NWS = 4
    wring = AL("wring", [128, NWS * WSLOT], BF16).ap()
    sqt = AL("sqt", [128, 2 * 513], BF16).ap()
    uh = AL("uh", [128, T + 8], F32).ap()
    hbf = AL("hbf", [DST, T], BF16).ap()
    Hbuf = AL("Hbuf", [DST, 2 * (T + 1)], BF16).ap()
    scr = AL("scr", [128, 3 * 512], F32).ap()
    scrb = AL("scrb", [128, 2 * 512], BF16).ap()
    cpat = AL("cpat", [128, CPC], F32).ap()
    cmisc = AL("cmisc", [128, 400], BF16).ap()
    ones_f = AL("ones_f", [128, 128], F32).ap()
    ones_b = AL("ones_b", [128, 128], BF16).ap()
    blk2 = AL("blk2", [128, 2], BF16).ap()
    halos = AL("halos", [128, 32], F32).ap()
    dconst = AL("dconst", [128, 8], F32).ap()
    kpr = [AL("kpr%d" % l, [128, QG * KRING], BF16).ap() for l in range(2)]
    bvr = [AL("bvr%d" % l, [128, NSLOT * HEADS * 65], BF16).ap() for l in range(2)]

    h2p = nc.alloc_sbuf_tensor_at("h2p", [128, NW1 * 512], BF16,
                                  offset=ARENA_BASE).ap()
    x2 = nc.alloc_sbuf_tensor_at("x2b", [128, G * (T + 1)], BF16,
                                 offset=ARENA_BASE).ap()
    strips = nc.alloc_sbuf_tensor_at("strips", [128, 4 * T], F32,
                                     offset=OB_BASE).ap()
    recbf = AL("recbf", [128, 1024], BF16).ap()

    rem = nc.sbuf_bytes_remaining
    if verbose:
        print("SBUF bytes remaining:", rem)
    assert rem >= (SB_TOP - ARENA_BASE) + 512, "allocator near pinned slabs"

    psA = nc.alloc_psum_tensor("psA", [128, 1024], F32).ap()
    psB = nc.alloc_psum_tensor("psB", [128, 1024], F32).ap()
    psR = nc.alloc_psum_tensor("psR", [128, 1024], F32).ap()
    psX = nc.alloc_psum_tensor("psX", [128, 512], F32).ap()
    psS = nc.alloc_psum_tensor("psS", [128, 512], F32).ap()
    PRES = {"psA": 0, "psB": 2, "psR": 4, "psX": 6, "psS": 7}

    em = Em()

    def bres(name, lo, hi):
        base = PRES[name]
        return tuple("pb%d" % (base + b) for b in range(lo // 512, (hi + 511) // 512))

    def mm(e, out, lhsT, rhs, start, stop):
        return e.matmul(out, lhsT, rhs, start=start, stop=stop,
                        skip_group_check=True)

    def cpc(c, rows=128):
        return cpat[0:rows, c:c + 1]

    def xg(g, a, b):
        return arena[:, g * T + a:g * T + b]

    def xfd(g, a, b):
        base = g * (T + 1)
        return xffn[:, base + 1 + a:base + 1 + b]

    def xfs(g, a, b):
        base = g * (T + 1)
        o = 0 if g >= 4 else 1
        return xffn[:, base + o + a:base + o + b]

    def xnd(g, a, b):
        base = g * (T + 1)
        return xnar[:, base + 1 + a:base + 1 + b]

    def xns(g, a, b):
        base = g * (T + 1)
        o = 0 if g >= 4 else 1
        return xnar[:, base + o + a:base + o + b]

    def x2d(g, a, b):
        base = g * (T + 1)
        return x2[:, base + 1 + a:base + 1 + b]

    def x2s(g, a, b):
        base = g * (T + 1)
        o = 0 if g >= 4 else 1
        return x2[:, base + o + a:base + o + b]

    AR_ALL = ("arQ0", "arQ1", "arQ2", "arQ3")
    H2_RES = ("arQ0", "arQ1", "arQ2")
    X2_RES = ("arQ0", "arQ1")

    wslot = [0]
    first_touch = set()

    def wload(key, nslots=1):
        a, b = WPLAN[key]
        width = b - a
        slot = wslot[0] % NWS
        wslot[0] += 1
        dst = wring[:, slot * WSLOT:slot * WSLOT + width]
        res_slot = "wr%d" % slot
        res_reg = "wreg%d" % a
        if key not in first_touch:
            first_touch.add(key)
            ld = em.op("gpsimd", lambda e, d=dst, s=wblob[:, a:b]: e.dma_start(d, s),
                       w=(res_slot,), dma=True, tag="wc%s" % (key,))
            em.op("gpsimd", lambda e, d=wscr[:, a:b], s=dst: e.dma_start(d, s),
                  r=(res_slot,), w=(res_reg,), dma=True, tag="ws%s" % (key,))
        else:
            ld = em.op("sync", lambda e, d=dst, s=wscr[:, a:b]: e.dma_start(d, s),
                       r=(res_reg,), w=(res_slot,), dma=True, tag="wl%s" % (key,))
        return dst, ld, res_slot

    scan = {"emitted": T, "udeps": [None, None]}

    def u_col(t):
        return uh[0:DST, t:t + 1]

    def H_col(t):
        return Hbuf[0:DST, 2 * t:2 * t + 1]

    pump_gate = [0]

    def scan_pump(n=1):
        if n == 1:
            pump_gate[0] += 1
            if pump_gate[0] % 2:
                return
        for _ in range(n):
            t = scan["emitted"]
            if t >= T:
                return
            udep = scan["udeps"][min(t // 512, NP - 1)]
            if udep is None:
                return
            o1 = em.op("tensor",
                       lambda e, t=t: mm(e, psS[0:DST, 0:1],
                                         cmisc[0:DST, 384:400],
                                         H_col(t), True, True),
                       r=("H", "cmisc") + bres("psS", 0, 1),
                       w=bres("psS", 0, 1), dep=[udep], tag="smm%d" % t)
            em.op("scalar",
                  lambda e, t=t: e.activation(H_col(t + 1), psS[0:DST, 0:1],
                                              AF.Tanh, bias=u_col(t)),
                  r=bres("psS", 0, 1) + ("u",), w=("H",), dep=[o1],
                  tag="stanh%d" % t)
            scan["emitted"] = t + 1

    R64 = 64

    def ln_derive(dep_ops, a_dst, b_dst, tag):
        a_ap, a_col, a_res = a_dst
        b_ap, b_col, b_res = b_dst
        o_mean = em.op("scalar",
                       lambda e: e.mul(shs[R64:R64 + 1, 0:512],
                                       psA[R64:R64 + 1, 0:512], 1.0 / D),
                       r=bres("psA", 0, 512), w=("shsm",), dep=dep_ops,
                       tag=tag + "mean")
        o_m2 = em.op("scalar",
                     lambda e: e.mul(shs[R64:R64 + 1, 512:1024],
                                     psA[R64:R64 + 1, 512:1024], 1.0 / D),
                     r=bres("psA", 512, 1024), w=("shsm2",), dep=dep_ops,
                     tag=tag + "m2")
        o_mm = em.op("vector",
                     lambda e: e.tensor_mul(shs[R64:R64 + 1, 1024:1536],
                                            shs[R64:R64 + 1, 0:512],
                                            shs[R64:R64 + 1, 0:512]),
                     r=("shsm",), w=("shs1",), dep=[o_mean], tag=tag + "mm")
        o_var = em.op("vector",
                      lambda e: e.tensor_sub(shs[R64:R64 + 1, 1024:1536],
                                             shs[R64:R64 + 1, 512:1024],
                                             shs[R64:R64 + 1, 1024:1536]),
                      r=("shsm2", "shs1"), w=("shs1",), dep=[o_m2, o_mm],
                      tag=tag + "var")
        o_sq = em.op("scalar",
                     lambda e: e.activation(shs[R64:R64 + 1, 1024:1536],
                                            shs[R64:R64 + 1, 1024:1536],
                                            AF.Sqrt, bias=dconst[R64:R64 + 1, 1:2]),
                     r=("shs1", "dconst"), w=("shs1",), dep=[o_var],
                     tag=tag + "sqrt")
        o_a = em.op("vector",
                    lambda e: e.reciprocal(a_ap[R64:R64 + 1, a_col:a_col + 512],
                                           shs[R64:R64 + 1, 1024:1536]),
                    r=("shs1",), w=(a_res,), dep=[o_sq], tag=tag + "a")
        o_ma = em.op("vector",
                     lambda e: e.tensor_mul(shs[R64:R64 + 1, 1024:1536],
                                            shs[R64:R64 + 1, 0:512],
                                            a_ap[R64:R64 + 1, a_col:a_col + 512]),
                     r=("shsm", a_res), w=("shs1",), dep=[o_a], tag=tag + "ma")
        o_b = em.op("scalar",
                    lambda e: e.mul(b_ap[R64:R64 + 1, b_col:b_col + 512],
                                    shs[R64:R64 + 1, 1024:1536], -1.0),
                    r=("shs1",), w=(b_res,), dep=[o_ma], tag=tag + "b")
        return o_a, o_b, o_mean

    def bcast64(dst_name, lo, src_ap, src_col, src_res, dep, tag="bc"):
        ps = {"psA": psA, "psB": psB, "psR": psR, "psX": psX}[dst_name]
        o = em.op("tensor",
                  lambda e: mm(e, ps[0:128, lo:lo + 512],
                               ones_f[R64:R64 + 1, 0:128],
                               src_ap[R64:R64 + 1, src_col:src_col + 512],
                               True, True),
                  r=(src_res, "onesf") + bres(dst_name, lo, lo + 512),
                  w=bres(dst_name, lo, lo + 512), dep=dep, tag=tag)
        return o

    def dump(name, ap, res, dep=()):
        if not dbg:
            return
        shp = list(ap.shape)
        dt = ap.dtype
        t = nc.dram_tensor("dbg_" + name, shp, dt, kind="ExternalOutput").ap()
        dbg_outs[name] = True
        em.op("gpsimd", lambda e, t=t, ap=ap: e.dma_start(t, ap),
              r=res, dma=True, force_inc=True, tag="dump" + name)

    em.op("sync", lambda e: e.dma_start(cpat, cpat_in), w=("cpat",), dma=True)
    em.op("sync", lambda e: e.dma_start(cmisc, cmisc_in), w=("cmisc",), dma=True)
    em.op("vector", lambda e: e.memset(ones_f, 1.0), w=("onesf",))
    em.op("vector", lambda e: e.memset(ones_b, 1.0), w=("onesb",))
    em.op("vector", lambda e: e.memset(blk2[0:64, 0:1], 1.0), w=("blk2",))
    em.op("vector", lambda e: e.memset(blk2[64:128, 0:1], 0.0), w=("blk2",))
    em.op("vector", lambda e: e.memset(blk2[0:64, 1:2], 0.0), w=("blk2",))
    em.op("vector", lambda e: e.memset(blk2[64:128, 1:2], 1.0), w=("blk2",))
    em.op("vector", lambda e: e.memset(halos, 0.0), w=("hB", "hC", "hD"))
    em.op("vector", lambda e: e.memset(H_col(0), 0.0), w=("H",))
    em.op("vector", lambda e: e.memset(dconst[:, 1:2], LNEPS), w=("dconst",))
    em.op("vector", lambda e: e.memset(dconst[:, 2:3], QKSCALE), w=("dconst",))
    em.op("vector",
          lambda e: e.tensor_add(dconst[0:DST, 0:1], cpat[0:DST, CP_BS:CP_BS + 1],
                                 cpat[0:DST, CP_BI:CP_BI + 1]),
          r=("cpat",), w=("dconst",))
    for _i, (_l, _k) in enumerate([(0, 1), (0, 3), (1, 1), (1, 3)]):
        em.op("vector",
              lambda e, _i=_i, _l=_l, _k=_k: e.tensor_mul(
                  dconst[:, 3 + _i:4 + _i], cpat[:, CP_QS[(_l, _k)]:CP_QS[(_l, _k)] + 1],
                  cpat[:, CP_SIGN:CP_SIGN + 1]),
              r=("cpat",), w=("dconst",), tag="sgn")
    for l in range(2):
        em.op("vector",
              lambda e, l=l: e.memset(
                  bvr[l].rearrange("p (s h c) -> p s h c", h=HEADS, c=65)[:, :, :, 64:65],
                  1.0),
              w=tuple("bv%d_%d" % (l, s) for s in range(NSLOT)))
    for g in range(G):
        em.op("vector", lambda e, g=g: e.memset(
            xnar[:, g * (T + 1):g * (T + 1) + 1], 0.0), w=("xa%d" % g,))
        em.op("vector", lambda e, g=g: e.memset(
            x2[:, g * (T + 1):g * (T + 1) + 1], 0.0), w=X2_RES)

    for c in range(NT):
        t0 = c * T
        scan["emitted"] = 0
        scan["udeps"] = [None, None]

        xin = em.op("sync",
                    lambda e, t0=t0: e.dma_start(
                        arena[:, 0:G * T].rearrange("p (g t) -> p g t", g=G),
                        xT[:, t0:t0 + T].rearrange("(g p) t -> p g t", p=128)),
                    w=AR_ALL, dma=True, tag="xload%d" % c)

        if c > 0:
            em.op("vector",
                  lambda e: e.tensor_copy(
                      xnar.rearrange("p (g t) -> p g t", g=G + 0, t=T + 1)[:, :, 0:1],
                      halos[:, 24:32].rearrange("p (g o) -> p g o", o=1)),
                  r=("hA",), w=tuple("xa%d" % g for g in range(G)),
                  tag="xnh%d" % c)
            em.op("vector", lambda e: e.tensor_copy(H_col(0), H_col(T)),
                  w=("H",), tag="hcar%d" % c)

        for g in range(G):
            em.op("scalar", lambda e, g=g: e.copy(xnd(g, 0, T), xg(g, 0, T)),
                  r=("arQ%d" % (g // 2),), w=("xa%d" % g,), dep=[xin],
                  tag="xbf%d" % g)

        ab1 = []
        for p in range(NP):
            a = p * 512
            prev = []
            for g in range(G):
                sl = g % 2
                o_sq = em.op("gpsimd",
                             lambda e, g=g, a=a, sl=sl:
                             e.tensor_mul(scrb[:, sl * 512:sl * 512 + 512],
                                          xnd(g, a, a + 512), xnd(g, a, a + 512)),
                             r=("xa%d" % g,), w=("scrb%d" % sl,), tag="sq")
                o1 = em.op("tensor",
                           lambda e, g=g, a=a, first=(g == 0):
                           mm(e, psA[R64:R64 + 1, 0:512], ones_b[:, 0:1],
                              xnd(g, a, a + 512), first, g == G - 1),
                           r=("xa%d" % g, "onesb") + (bres("psA", 0, 512) if g == 0 else ()),
                           w=bres("psA", 0, 512) if g == 0 else (),
                           dep=prev, tag="sm")
                prev = [o1]
                o2 = em.op("tensor",
                           lambda e, sl=sl, first=(g == 0), g=g:
                           mm(e, psA[R64:R64 + 1, 512:1024], ones_b[:, 0:1],
                              scrb[:, sl * 512:sl * 512 + 512], first, g == G - 1),
                           r=("scrb%d" % sl, "onesb") + (bres("psA", 512, 1024) if g == 0 else ()),
                           w=bres("psA", 512, 1024) if g == 0 else (),
                           dep=[o_sq] + prev, tag="sms")
                prev = [o2]
            ab1.append(ln_derive(prev, (strips, a, "stA"),
                                 (strips, T + a, "stB"), "ln1_%d_" % p))

        xn_done = []
        for p in range(NP):
            a = p * 512
            o_a, o_b, _ = ab1[p]
            bc_a = bcast64("psX", 0, strips, a, "stA", [o_a], tag="bca")
            bc_b = bcast64("psR", 0, strips, T + a, "stB", [o_b], tag="bcb")
            for g in range(G):
                sl = g % 2
                o1 = em.op("vector",
                           lambda e, g=g, a=a, sl=sl:
                           e.tensor_mul(scr[:, sl * 512:sl * 512 + 512],
                                        xg(g, a, a + 512), psX[0:128, 0:512]),
                           r=("arQ%d" % (g // 2),) + bres("psX", 0, 512),
                           w=("scr%d" % sl,), dep=[bc_a, xin], tag="xn1")
                o2 = em.op("vector",
                           lambda e, g=g, a=a, sl=sl:
                           e.tensor_add(xfd(g, a, a + 512),
                                        scr[:, sl * 512:sl * 512 + 512],
                                        psR[0:128, 0:512]),
                           r=("scr%d" % sl,) + bres("psR", 0, 512),
                           w=("xf%d" % g,), dep=[bc_b, o1], tag="xn2")
                o3 = em.op("scalar",
                           lambda e, g=g, a=a: e.copy(xnd(g, a, a + 512),
                                                      xfd(g, a, a + 512)),
                           r=("xf%d" % g,), w=("xa%d" % g,), dep=[o2], tag="xnc")
                xn_done.append(o3)

        if c == 0:
            dump("xar", arena, AR_ALL)
            dump("xnbf", xnar, tuple("xa%d" % g for g in range(G)))
            dump("xffn0", xffn, tuple("xf%d" % g for g in range(G)))
            dump("strips", strips, ("stA", "stB"))

        em.op("vector",
              lambda e: e.tensor_copy(
                  halos[:, 24:32].rearrange("p (g o) -> p g o", o=1),
                  xnar.rearrange("p (g t) -> p g t", g=G)[:, :, T:T + 1]),
              r=tuple("xa%d" % g for g in range(G)), w=("hA",),
              dep=xn_done[-1:], tag="hsA")

        ab2 = []
        for p in range(NP):
            a = p * 512
            prev = []
            for g in range(G):
                sl = g % 2
                o_sq = em.op("gpsimd",
                             lambda e, g=g, a=a, sl=sl:
                             e.tensor_mul(scrb[:, sl * 512:sl * 512 + 512],
                                          xnd(g, a, a + 512), xnd(g, a, a + 512)),
                             r=("xa%d" % g,), w=("scrb%d" % sl,), tag="sq2")
                o1 = em.op("tensor",
                           lambda e, g=g, a=a, first=(g == 0):
                           mm(e, psA[R64:R64 + 1, 0:512], ones_b[:, 0:1],
                              xnd(g, a, a + 512), first, g == G - 1),
                           r=("xa%d" % g, "onesb") + (bres("psA", 0, 512) if g == 0 else ()),
                           w=bres("psA", 0, 512) if g == 0 else (),
                           dep=prev, tag="s2")
                prev = [o1]
                o2 = em.op("tensor",
                           lambda e, sl=sl, first=(g == 0), g=g:
                           mm(e, psA[R64:R64 + 1, 512:1024], ones_b[:, 0:1],
                              scrb[:, sl * 512:sl * 512 + 512], first, g == G - 1),
                           r=("scrb%d" % sl, "onesb") + (bres("psA", 512, 1024) if g == 0 else ()),
                           w=bres("psA", 512, 1024) if g == 0 else (),
                           dep=[o_sq] + prev, tag="ss2")
                prev = [o2]
            o_r2, o_nm2r2, _ = ln_derive(prev, (strips, 2 * T + a, "stA2"),
                                         (strips, 3 * T + a, "stB2"),
                                         "ln2_%d_" % p)
            o_a, o_b, _ = ab1[p]
            o_br = em.op("vector",
                         lambda e, a=a: e.tensor_mul(
                             shs[R64:R64 + 1, 1024:1536],
                             strips[R64:R64 + 1, T + a:T + a + 512],
                             strips[R64:R64 + 1, 2 * T + a:2 * T + a + 512]),
                         r=("stA2", "stB", "shs1"), w=("shs1",),
                         dep=[o_r2, o_b], tag="br")
            o_A2 = em.op("vector",
                         lambda e, a=a: e.tensor_mul(
                             strips[R64:R64 + 1, 2 * T + a:2 * T + a + 512],
                             strips[R64:R64 + 1, 2 * T + a:2 * T + a + 512],
                             strips[R64:R64 + 1, a:a + 512]),
                         r=("stA2", "stA"), w=("stA2",),
                         dep=[o_br, o_a], tag="A2")
            o_B2 = em.op("vector",
                         lambda e, a=a: e.tensor_add(
                             strips[R64:R64 + 1, 3 * T + a:3 * T + a + 512],
                             strips[R64:R64 + 1, 3 * T + a:3 * T + a + 512],
                             shs[R64:R64 + 1, 1024:1536]),
                         r=("stB2", "shs1"), w=("stB2",),
                         dep=[o_nm2r2, o_br], tag="B2")
            ab2.append((o_A2, o_B2))

        wu, wu_ld, wu_res = wload(("u",))
        for p in range(NP):
            a = p * 512
            o_A2, o_B2 = ab2[p]
            bcA = bcast64("psX", 0, strips, 2 * T + a, "stA2", [o_A2], tag="bcA2")
            bcB = bcast64("psR", 0, strips, 3 * T + a, "stB2", [o_B2], tag="bcB2")
            prev_mm = []
            for g in range(G):
                sl = g % 2
                o1 = em.op("vector",
                           lambda e, g=g, a=a, sl=sl:
                           e.tensor_mul(scr[:, sl * 512:sl * 512 + 512],
                                        xg(g, a, a + 512), psX[0:128, 0:512]),
                           r=("arQ%d" % (g // 2),) + bres("psX", 0, 512),
                           w=("scr%d" % sl,), dep=[bcA, xin], tag="u1")
                o2 = em.op("vector",
                           lambda e, g=g, sl=sl:
                           e.tensor_add(scr[:, sl * 512:sl * 512 + 512],
                                        scr[:, sl * 512:sl * 512 + 512],
                                        psR[0:128, 0:512]),
                           r=("scr%d" % sl,) + bres("psR", 0, 512),
                           w=("scr%d" % sl,), dep=[bcB, o1], tag="u2")
                o3 = em.op("scalar",
                           lambda e, g=g, sl=sl:
                           e.activation(scrb[:, sl * 512:sl * 512 + 512],
                                        scr[:, sl * 512:sl * 512 + 512],
                                        AF.Identity, bias=cpc(CP_LNB + g),
                                        scale=cpc(CP_LNW + g)),
                           r=("scr%d" % sl, "cpat"), w=("scrb%d" % sl,),
                           dep=[o2], tag="u3")
                m1 = em.op("tensor",
                           lambda e, g=g, sl=sl, first=(g == 0), wu=wu:
                           mm(e, psB[0:DST, 0:512], wu[:, g * DST:(g + 1) * DST],
                              scrb[:, sl * 512:sl * 512 + 512], first, False),
                           r=("scrb%d" % sl, wu_res) + (bres("psB", 0, 512) if g == 0 else ()),
                           w=bres("psB", 0, 512) if g == 0 else (),
                           dep=[o3, wu_ld] + prev_mm, tag="umm")
                prev_mm = [m1]
                m2 = em.op("tensor",
                           lambda e, g=g, sl=sl, wu=wu:
                           mm(e, psB[0:DST, 0:512],
                              wu[:, G * DST + g * DST:G * DST + (g + 1) * DST],
                              scrb[:, sl * 512:sl * 512 + 512], False, g == G - 1),
                           r=("scrb%d" % sl, wu_res),
                           dep=[o3, wu_ld] + prev_mm, tag="umm2")
                prev_mm = [m2]
            uo = em.op("scalar",
                       lambda e, a=a: e.activation(uh[0:DST, a:a + 512],
                                                   psB[0:DST, 0:512],
                                                   AF.Identity,
                                                   bias=dconst[0:DST, 0:1]),
                       r=bres("psB", 0, 512) + ("dconst",), w=("u",),
                       dep=prev_mm, tag="uev%d" % p)
            scan["udeps"][p] = uo

        if c == 0:
            dump("u", uh, ("u",), dep=[x for x in scan["udeps"] if x])

        for l in range(2):
            lin_s = xns if l == 0 else x2s
            lres = (lambda g: "xa%d" % g) if l == 0 else (lambda g, _r=X2_RES: _r)

            def lrtup(g):
                r = lres(g)
                return (r,) if isinstance(r, str) else tuple(r)

            def qkv_m(m, pump=False):
                wt, wld, wres = wload(("qkv", l, m))
                ps, pn = (psA, "psA") if m % 2 == 0 else (psB, "psB")
                prev = []
                for k in range(G):
                    if pump and k in (3, 6):
                        ilq_pump(1)
                    for cc2 in range(NP):
                        scan_pump(1)
                        prev.append(em.op(
                            "tensor",
                            lambda e, k=k, cc2=cc2, ps=ps, wt=wt, lin_s=lin_s:
                            mm(e, ps[:, cc2 * 512:cc2 * 512 + 512],
                               wt[:, k * 128:(k + 1) * 128],
                               lin_s(k, cc2 * 512, cc2 * 512 + 512),
                               k == 0, k == G - 1),
                            r=lrtup(k) + (wres,) + (bres(pn, 0, 1024) if (k == 0 and cc2 == 0) else ()),
                            w=bres(pn, 0, 1024) if (k == 0 and cc2 == 0) else (),
                            dep=[wld] + prev[-1:], tag="qkv"))
                q = m % 4
                em.op("scalar",
                      lambda e, q=q, ps=ps: e.copy(qk[:, q * T:q * T + T],
                                                   ps[:, 0:T]),
                      r=bres(pn, 0, 1024), w=("qk%d" % q,), dep=prev[-1:],
                      tag="qkev")

            def rope_swap(q):
                for (d0, d1, s0, s1) in ((0, 32, 32, 64), (32, 64, 0, 32),
                                         (64, 96, 96, 128), (96, 128, 64, 96)):
                    em.op("sync",
                          lambda e, q=q, d0=d0, d1=d1, s0=s0, s1=s1:
                          e.dma_start(scrb[d0:d1, 0:T],
                                      qk[s0:s1, q * T:q * T + T]),
                          r=("qk%d" % q,), w=("scrb0", "scrb1"), dma=True,
                          tag="swdma")

            def rope_pre(is_k, q, p):
                cS = CP_QS[(l, 2 if is_k else 0)]
                sgn_col = 3 + 2 * l + (1 if is_k else 0)
                tabbase = (2 * S if is_k else 0)
                a = p * 512
                st = {}
                st["tl"] = em.op("sync",
                           lambda e, a=a, tabbase=tabbase, t0=t0:
                           e.dma_start(
                               tbuf.rearrange("p (u t) -> p u t", u=2),
                               tabs.rearrange("p (u s) -> p u s", u=4)
                               [:, (tabbase // S):(tabbase // S) + 2,
                                t0 + a:t0 + a + 512]),
                           w=("tbuf",), dma=True, tag="tab")
                o_sq = em.op("gpsimd",
                             lambda e, q=q, a=a:
                             e.tensor_mul(sqt[:, 0:512],
                                          qk[:, q * T + a:q * T + a + 512],
                                          qk[:, q * T + a:q * T + a + 512]),
                             r=("qk%d" % q,), w=("sqtA",), tag="qsq")
                o_ss = em.op("tensor",
                             lambda e: mm(e, psX[R64:R64 + 2, 0:512],
                                          blk2, sqt[:, 0:512], True, True),
                             r=("sqtA", "blk2") + bres("psX", 0, 512),
                             w=bres("psX", 0, 512), dep=[o_sq], tag="ssq")
                o_nrm = em.op("scalar",
                              lambda e: e.activation(shs[R64:R64 + 2, 0:512],
                                                     psX[R64:R64 + 2, 0:512],
                                                     AF.Sqrt),
                              r=bres("psX", 0, 512), w=("shsm",),
                              dep=[o_ss], tag="nsqrt")
                o_mx = em.op("vector",
                             lambda e: e.tensor_scalar_max(
                                 shs[R64:R64 + 2, 0:512],
                                 shs[R64:R64 + 2, 0:512], 1e-12),
                             r=("shsm",), w=("shsm",), dep=[o_nrm])
                st["rc"] = em.op("vector",
                              lambda e: e.reciprocal(shs[R64:R64 + 2, 0:512],
                                                     shs[R64:R64 + 2, 0:512]),
                              r=("shsm",), w=("shsm",), dep=[o_mx], tag="nrc")
                st["args"] = (is_k, q, p, cS, sgn_col)
                return st

            def rope_post(st):
                is_k, q, p, cS, sgn_col = st["args"]
                a = p * 512
                tl = st["tl"]
                o_fbc = em.op("tensor",
                              lambda e: mm(e, psX[0:128, 0:512],
                                           cpat[R64:R64 + 2, CP_SEL:CP_SEL + 128],
                                           shs[R64:R64 + 2, 0:512],
                                           True, True),
                              r=("shsm", "cpat") + bres("psX", 0, 512),
                              w=bres("psX", 0, 512), dep=[st["rc"]], tag="fbc")
                o_t1 = em.op("vector",
                             lambda e, q=q, a=a, cS=cS:
                             e.scalar_tensor_tensor(
                                 scr[:, 0:512],
                                 qk[:, q * T + a:q * T + a + 512],
                                 cpc(cS), tbuf[:, 0:512],
                                 ALU.mult, ALU.mult),
                             r=("qk%d" % q, "tbuf", "cpat"),
                             w=("scr0",), dep=[tl], tag="t1")
                o_t2 = em.op("vector",
                             lambda e, a=a, sgn_col=sgn_col:
                             e.scalar_tensor_tensor(
                                 scr[:, 512:1024],
                                 scrb[:, a:a + 512],
                                 dconst[:, sgn_col:sgn_col + 1],
                                 tbuf[:, 512:1024],
                                 ALU.mult, ALU.mult),
                             r=("scrb0", "scrb1", "dconst", "tbuf"),
                             w=("scr1",), dep=[tl], tag="t2")
                o_pre = em.op("vector",
                              lambda e: e.tensor_add(scr[:, 0:512],
                                                     scr[:, 0:512],
                                                     scr[:, 512:1024]),
                              r=("scr0", "scr1"), w=("scr0",),
                              dep=[o_t1, o_t2], tag="pre")
                if is_k:
                    rc0 = (t0 + a) % KRING
                    dst = kpr[l][:, q * KRING + rc0:q * KRING + rc0 + 512]
                    wres_k = tuple(
                        "kp%d_%d" % (l, ((t0 + a) // 128 + i) % NSLOT)
                        for i in range(4))
                    em.op("vector",
                          lambda e, dst=dst: e.tensor_mul(
                              dst, scr[:, 0:512], psX[0:128, 0:512]),
                          r=("scr0",) + bres("psX", 0, 512),
                          w=wres_k, dep=[o_pre, o_fbc], tag="kout")
                else:
                    em.op("vector",
                          lambda e, q=q, a=a: e.tensor_mul(
                              qk[:, q * T + a:q * T + a + 512],
                              scr[:, 0:512], psX[0:128, 0:512]),
                          r=("scr0",) + bres("psX", 0, 512),
                          w=("qk%d" % q,), dep=[o_pre, o_fbc], tag="qout")

            ilq = []

            def ilq_pump(n=1):
                for _ in range(n):
                    if ilq:
                        ilq.pop(0)()

            def ilq_drain():
                while ilq:
                    ilq.pop(0)()

            def enqueue_rope(is_k, q):
                def pre0(q=q, is_k=is_k):
                    rope_swap(q)
                    enqueue_rope.state[(is_k, q, 0)] = rope_pre(is_k, q, 0)
                def post0(q=q, is_k=is_k):
                    rope_post(enqueue_rope.state.pop((is_k, q, 0)))
                def pre1(q=q, is_k=is_k):
                    enqueue_rope.state[(is_k, q, 1)] = rope_pre(is_k, q, 1)
                def post1(q=q, is_k=is_k):
                    rope_post(enqueue_rope.state.pop((is_k, q, 1)))
                ilq.extend([pre0, post0, pre1, post1])
            enqueue_rope.state = {}

            def v_tb(tb, wv):
                slot = (t0 // 128 + tb) % NSLOT
                ps, pn = (psA, "psA") if tb % 2 == 0 else (psB, "psB")
                prev = []
                for k in range(G):
                    scan_pump(1)
                    wtv, wldv, wresv = wv[k // 2]
                    prev.append(em.op(
                        "tensor",
                        lambda e, k=k, tb=tb, ps=ps, wtv=wtv, lin_s=lin_s:
                        mm(e, ps[:, 0:512],
                           lin_s(k, tb * 128, tb * 128 + 128),
                           wtv[:, (k % 2) * 512:(k % 2) * 512 + 512],
                           k == 0, k == G - 1),
                        r=lrtup(k) + (wresv,) + (bres(pn, 0, 512) if k == 0 else ()),
                        w=bres(pn, 0, 512) if k == 0 else (),
                        dep=[wldv] + prev[-1:], tag="vmm"))
                em.op("vector",
                      lambda e, slot=slot, ps=ps, l=l:
                      e.tensor_copy(
                          bvr[l][:, slot * HEADS * 65:(slot + 1) * HEADS * 65]
                          .rearrange("p (h z) -> p h z", z=65)[:, :, 0:64],
                          ps[:, 0:512].rearrange("p (h z) -> p h z", z=64)),
                      r=bres(pn, 0, 512), w=("bv%d_%d" % (l, slot),),
                      dep=prev[-1:], tag="vev")

            for m in range(4, 8):
                qkv_m(m)
            for q in range(QG):
                enqueue_rope(True, q)
            wv = [wload(("v", l, h4)) for h4 in range(4)]
            for tb in range(T // 128):
                v_tb(tb, wv)
                ilq_pump(1)
            for m in range(0, 4):
                qkv_m(m, pump=True)
                if m >= 1:
                    enqueue_rope(False, m - 1)
                ilq_pump(1)
            enqueue_rope(False, 3)

            Jlo = max(0, t0 // 128 - 4)
            Jhi = (t0 + T) // 128 - 1
            pend_drain = []

            def attn_head(h):
                hp = h % 2
                q = h // 2
                av_last = []
                av_virgin = {4: True, 5: True}
                pend_av = []
                for J in range(Jlo, Jhi + 1):
                    jp = J % 2
                    simps, simn = (psA, "psA") if jp == 0 else (psB, "psB")
                    r = J % 4
                    sn = 512 - 128 * r
                    a = max(128 * J, t0) - 128 * J
                    b = min(128 * J + 640, t0 + T) - 128 * J
                    slot = J % NSLOT
                    rc0 = (128 * J) % KRING
                    kres = "kp%d_%d" % (l, slot)
                    need_next = b > sn
                    if need_next:
                        o_knx = em.op("vector",
                                      lambda e, q=q, rc0=rc0, jp=jp, l=l:
                                      e.tensor_scalar_mul(
                                          knx[:, jp * 128:jp * 128 + 128],
                                          kpr[l][:, q * KRING + rc0:q * KRING + rc0 + 128],
                                          cpc(CP_BSQ)),
                                      r=(kres, "cpat"), w=("knx%d" % jp,),
                                      tag="knx")
                    cuts = sorted({a, b, min(max(a, sn), b), min(max(a, 512), b)})
                    sim_ops = []
                    for ci in range(len(cuts) - 1):
                        p0, p1 = cuts[ci], cuts[ci + 1]
                        if p1 <= p0:
                            continue
                        use_next = p0 >= sn
                        scan_pump(1)
                        if use_next:
                            lhs = knx[64 * hp:64 * hp + 64, jp * 128:jp * 128 + 128]
                            lr = ("knx%d" % jp,)
                            ldep = [o_knx]
                        else:
                            lhs = kpr[l][64 * hp:64 * hp + 64,
                                         q * KRING + rc0:q * KRING + rc0 + 128]
                            lr = (kres,)
                            ldep = []
                        sim_ops.append(em.op(
                            "tensor",
                            lambda e, p0=p0, p1=p1, lhs=lhs, simps=simps, J=J, q=q,
                            h=h, t0=t0:
                            mm(e, simps[:, p0:p1], lhs,
                               qk[64 * (h % 2):64 * (h % 2) + 64,
                                  q * T + 128 * J + p0 - t0:q * T + 128 * J + p1 - t0],
                               True, True),
                            r=lr + ("qk%d" % q,) + bres(simn, p0, p1),
                            w=bres(simn, p0, p1), dep=ldep, tag="sim"))
                    o_exp = em.op("scalar",
                                  lambda e, a=a, b=b, jp=jp, simps=simps:
                                  e.activation(expS[:, jp * 640 + a:jp * 640 + b],
                                               simps[:, a:b], AF.Exp,
                                               scale=QKSCALE),
                                  r=bres(simn, a, b), w=("ex%d" % jp,),
                                  dep=sim_ops, tag="exp")
                    mdeps = [o_exp]
                    mask_ranges = []
                    u0, u1 = a, min(b, 128)
                    if u1 > u0:
                        mask_ranges.append((u0, u1))
                        mdeps.append(em.op(
                            "gpsimd",
                            lambda e, u0=u0, u1=u1, jp=jp:
                            e.tensor_mul(expS[:, jp * 640 + u0:jp * 640 + u1],
                                         expS[:, jp * 640 + u0:jp * 640 + u1],
                                         cmisc[:, 128 + u0:128 + u1]),
                            r=("ex%d" % jp, "cmisc"), w=("ex%d" % jp,),
                            dep=[o_exp], tag="mU"))
                    l0, l1 = max(a, 512), b
                    if l1 > l0:
                        mask_ranges.append((l0, l1))
                        mdeps.append(em.op(
                            "gpsimd",
                            lambda e, l0=l0, l1=l1, jp=jp:
                            e.tensor_mul(expS[:, jp * 640 + l0:jp * 640 + l1],
                                         expS[:, jp * 640 + l0:jp * 640 + l1],
                                         cmisc[:, 256 + l0 - 512:256 + l1 - 512]),
                            r=("ex%d" % jp, "cmisc"), w=("ex%d" % jp,),
                            dep=[o_exp], tag="mL"))
                    if dbg and c == 0 and l == 0 and h == 0 and J in (0, 1):
                        dump("expS%d" % J, expS, ("ex%d" % jp,), )

                    ilo = 128 * J + a
                    ihi = 128 * J + b
                    newlo = 128 * J + 512
                    avcuts = sorted({ilo, ihi, min(max(ilo, newlo), ihi),
                                     min(max(ilo, t0 + 512), ihi)})
                    av_emits = []
                    for ci in range(len(avcuts) - 1):
                        i0, i1 = avcuts[ci], avcuts[ci + 1]
                        if i1 <= i0:
                            continue
                        bank = 4 + (i0 - t0) // 512
                        st = av_virgin[bank]
                        av_virgin[bank] = False
                        scan_pump(1)
                        c0m, c1m = i0 - 128 * J, i1 - 128 * J
                        if any(c1m > m0 and c0m < m1 for m0, m1 in mask_ranges):
                            pdeps = list(mdeps)
                        else:
                            pdeps = [o_exp]
                        av_emits.append((i0, i1, st, jp, J, slot, pdeps))
                    if J == Jlo + 1 and pend_drain:
                        pend_drain.pop(0)()
                    if (J - Jlo) % 3 == 2:
                        ilq_pump(1)
                    pend_av.append(av_emits)
                    if len(pend_av) > globals().get("AVLAG", 1):
                        for i0, i1, st, jp, Jx, slot, md in pend_av.pop(0):
                            av_last.append(em.op(
                                "tensor",
                                lambda e, i0=i0, i1=i1, st=st, jp=jp, Jx=Jx,
                                slot=slot, l=l, h=h, t0=t0:
                                mm(e, psR[0:65, i0 - t0:i1 - t0],
                                   bvr[l][:, slot * HEADS * 65 + h * 65:
                                          slot * HEADS * 65 + h * 65 + 65],
                                   expS[:, jp * 640 + i0 - 128 * Jx:jp * 640 + i1 - 128 * Jx],
                                   st, True),
                                r=("bv%d_%d" % (l, slot), "ex%d" % jp)
                                + bres("psR", i0 - t0, i1 - t0),
                                w=bres("psR", i0 - t0, i1 - t0),
                                dep=md, tag="av"))
                for i0, i1, st, jp, Jx, slot, md in (pend_av.pop(0) if pend_av else []):
                    av_last.append(em.op(
                        "tensor",
                        lambda e, i0=i0, i1=i1, st=st, jp=jp, Jx=Jx,
                        slot=slot, l=l, h=h, t0=t0:
                        mm(e, psR[0:65, i0 - t0:i1 - t0],
                           bvr[l][:, slot * HEADS * 65 + h * 65:
                                  slot * HEADS * 65 + h * 65 + 65],
                           expS[:, jp * 640 + i0 - 128 * Jx:jp * 640 + i1 - 128 * Jx],
                           st, True),
                        r=("bv%d_%d" % (l, slot), "ex%d" % jp)
                        + bres("psR", i0 - t0, i1 - t0),
                        w=bres("psR", i0 - t0, i1 - t0),
                        dep=md, tag="av"))

                def emit_drain(h=h, av_last=av_last):
                    o_rc = em.op("vector",
                                 lambda e: e.reciprocal(recbf[R64:R64 + 1, 0:T],
                                                        psR[R64:R64 + 1, 0:T]),
                                 r=bres("psR", 0, T), w=("rec",), dep=av_last,
                                 tag="drec")
                    for p in range(NP):
                        a = p * 512
                        o_bc = em.op("tensor",
                                     lambda e, a=a: mm(e, psX[0:64, 0:512],
                                                       ones_b[R64:R64 + 1, 0:64],
                                                       recbf[R64:R64 + 1, a:a + 512],
                                                       True, True),
                                     r=("rec", "onesb") + bres("psX", 0, 512),
                                     w=bres("psX", 0, 512), dep=[o_rc], tag="dbc")
                        o_cf = em.op("scalar",
                                     lambda e, a=a: e.copy(shs[0:64, a:a + 512],
                                                           psX[0:64, 0:512]),
                                     r=bres("psX", 0, 512), w=("fbc%d" % p,),
                                     dep=[o_bc], tag="dcf")
                        em.op("vector",
                              lambda e, a=a, h=h: e.tensor_mul(
                                  obf[0:64, h * T + a:h * T + a + 512],
                                  psR[0:64, a:a + 512], shs[0:64, a:a + 512]),
                              r=bres("psR", a, a + 512) + ("fbc%d" % p,),
                              w=("ob%d" % h,), dep=[o_cf] + av_last, tag="dmul")
                pend_drain.append(emit_drain)

            for q in range(QG):
                while len(ilq) > 4 * (QG - 1 - q):
                    ilq_pump(1)
                attn_head(2 * q)
                attn_head(2 * q + 1)

            if c == 0:
                dump("kpr%d" % l, kpr[l],
                     tuple("kp%d_%d" % (l, s) for s in range(NSLOT)))
                dump("qp%d" % l, qk, tuple("qk%d" % i for i in range(4)))
                dump("bvr%d" % l, bvr[l],
                     tuple("bv%d_%d" % (l, s) for s in range(NSLOT)))
                dump("obf%d" % l, obf, tuple("ob%d" % h for h in range(HEADS)))

            while pend_drain:
                pend_drain.pop(0)()

            for m in range(8):
                wt, wld, wres = wload(("wo", l, m))
                for cc2 in range(NP):
                    u2 = m * NP + cc2
                    ps, pn = (psA, "psA") if u2 % 2 == 0 else (psB, "psB")
                    po3 = ((u2 // 2) % 2) * 512
                    prev = []
                    for h in range(HEADS):
                        scan_pump(1)
                        prev.append(em.op(
                            "tensor",
                            lambda e, h=h, cc2=cc2, ps=ps, wt=wt, po3=po3:
                            mm(e, ps[:, po3:po3 + 512], wt[0:64, h * 128:(h + 1) * 128],
                               obf[0:64, h * T + cc2 * 512:h * T + cc2 * 512 + 512],
                               h == 0, h == HEADS - 1),
                            r=("ob%d" % h, wres) + (bres(pn, po3, po3 + 512) if h == 0 else ()),
                            w=bres(pn, po3, po3 + 512) if h == 0 else (),
                            dep=[wld] + prev[-1:], tag="womm"))
                    em.op("vector",
                          lambda e, m=m, cc2=cc2, ps=ps, po3=po3:
                          e.tensor_add(xfd(m, cc2 * 512, cc2 * 512 + 512),
                                       ps[:, po3:po3 + 512],
                                       xfd(m, cc2 * 512, cc2 * 512 + 512)),
                          r=bres(pn, po3, po3 + 512) + ("xf%d" % m,), w=("xf%d" % m,),
                          dep=prev[-1:], tag="woev")

            if c == 0:
                dump("xwo%d" % l, xffn, tuple("xf%d" % g for g in range(G)))

            hres = "hB" if l == 0 else "hD"
            hcol = 0 if l == 0 else 16
            o_hr = em.op("vector",
                         lambda e, hcol=hcol: e.tensor_copy(
                             xffn.rearrange("p (g t) -> p g t", g=G)[:, :, 0:1],
                             halos[:, hcol:hcol + 8].rearrange("p (g o) -> p g o", o=1)),
                         r=(hres,), w=tuple("xf%d" % g for g in range(G)),
                         tag="hrest")
            o_hs = em.op("vector",
                         lambda e, hcol=hcol: e.tensor_copy(
                             halos[:, hcol:hcol + 8].rearrange("p (g o) -> p g o", o=1),
                             xffn.rearrange("p (g t) -> p g t", g=G)[:, :, T:T + 1]),
                         r=tuple("xf%d" % g for g in range(G)), w=(hres,),
                         tag="hsave")

            for p in range(NP):
                a = p * 512
                prev = []
                for g in range(G):
                    base = g * (T + 1)
                    o_c = em.op("scalar",
                                lambda e, base=base, a=a:
                                e.copy(sqt[:, 0:513],
                                       xffn[:, base + a:base + a + 513]),
                                r=("xf%d" % g,), w=("sqtA",), dep=[o_hr],
                                tag="l2c")
                    o_s = em.op("scalar",
                                lambda e, base=base, a=a:
                                e.activation(sqt[:, 513:1026],
                                             xffn[:, base + a:base + a + 513],
                                             AF.Square),
                                r=("xf%d" % g,), w=("sqtB",), dep=[o_hr],
                                tag="l2s")
                    off = 0 if g >= 4 else 1
                    o1 = em.op("tensor",
                               lambda e, off=off, first=(g == 0), g=g:
                               mm(e, psA[R64:R64 + 1, 0:512], ones_b[:, 0:1],
                                  sqt[:, off:off + 512], first, g == G - 1),
                               r=("sqtA", "onesb") + (bres("psA", 0, 512) if g == 0 else ()),
                               w=bres("psA", 0, 512) if g == 0 else (),
                               dep=[o_c] + prev, tag="l2m")
                    prev = [o1]
                    o2 = em.op("tensor",
                               lambda e, off=off, first=(g == 0), g=g:
                               mm(e, psA[R64:R64 + 1, 512:1024], ones_b[:, 0:1],
                                  sqt[:, 513 + off:513 + off + 512], first,
                                  g == G - 1),
                               r=("sqtB", "onesb") + (bres("psA", 512, 1024) if g == 0 else ()),
                               w=bres("psA", 512, 1024) if g == 0 else (),
                               dep=[o_s] + prev, tag="l2ms")
                    prev = [o2]
                o_a, o_b, _ = ln_derive(prev, (shs, 1536, "l2a"),
                                        (shs, 2048, "l2b"), "l2d_%d_" % p)
                bc_a = bcast64("psX", 0, shs, 1536, "l2a", [o_a], tag="l2bca")
                bc_b = bcast64("psR", 0, shs, 2048, "l2b", [o_b], tag="l2bcb")
                for g in range(G):
                    sl = g % 2
                    o1 = em.op("vector",
                               lambda e, g=g, a=a, sl=sl:
                               e.tensor_mul(scr[:, sl * 512:sl * 512 + 512],
                                            xfs(g, a, a + 512), psX[0:128, 0:512]),
                               r=("xf%d" % g,) + bres("psX", 0, 512),
                               w=("scr%d" % sl,), dep=[bc_a], tag="xh1")
                    o2 = em.op("vector",
                               lambda e, g=g, sl=sl:
                               e.tensor_add(scr[:, sl * 512:sl * 512 + 512],
                                            scr[:, sl * 512:sl * 512 + 512],
                                            psR[0:128, 0:512]),
                               r=("scr%d" % sl,) + bres("psR", 0, 512),
                               w=("scr%d" % sl,), dep=[bc_b, o1], tag="xh2")
                    em.op("scalar",
                          lambda e, g=g, a=a, sl=sl, l=l:
                          e.activation(xnd(g, a, a + 512),
                                       scr[:, sl * 512:sl * 512 + 512],
                                       AF.Identity, bias=cpc(CP_LB + 8 * l + g),
                                       scale=cpc(CP_LG + 8 * l + g)),
                          r=("scr%d" % sl, "cpat"), w=("xa%d" % g,),
                          dep=[o2], tag="xh3")

            if c == 0:
                dump("xh%d" % l, xnar, tuple("xa%d" % g for g in range(G)))

            for cc2 in range(NP):
                a = cc2 * 512
                for m in range(NW1):
                    w = W1M[m]
                    wtA, wldA, wresA = wload(("w1a", l, m))
                    wtB, wldB, wresB = wload(("w1g", l, m))
                    prevA, prevB = [], []
                    po = (m % 2) * 512
                    for k in range(G):
                        scan_pump(1)
                        prevA.append(em.op(
                            "tensor",
                            lambda e, k=k, w=w, a=a, wt=wtA, po=po:
                            mm(e, psA[0:w, po:po + 512], wt[:, k * w:(k + 1) * w],
                               xnd(k, a, a + 512), k == 0, k == G - 1),
                            r=("xa%d" % k, wresA) + (bres("psA", po, po + 512) if k == 0 else ()),
                            w=bres("psA", po, po + 512) if k == 0 else (),
                            dep=[wldA] + prevA[-1:], tag="w1a"))
                        scan_pump(1)
                        prevB.append(em.op(
                            "tensor",
                            lambda e, k=k, w=w, a=a, wt=wtB, po=po:
                            mm(e, psB[0:w, po:po + 512],
                               wt[:, k * w:(k + 1) * w],
                               xnd(k, a, a + 512), k == 0, k == G - 1),
                            r=("xa%d" % k, wresB) + (bres("psB", po, po + 512) if k == 0 else ()),
                            w=bres("psB", po, po + 512) if k == 0 else (),
                            dep=[wldB] + prevB[-1:], tag="w1g"))
                    o_g = em.op("scalar",
                                lambda e, w=w, m=m, l=l, po=po:
                                e.activation(scr[:, 1024:1024 + 512][0:w, :],
                                             psB[0:w, po:po + 512], AF.Gelu,
                                             bias=cpc(CP_B1G + NW1 * l + m, w)),
                                r=bres("psB", po, po + 512) + ("cpat",), w=("scr2",),
                                dep=prevB[-1:], tag="gelu")
                    em.op("vector",
                          lambda e, w=w, m=m, l=l, po=po:
                          e.scalar_tensor_tensor(
                              h2p[0:w, m * 512:m * 512 + 512],
                              psA[0:w, po:po + 512], cpc(CP_B1 + NW1 * l + m, w),
                              scr[0:w, 1024:1024 + 512], ALU.add, ALU.mult),
                          r=bres("psA", po, po + 512) + ("scr2", "cpat"),
                          w=H2_RES + ("h2t%d" % m,), dep=[o_g] + prevA[-1:],
                          tag="h2")
                for mo in range(8):
                    wh = [wload(("w2", l, mo, 0)), wload(("w2", l, mo, 1))]
                    prev = []
                    for k in range(NW1):
                        kk = W1M[k]
                        scan_pump(1)
                        wt, wld, wres = wh[k // 11]
                        po2 = (mo % 2) * 512
                        prev.append(em.op(
                            "tensor",
                            lambda e, k=k, kk=kk, wt=wt, po2=po2:
                            mm(e, psR[0:128, po2:po2 + 512],
                               wt[0:kk, (k % 11) * 128:(k % 11) * 128 + 128],
                               h2p[0:kk, k * 512:k * 512 + 512],
                               k == 0, k == NW1 - 1),
                            r=("h2t%d" % k, wres) + (bres("psR", po2, po2 + 512) if k == 0 else ()),
                            w=bres("psR", po2, po2 + 512) if k == 0 else (),
                            dep=[wld] + prev[-1:], tag="w2mm"))
                    po2 = (mo % 2) * 512
                    em.op("vector",
                          lambda e, mo=mo, a=a, l=l, po2=po2:
                          e.scalar_tensor_tensor(
                              xfd(mo, a, a + 512), psR[0:128, po2:po2 + 512],
                              cpc(CP_B2 + 8 * l + mo),
                              xfd(mo, a, a + 512), ALU.add, ALU.add),
                          r=bres("psR", po2, po2 + 512) + ("cpat", "xf%d" % mo),
                          w=("xf%d" % mo,), dep=prev[-1:], tag="w2ev")

            if l == 0:
                o_hr2 = em.op("vector",
                              lambda e: e.tensor_copy(
                                  xffn.rearrange("p (g t) -> p g t", g=G)[:, :, 0:1],
                                  halos[:, 8:16].rearrange("p (g o) -> p g o", o=1)),
                              r=("hC",), w=tuple("xf%d" % g for g in range(G)),
                              tag="hrC")
                em.op("vector",
                      lambda e: e.tensor_copy(
                          halos[:, 8:16].rearrange("p (g o) -> p g o", o=1),
                          xffn.rearrange("p (g t) -> p g t", g=G)[:, :, T:T + 1]),
                      r=tuple("xf%d" % g for g in range(G)), w=("hC",),
                      tag="hsC")
                for g in range(G):
                    em.op("scalar",
                          lambda e, g=g: e.copy(
                              x2[:, g * (T + 1):(g + 1) * (T + 1)],
                              xffn[:, g * (T + 1):(g + 1) * (T + 1)]),
                          r=("xf%d" % g,), w=X2_RES, dep=[o_hr2], tag="x2c")

        if c == 0:
            dump("xw2", xffn, tuple("xf%d" % g for g in range(G)))
            dump("H", Hbuf, ("H",))

        scan_pump(T)
        for g in range(G):
            em.op("scalar",
                  lambda e, g=g: e.activation(xfd(g, 0, T), xfd(g, 0, T),
                                              AF.Silu),
                  r=("xf%d" % g,), w=("xf%d" % g,), tag="silu")
        h2rd = [em.res.get("h2t%d" % m, [None, []]) for m in range(NW1)]
        h2deps = [o for e2 in h2rd for o in ([e2[0]] if e2[0] else []) + e2[1]]
        xin2 = em.op("sync",
                     lambda e, t0=t0: e.dma_start(
                         arena[:, 0:G * T].rearrange("p (g t) -> p g t", g=G),
                         xT[:, t0:t0 + T].rearrange("(g p) t -> p g t", p=128)),
                     w=AR_ALL, dma=True, dep=h2deps, tag="xre%d" % c)
        for p in range(NP):
            a = p * 512
            bcA = bcast64("psX", 0, strips, 2 * T + a, "stA2", [], tag="cbcA")
            bcB = bcast64("psR", 512, strips, 3 * T + a, "stB2", [], tag="cbcB")
            for g in range(G):
                sl = g % 2
                o1 = em.op("vector",
                           lambda e, g=g, a=a, sl=sl:
                           e.tensor_mul(scr[:, sl * 512:sl * 512 + 512],
                                        xg(g, a, a + 512), psX[0:128, 0:512]),
                           r=("arQ%d" % (g // 2),) + bres("psX", 0, 512),
                           w=("scr%d" % sl,), dep=[bcA, xin2], tag="c1")
                o2 = em.op("vector",
                           lambda e, g=g, sl=sl:
                           e.tensor_add(scr[:, sl * 512:sl * 512 + 512],
                                        scr[:, sl * 512:sl * 512 + 512],
                                        psR[0:128, 512:1024]),
                           r=("scr%d" % sl,) + bres("psR", 512, 1024),
                           w=("scr%d" % sl,), dep=[bcB, o1], tag="c2")
                em.op("scalar",
                      lambda e, g=g, a=a, sl=sl:
                      e.activation(xnd(g, a, a + 512),
                                   scr[:, sl * 512:sl * 512 + 512],
                                   AF.Identity, bias=cpc(CP_LNB + g),
                                   scale=cpc(CP_LNW + g)),
                      r=("scr%d" % sl, "cpat"), w=("xa%d" % g,), dep=[o2],
                      tag="c3")
        o_hb = em.op("scalar", lambda e: e.copy(
                         hbf, Hbuf.rearrange("p (t u) -> p t u", u=2)[0:DST, 1:T + 1, 0:1]),
                     r=("H",), w=("hbf",), tag="hcast")
        for g in range(G):
            wt, wld, wres = wload(("wg", g))
            womt, womld, womres = wload(("wom",))
            for p in range(NP):
                a = p * 512
                po4 = (((g * NP) + p) % 2) * 512
                prev = []
                for k in range(G):
                    scan_pump(1)
                    prev.append(em.op(
                        "tensor",
                        lambda e, k=k, a=a, wt=wt, po4=po4: mm(e, psA[0:128, po4:po4 + 512],
                                               wt[:, k * 128:(k + 1) * 128],
                                               xnd(k, a, a + 512),
                                               k == 0, k == G - 1),
                        r=("xa%d" % k, wres) + (bres("psA", po4, po4 + 512) if k == 0 else ()),
                        w=bres("psA", po4, po4 + 512) if k == 0 else (),
                        dep=[wld] + prev[-1:], tag="gmm"))
                o_om = em.op("tensor",
                             lambda e, g=g, a=a, womt=womt, po4=po4:
                             mm(e, psB[0:128, po4:po4 + 512],
                                womt[0:DST, g * 128:(g + 1) * 128],
                                hbf[:, a:a + 512], True, True),
                             r=("hbf", womres) + bres("psB", po4, po4 + 512),
                             w=bres("psB", po4, po4 + 512), dep=[womld, o_hb],
                             tag="ommm")
                o_sg = em.op("scalar",
                             lambda e, g=g, po4=po4: e.activation(scr[:, 0:512],
                                                         psA[0:128, po4:po4 + 512],
                                                         AF.Sigmoid,
                                                         bias=cpc(CP_BG + g)),
                             r=bres("psA", po4, po4 + 512) + ("cpat",), w=("scr0",),
                             dep=prev[-1:], tag="sig")
                o_s = em.op("vector",
                            lambda e, g=g, po4=po4: e.scalar_tensor_tensor(
                                scr[:, 512:1024], psB[0:128, po4:po4 + 512],
                                cpc(CP_BO + g), scr[:, 0:512],
                                ALU.add, ALU.mult),
                            r=bres("psB", po4, po4 + 512) + ("cpat", "scr0"),
                            w=("scr1",), dep=[o_om, o_sg], tag="sg")
                bcA2 = bcast64("psX", 0, strips, a, "stA", [], tag="fbca")
                bcB2 = bcast64("psR", 512, strips, T + a, "stB", [], tag="fbcb")
                o_t = em.op("vector",
                            lambda e, g=g, a=a: e.tensor_mul(
                                scr[:, 1024:1536], xg(g, a, a + 512),
                                psX[0:128, 0:512]),
                            r=("arQ%d" % (g // 2),) + bres("psX", 0, 512),
                            w=("scr2",), dep=[bcA2, xin2], tag="ft")
                o_v = em.op("vector",
                            lambda e: e.tensor_add(scr[:, 1024:1536],
                                                   scr[:, 1024:1536],
                                                   psR[0:128, 512:1024]),
                            r=("scr2",) + bres("psR", 512, 1024),
                            w=("scr2",), dep=[bcB2, o_t], tag="fv")
                o_w = em.op("vector",
                            lambda e: e.tensor_add(scr[:, 1024:1536],
                                                   scr[:, 1024:1536],
                                                   scr[:, 512:1024]),
                            r=("scr2", "scr1"), w=("scr2",), dep=[o_v, o_s],
                            tag="fw")
                o_z = em.op("vector",
                            lambda e, g=g, a=a: e.tensor_mul(
                                scr[:, 1024:1536], scr[:, 1024:1536],
                                xfd(g, a, a + 512)),
                            r=("scr2", "xf%d" % g), w=("scr2",), dep=[o_w],
                            tag="fz")
                o_f = em.op("vector",
                            lambda e, g=g, a=a: e.tensor_add(
                                xfd(g, a, a + 512), scr[:, 1024:1536],
                                xg(g, a, a + 512)),
                            r=("scr2", "arQ%d" % (g // 2)),
                            w=("xf%d" % g,), dep=[o_z, xin2], tag="ff")
                em.op("gpsimd",
                      lambda e, g=g, a=a, t0=t0: e.dma_start(
                          yout[:, g * S + t0 + a:g * S + t0 + a + 512],
                          xfd(g, a, a + 512)),
                      r=("xf%d" % g,), dma=True, force_inc=True,
                      dep=[o_f], tag="yout")

    emit_program(nc, em)
    return nc


def _host_prep(inputs, S):
    f = inputs
    wblob = pack_wblob(f)
    TQc, TQs, TKc, TKs, base64 = rope_tables(S)
    tabs = np.concatenate([TQc, TQs, TKc, TKs], 1)
    cpat = pack_cpat(f, base64)
    cmisc = const_misc(np.asarray(inputs["L0_attn_A"], np.float32).T)
    x = np.asarray(f["L0_attn_x"], np.float32)
    in_maps = []
    for b in range(NCORE):
        in_maps.append({
            "xT": np.ascontiguousarray(x[b].T),
            "wblob": wblob,
            "tabs": tabs,
            "cpin": cpat,
            "cmin": cmisc,
        })
    return in_maps


def run(inputs, S=S_FULL, T=T_CHUNK, verbose=False, dbg=False):
    in_maps = _host_prep(inputs, S)
    nc = build_nc(S, T, verbose=verbose, dbg=dbg)
    res = run_bass_kernel_spmd(nc, in_maps, core_ids=list(range(NCORE)))
    outs = []
    for b in range(NCORE):
        y = res.results[b]["y"]
        y = y.reshape(128, G, S).transpose(1, 0, 2).reshape(D, S)
        outs.append(y.T)
    out = np.stack(outs, 0).astype(np.float32)
    if dbg:
        return out, [{k: v for k, v in res.results[b].items() if k.startswith("dbg_")}
                     for b in range(NCORE)]
    return out


def kernel(**inputs):
    return run(inputs, S=S_FULL, T=T_CHUNK)
